# version 01
# baseline (speedup 1.0000x reference)
import numpy as np
import ml_dtypes
from contextlib import ExitStack

import concourse.bass as bass
import concourse.mybir as mybir
from concourse.bass_utils import run_bass_kernel_spmd

F32 = mybir.dt.float32
BF16 = mybir.dt.bfloat16
I32 = mybir.dt.int32
AF = mybir.ActivationFunctionType
ALU = mybir.AluOpType
AX = mybir.AxisListType
BF = ml_dtypes.bfloat16

B, T, D = 4, 2048, 2048
DI, DXB, DTR = 4096, 1024, 128
INTER = 8192
NE, HD = 16, 128
HALF = DI // 2
NCB, CBW = 16, 128
TCH = 1024
NK = D // 128
TH = T // 2
EPS = 1e-5
NEG = -1e9
ATT_SCALE = float(1.0 / np.sqrt(128.0))

ENGINES = ("te", "ve", "se", "gp", "sy")
PROG_SEMS = ("te", "ve", "se", "gp", "d_sy", "d_gp", "gpx")


class Prog:

    def __init__(self):
        self.fns = {e: [] for e in ENGINES}
        self.cnt = {s: 0 for s in PROG_SEMS}
        self.meta = []

    @staticmethod
    def merge(waits):
        m = {}
        for w in waits:
            if w is None:
                continue
            s, v = w
            if v > m.get(s, 0):
                m[s] = v
        return tuple(sorted(m.items()))

    def _emit(self, eng, f, waits, inc):
        waits = Prog.merge(waits)
        self.meta.append((eng, waits, inc))

        def g(ob, semd):
            for s, v in waits:
                ob.wait_ge(semd[s], v)
            inst = f(ob, semd) if getattr(f, "wants_semd", False) else f(ob)
            if inc is not None:
                s, _ = inc
                amt = 16 if s.startswith("d_") else 1
                if inst is not None:
                    inst.then_inc(semd[s], amt)
                else:
                    ob.sem_inc(semd[s], amt)
        self.fns[eng].append(g)

    def op(self, eng, f, waits=()):
        assert eng in ("te", "ve", "se", "gp")
        self.cnt[eng] += 1
        tok = (eng, self.cnt[eng])
        self._emit(eng, f, waits, tok)
        return tok

    def raw(self, eng, f, waits=()):
        self._emit(eng, f, waits, None)

    CHAIN_DMA = True

    def dma(self, eng, f, waits=()):
        sem = {"sy": "d_sy", "gp": "d_gp"}[eng]
        if Prog.CHAIN_DMA:
            waits = list(waits) + [(sem, self.cnt[sem])]
        self.cnt[sem] += 16
        tok = (sem, self.cnt[sem])
        self._emit(eng, f, waits, tok)
        return tok

    def dma_manual(self, eng, f, waits=()):
        sem = {"sy": "d_sy", "gp": "d_gp"}[eng]
        if Prog.CHAIN_DMA:
            waits = list(waits) + [(sem, self.cnt[sem])]
        self.cnt[sem] += 16
        tok = (sem, self.cnt[sem])

        @wants_semd
        def g(ob, semd):
            f(ob, semd[sem])
            return None
        self._emit(eng, g, waits, None)
        self.meta[-1] = (eng, self.meta[-1][1], tok)
        return tok

    def dma2(self, eng, fs, waits=()):
        tok = None
        for f in fs:
            tok = self.dma(eng, f, waits)
        return tok

    def mark(self, eng, f=None, waits=()):
        self.cnt["gpx"] += 1
        tok = ("gpx", self.cnt["gpx"])
        self._emit(eng, f or (lambda ob: None), waits, tok)
        return tok

    def now(self):
        return [(s, self.cnt[s]) for s in PROG_SEMS]

    def barrier(self):
        toks = self.now()
        for e in ENGINES:
            self.raw(e, lambda ob: None, waits=toks)
        return toks

    def check_deadlock(self):
        streams = {e: [] for e in ENGINES}
        for eng, waits, inc in self.meta:
            streams[eng].append((waits, inc))
        ptr = {e: 0 for e in ENGINES}
        vals = {s: 0 for s in PROG_SEMS}
        progress = True
        while progress:
            progress = False
            for e in ENGINES:
                while ptr[e] < len(streams[e]):
                    waits, inc = streams[e][ptr[e]]
                    if all(vals[s] >= v for s, v in waits):
                        if inc is not None:
                            s, n = inc
                            vals[s] = max(vals[s], n)
                        ptr[e] += 1
                        progress = True
                    else:
                        break
        stuck = {e: ptr[e] for e in ENGINES if ptr[e] < len(streams[e])}
        if stuck:
            lines = []
            for e, p in stuck.items():
                waits, _ = streams[e][p]
                unmet = [(s, v, vals[s]) for s, v in waits if vals[s] < v]
                lines.append(f"  {e}: op {p}/{len(streams[e])} unmet={unmet}")
            raise RuntimeError("schedule deadlock:\n" + "\n".join(lines))


def wants_semd(f):
    f.wants_semd = True
    return f



def host_prepare(inputs):
    x = np.asarray(inputs["L0_attn_hidden_states"], np.float32)
    w_rms1 = np.asarray(inputs["L0_attn_rms1_w"], np.float32)
    Win = np.asarray(inputs["L0_attn_in_proj_w"], np.float32)
    Wdt = np.asarray(inputs["L0_attn_dt_proj_w"], np.float32)
    dtb = np.asarray(inputs["L0_attn_dt_proj_b"], np.float32)
    Alog = np.asarray(inputs["L0_attn_A_log"], np.float32)
    Dsk = np.asarray(inputs["L0_attn_D_skip"], np.float32)
    Wout = np.asarray(inputs["L0_attn_out_proj_w"], np.float32)
    w_rms2 = np.asarray(inputs["L0_attn_rms2_w"], np.float32)
    WgT = np.ascontiguousarray(np.asarray(inputs["L0_attn_gate_w"], np.float32).T)
    WuT = np.ascontiguousarray(np.asarray(inputs["L0_attn_up_w"], np.float32).T)
    WdT = np.ascontiguousarray(np.asarray(inputs["L0_attn_down_w"], np.float32).T)
    sim = np.asarray(inputs["L0_ffn_sim_matrix"], np.float32)
    gates = np.asarray(inputs["L0_ffn_gates"], np.float32)
    Q = np.asarray(inputs["L0_ffn_q_proj"], np.float32)
    K = np.asarray(inputs["L0_ffn_k_proj"], np.float32)
    V = np.asarray(inputs["L0_ffn_v_proj"], np.float32)
    O = np.asarray(inputs["L0_ffn_o_proj"], np.float32)

    WinT = np.ascontiguousarray(Win.T)
    WdtT = np.ascontiguousarray(Wdt.T)
    WoutT = np.ascontiguousarray(Wout.T)

    per_half = {}
    for half in (0, 1):
        zoff = half * HALF
        cols = []
        for cb in range(NCB):
            base = zoff + cb * CBW
            cols.extend(range(base, base + CBW))
            cols.extend(range(6144 + base, 6144 + base + CBW))
            xvrows = sorted({((base + c) // 64) * 16 + ((base + c) % 16) for c in range(CBW)})
            cols.extend(4096 + r for r in xvrows)
            cols.extend(5120 + r for r in xvrows)
        cols.extend(range(10240, 10240 + DTR))
        win_slice = np.ascontiguousarray(WinT[:, cols])

        Aperm = np.empty((128, 256), np.float32)
        cc_i = np.arange(128) // 16
        n_i = np.arange(128) % 16
        for st in range(256):
            cb, sl = st // 16, st % 16
            Aperm[:, st] = Alog[zoff + cb * CBW + cc_i * 16 + sl, n_i]
        per_half[half] = dict(
            win=win_slice,
            wout=np.ascontiguousarray(WoutT[zoff:zoff + HALF, :]),
            wdt=np.ascontiguousarray(WdtT[:, zoff:zoff + HALF]).astype(BF),
            aperm=Aperm,
            d_arr=Dsk[zoff:zoff + HALF].reshape(NCB, CBW).T.copy(),
            dtb_arr=dtb[zoff:zoff + HALF].reshape(NCB, CBW).T.copy(),
        )

    S_dt = np.zeros((128, 16 * 128), np.float32)
    S_ns = np.zeros((128, 16 * 128), np.float32)
    for sl in range(16):
        for p in range(128):
            S_dt[(p // 16) * 16 + sl, sl * 128 + p] = 1.0
            S_ns[p, sl * 128 + (p // 16) * 16 + sl] = 1.0
    S_xf = np.zeros((64, 128), np.float32)
    S_B = np.zeros((64, 128), np.float32)
    for c in range(128):
        S_xf[(c // 64) * 16 + (c % 16), c] = 1.0
        S_B[32 + ((c // 16) // 4) * 16 + (c % 16), c] = 1.0

    tri = np.triu(np.full((128, 128), NEG, np.float32), 1)
    fullm = np.full((128, 128), NEG, np.float32)
    masks = {}
    for par in (0, 1):
        m = np.zeros((8, 16, 128, 128), np.float32)
        for qc in range(8):
            dg = par * 8 + qc
            for kkc in range(16):
                if kkc == dg:
                    m[qc, kkc] = tri
                elif kkc > dg:
                    m[qc, kkc] = fullm
        masks[par] = np.ascontiguousarray(m.astype(BF))

    nonces = np.random.default_rng().integers(1, 2**30, size=16).astype(np.int32)
    sel16 = np.zeros((16, 16 * 128), np.float32)
    for e in range(16):
        sel16[e, e * 128:(e + 1) * 128] = 1.0
    w1_arr = w_rms1.reshape(NK, 128).T.copy()
    w2_arr = w_rms2.reshape(NK, 128).T.copy()
    qkv = np.ascontiguousarray(np.stack([Q, K, V], 0))

    ins = []
    for core in range(8):
        b, par = core // 2, core % 2
        meta = np.zeros((1, 32), np.int32)
        meta[0, 0] = par
        meta[0, 1:17] = nonces
        ph = per_half[par]
        xt = np.ascontiguousarray(x[b].T)
        ins.append({
            "x_t": xt,
            "x_h": np.ascontiguousarray(xt[:, par * TH:(par + 1) * TH]),
            "win": ph["win"], "wdt": ph["wdt"], "wout": ph["wout"],
            "aperm": ph["aperm"], "d_arr": ph["d_arr"], "dtb_arr": ph["dtb_arr"],
            "w1_arr": w1_arr, "w2_arr": w2_arr,
            "wg": WgT, "wu": WuT, "wd": WdT,
            "sim": sim, "gates": np.ascontiguousarray(gates.reshape(1, NE)),
            "qkv": qkv, "oproj": O,
            "sdt": S_dt.astype(BF), "sns": S_ns.astype(BF),
            "sxf": S_xf.astype(BF), "sbsel": S_B.astype(BF),
            "identb": np.eye(128, dtype=np.float32).astype(BF),
            "sel16": sel16.astype(BF),
            "identf": np.eye(128, dtype=np.float32),
            "ones_col": np.ones((128, 1), np.float32).astype(BF),
            "ones_row": np.ones((1, 128), np.float32).astype(BF),
            "masks": masks[par],
            "meta": meta,
        })
    return ins



def build_program(dbg=(), stop_after=None, n_cb=NCB, n_tc=2):
    nc = bass.Bass("TRN2", target_bir_lowering=False)
    P = Prog()
    top = ExitStack()

    def sb(name, shape, dt, stack=None):
        return (stack or top).enter_context(nc.sbuf_tensor(name, shape, dt))

    def pst(name, shape, dt=F32, stack=None):
        return (stack or top).enter_context(nc.psum_tensor(name, shape, dt))

    def din(name, shape, dt):
        return nc.dram_tensor(name, shape, dt, kind="ExternalInput").ap()

    x_t = din("x_t", [D, T], F32)
    x_h = din("x_h", [D, TH], F32)
    win = din("win", [D, NCB * 320 + 128], F32)
    wdt = din("wdt", [128, HALF], BF16)
    wout = din("wout", [HALF, D], F32)
    aperm = din("aperm", [128, 256], F32)
    d_arr_d = din("d_arr", [128, 16], F32)
    dtb_d = din("dtb_arr", [128, 16], F32)
    w1_d = din("w1_arr", [128, 16], F32)
    w2_d = din("w2_arr", [128, 16], F32)
    wg_d = din("wg", [D, INTER], F32)
    wu_d = din("wu", [D, INTER], F32)
    wd_d = din("wd", [INTER, D], F32)
    sim_d = din("sim", [D, NE], F32)
    gates_d = din("gates", [1, NE], F32)
    qkv_d = din("qkv", [3, NE, D, HD], F32)
    oproj_d = din("oproj", [NE, HD, D], F32)
    sdt_d = din("sdt", [128, 2048], BF16)
    sns_d = din("sns", [128, 2048], BF16)
    sxf_d = din("sxf", [64, 128], BF16)
    sbsel_d = din("sbsel", [64, 128], BF16)
    identb_d = din("identb", [128, 128], BF16)
    sel16_d = din("sel16", [16, 16 * 128], BF16)
    identf_d = din("identf", [128, 128], F32)
    onesc_d = din("ones_col", [128, 1], BF16)
    onesr_d = din("ones_row", [1, 128], BF16)
    masks_d = din("masks", [8, 16, 128, 128], BF16)
    meta_d = din("meta", [1, 32], I32)

    out_d = nc.dram_tensor("out", [TH, D], F32, kind="ExternalOutput").ap()

    dbg_aps = {}
    def dbgt(name, shape):
        if name in dbg and name not in dbg_aps:
            dbg_aps[name] = nc.dram_tensor("dbg_" + name, shape, F32,
                                           kind="ExternalOutput").ap()
        return dbg_aps.get(name)

    h_scr = nc.dram_tensor("h_scr", [D, TH], F32).ap()
    s1_scr = nc.dram_tensor("s1_scr", [D, TH], F32).ap()
    x1_scr = nc.dram_tensor("x1_scr", [D, TH], F32).ap()
    s2_scr = nc.dram_tensor("s2_scr", [D, TH], F32).ap()
    macc_scr = nc.dram_tensor("macc_scr", [D, TH], F32).ap()
    ex4 = nc.dram_tensor("ex4", [4, D, TCH], F32, addr_space="Shared").ap()
    kvx = nc.dram_tensor("kvx", [2, 2, 128, TH], BF16, addr_space="Shared").ap()
    flg = nc.dram_tensor("flg", [2, 8, 16], I32, addr_space="Shared").ap()

    smeta = sb("smeta", [1, 32], I32)
    spoll = sb("spoll", [1, 16], I32)
    identb_s = sb("identb_s", [128, 128], BF16)
    sel16_s = sb("sel16_s", [16, 16 * 128], BF16)
    identf_s = sb("identf_s", [128, 128], F32)
    onesc_s = sb("onesc_s", [128, 1], BF16)
    onesr_s = sb("onesr_s", [1, 128], BF16)
    w1_s = sb("w1_s", [128, 16], F32)
    w2_s = sb("w2_s", [128, 16], F32)
    rowA = sb("rowA", [1, TH], F32)
    rowB = sb("rowB", [1, TH], F32)
    rowC = sb("rowC", [1, TH], F32)
    rowD = sb("rowD", [1, TH], F32)
    rowbf = sb("rowbf", [1, TH], BF16)
    rowbf2 = sb("rowbf2", [1, TH], BF16)
    eps_col = sb("eps_col", [128, 1], F32)
    one_col = sb("one_col", [128, 1], F32)
    esm = ExitStack()
    sdt_s = sb("sdt_s", [128, 2048], BF16, esm)
    sns_s = sb("sns_s", [128, 2048], BF16, esm)
    sxf_s = sb("sxf_s", [64, 128], BF16, esm)
    sbsel_s = sb("sbsel_s", [64, 128], BF16, esm)
    ascan = sb("ascan", [128, 256], F32, esm)
    darr_s = sb("darr_s", [128, 16], F32, esm)
    dtb_s = sb("dtb_s", [128, 16], F32, esm)
    carry = sb("carry", [128, 256], F32, esm)
    wdt_s = sb("wdt_s", [128, HALF], BF16, esm)

    t_meta = P.dma("sy", lambda o: o.dma_start(out=smeta[:], in_=meta_d))
    ct = [t_meta]
    for s, dd in ((sdt_s, sdt_d), (sns_s, sns_d), (sxf_s, sxf_d), (sbsel_s, sbsel_d),
                  (identb_s, identb_d), (sel16_s, sel16_d), (identf_s, identf_d),
                  (onesc_s, onesc_d), (onesr_s, onesr_d), (darr_s, d_arr_d),
                  (dtb_s, dtb_d), (w1_s, w1_d), (w2_s, w2_d), (wdt_s, wdt)):
        ct.append(P.dma("sy", lambda o, s=s, dd=dd: o.dma_start(out=s[:], in_=dd)))
    ct.append(P.op("ve", lambda o: o.memset(eps_col[:], EPS)))
    ct.append(P.op("ve", lambda o: o.memset(one_col[:], 1.0)))
    t_ap = P.dma("sy", lambda o: o.dma_start(out=ascan[:], in_=aperm))
    t_a1 = P.op("se", lambda o: o.activation(carry[:], ascan[:], AF.Exp), waits=[t_ap])
    ct.append(P.op("se", lambda o: o.mul(ascan[:], carry[:], -1.0), waits=[t_a1]))
    CR = list(Prog.merge(ct))

    par_reg = nc.gpsimd.alloc_register("par_reg")
    PAR_TOK = P.mark("gp", lambda o: o.reg_load(par_reg, smeta[0:1, 0:1]),
                     waits=[t_meta])

    ps_stack = ExitStack()
    PS = [pst(f"ps{i}", [128, 512], F32, ps_stack) for i in range(5)]
    PROW = pst("prow", [1, 1024], F32, ps_stack)
    PTB = pst("ptb", [128, 512], BF16, ps_stack)

    def mm_group(psum_ap, tiles, waits, start=True, stop=True):
        n = len(tiles)
        def f(o):
            inst = None
            for i, (lf, rf) in enumerate(tiles):
                inst = o.matmul(psum_ap, lf(), rf(),
                                start=(start and i == 0), stop=(stop and i == n - 1))
            return inst
        return P.op("te", f, waits=waits)

    def bcast_f32(row_bf_fn, dst_fn, waits, ps_tile, width=512):
        t1 = mm_group(ps_tile[:, 0:width], [(lambda: onesr_s[:], row_bf_fn)], waits=waits)
        return P.op("se", lambda o: o.copy(dst_fn(), ps_tile[:, 0:width]), waits=[t1])

    poll_id = [0]

    def gp_poll(spec_p0, spec_p1, waits):
        (s0, n0), (s1, n1) = spec_p0, spec_p1
        poll_id[0] += 1
        pid = poll_id[0]
        psem = f"pf{pid}"

        @wants_semd
        def f(o, semd):
            with o.register(f"rn{pid}") as rn, o.register(f"fv{pid}") as fv, \
                 o.register(f"cnt{pid}") as cnt, o.register(f"nd{pid}") as nd:
                with o.If_eq(par_reg, 0):
                    o.reg_load(rn, smeta[0:1, 1 + n0:2 + n0])
                with o.Else():
                    o.reg_load(rn, smeta[0:1, 1 + n1:2 + n1])
                o.reg_mov(nd, 1)
                o.reg_mov(cnt, 0)
                with o.While(nd):
                    with o.If_eq(par_reg, 0):
                        o.dma_start(out=spoll[0:1, 0:1],
                                    in_=flg[s0[0], s0[1], 0:1]).then_inc(semd[psem], 16)
                    with o.Else():
                        o.dma_start(out=spoll[0:1, 0:1],
                                    in_=flg[s1[0], s1[1], 0:1]).then_inc(semd[psem], 16)
                    o.reg_add(cnt, cnt, 16)
                    o.wait_ge(semd[psem], cnt)
                    o.reg_load(fv, spoll[0:1, 0:1])
                    o.reg_alu(nd, fv, rn, ALU.not_equal)
            return None
        return P.mark("gp", f, waits=list(waits) + [PAR_TOK])

    def gp_flag(slot, nidx, waits):
        def f(o, sem):
            with o.If_eq(par_reg, 0):
                o.dma_start(out=flg[0, slot, 0:1],
                            in_=smeta[0:1, 1 + nidx:2 + nidx]).then_inc(sem, 16)
            with o.Else():
                o.dma_start(out=flg[1, slot, 0:1],
                            in_=smeta[0:1, 1 + nidx:2 + nidx]).then_inc(sem, 16)
        return P.dma_manual("gp", f, waits=list(waits) + [PAR_TOK])

    xnorm = sb("xnorm", [128, NK * TCH], BF16, esm)
    yg = sb("yg", [128, NCB * TCH], BF16, esm)
    dtin_sb = sb("dtin_sb", [128, TCH], BF16, esm)
    rstd_bc = sb("rstd_bc", [128, TCH], F32, esm)
    xtl = [sb(f"xtl{i}", [128, TCH], F32, esm) for i in range(2)]
    sqb = sb("sqb", [128, 2 * TCH], BF16, esm)
    g_cb = sb("g_cb", [128, TCH], F32, esm)
    cm_cb = sb("cm_cb", [128, TCH], F32, esm)
    xb_cb = sb("xb_cb", [64, TCH], BF16, esm)
    dtsp = sb("dtsp", [128, TCH], F32, esm)
    dtspb = sb("dtspb", [128, TCH], BF16, esm)
    dtx = sb("dtx", [128, TCH], F32, esm)
    dtxb = sb("dtxb", [128, TCH], BF16, esm)
    xfb = sb("xfb", [128, TCH], BF16, esm)
    brep = sb("brep", [128, TCH], F32, esm)
    diagd = sb("diagd", [128, 128], BF16, esm)
    sp_m = sb("sp_m", [128, TCH], F32, esm)
    sp_l = sb("sp_l", [128, TCH], F32, esm)
    da_sb = sb("da_sb", [128, TCH], F32, esm)
    dbx_sb = sb("dbx_sb", [128, TCH], F32, esm)
    h_sb = sb("h_sb", [128, TCH], F32, esm)
    m_sb = sb("m_sb", [128, TCH], BF16, esm)
    wcb = [sb(f"wcb{i}", [128, 16 * 320], BF16, esm) for i in range(2)]
    wob = [sb(f"wob{i}", [128, 16 * 128], BF16, esm) for i in range(2)]
    stage = sb("stage", [128, TCH], F32, esm)

    d_xn = dbgt("xn", [NK, 128, T])
    d_zx = dbgt("zx", [NCB, 5, 128, TCH])
    d_hh = dbgt("hh", [3, 128, TCH])
    d_y = dbgt("y", [NCB, 128, T])
    d_op = dbgt("op", [D, T])

    wcb_busy = [None, None]
    wob_busy = [None, None]

    def load_kpm(eng, dst, src_cols, kcnt, mw, waits):
        tok = None
        for k0 in range(0, kcnt, 8):
            kc = min(8, kcnt - k0)
            def f(o, k0=k0, kc=kc):
                return o.dma_start(
                    out=dst[:, k0 * mw:(k0 + kc) * mw].rearrange("p (k m) -> p k m", k=kc),
                    in_=src_cols[k0 * 128:(k0 + kc) * 128, :].rearrange("(k p) m -> p k m", p=128))
            tok = P.dma(eng, f, waits)
        return tok

    for tc in range(n_tc):
        t0 = tc * TCH

        prev_mm = {0: None, 1: None}
        sq_of = {}
        for k in range(NK):
            tx = P.dma("sy", lambda o, k=k, t0=t0: o.dma_start(
                out=xtl[k % 2][:], in_=x_t[k * 128:(k + 1) * 128, t0:t0 + TCH]),
                waits=CR + [sq_of.get(k - 2)])
            tq = P.op("se", lambda o, k=k: o.activation(
                sqb[:, (k % 2) * TCH:(k % 2) * TCH + TCH], xtl[k % 2][:], AF.Square),
                waits=[tx, prev_mm[1]])
            sq_of[k] = tq
            for nn in range(2):
                prev_mm[nn] = mm_group(
                    PROW[0:1, nn * 512:nn * 512 + 512],
                    [(lambda: onesc_s[:],
                      lambda k=k, nn=nn: sqb[:, (k % 2) * TCH + nn * 512:(k % 2) * TCH + nn * 512 + 512])],
                    waits=[tq, prev_mm[nn]], start=(k == 0), stop=(k == NK - 1))
        st_toks = []
        for nn in range(2):
            tr1 = P.op("se", lambda o, nn=nn: o.activation(
                rowA[0:1, nn * 512:nn * 512 + 512], PROW[0:1, nn * 512:nn * 512 + 512],
                AF.Ln, bias=eps_col[0:1, 0:1], scale=1.0 / 2048.0), waits=[prev_mm[nn]])
            tr2 = P.op("se", lambda o, nn=nn: o.activation(
                rowA[0:1, nn * 512:nn * 512 + 512], rowA[0:1, nn * 512:nn * 512 + 512],
                AF.Exp, scale=-0.5), waits=[tr1])
            trb = P.op("ve", lambda o, nn=nn: o.tensor_copy(
                rowbf[0:1, nn * 512:nn * 512 + 512], rowA[0:1, nn * 512:nn * 512 + 512]),
                waits=[tr2])
            st_toks.append(bcast_f32(
                lambda nn=nn: rowbf[0:1, nn * 512:nn * 512 + 512],
                lambda nn=nn: rstd_bc[:, nn * 512:nn * 512 + 512],
                [trb] + st_toks, PS[4]))
        xn_tok = [None] * NK
        for k in range(NK):
            tx = P.dma("sy", lambda o, k=k, t0=t0: o.dma_start(
                out=xtl[k % 2][:], in_=x_t[k * 128:(k + 1) * 128, t0:t0 + TCH]),
                waits=st_toks + [xn_tok[k - 2] if k >= 2 else None])
            xn_tok[k] = P.op("ve", lambda o, k=k: o.scalar_tensor_tensor(
                xnorm[:, k * TCH:(k + 1) * TCH], xtl[k % 2][:],
                w1_s[:, k:k + 1], rstd_bc[:], ALU.mult, ALU.mult),
                waits=[tx] + st_toks)
            if d_xn is not None:
                P.dma("gp", lambda o, k=k, t0=t0: o.dma_start(
                    out=dbg_aps["xn"][k, :, t0:t0 + TCH],
                    in_=xnorm[:, k * TCH:(k + 1) * TCH]), waits=[xn_tok[k], PAR_TOK])

        if stop_after == "m0":
            return _render(nc, P, top)
        twd = load_kpm("gp", wcb[0], win[:, NCB * 320:NCB * 320 + 128], NK, 128,
                       [wcb_busy[0], PAR_TOK])
        dtin_toks = []
        prev = None
        for nn in range(2):
            tok = mm_group(PS[0][:, 0:512],
                           [(lambda k=k: wcb[0][:, k * 128:(k + 1) * 128],
                             lambda k=k, nn=nn: xnorm[:, k * TCH + nn * 512:k * TCH + nn * 512 + 512])
                            for k in range(NK)],
                           waits=[twd, prev] + xn_tok)
            prev = P.op("ve", lambda o, nn=nn: o.tensor_copy(
                dtin_sb[:, nn * 512:nn * 512 + 512], PS[0][:, 0:512]), waits=[tok])
            dtin_toks.append(prev)
        wcb_busy[0] = dtin_toks[-1]

        if stop_after == "m1":
            return _render(nc, P, top)
        for cb in range(n_cb):
            wb = wcb[cb % 2]
            tw = load_kpm("gp", wb, win[:, cb * 320:(cb + 1) * 320], NK, 320,
                          [wcb_busy[cb % 2], PAR_TOK])

            def rhs(k, nn):
                return xnorm[:, k * TCH + nn * 512:k * TCH + nn * 512 + 512]

            z_mm, cm_mm = [], []
            for nn in range(2):
                z_mm.append(mm_group(
                    PS[0 + nn][:, 0:512],
                    [(lambda k=k, wb=wb: wb[:, k * 320:k * 320 + 128],
                      lambda k=k, nn=nn: rhs(k, nn)) for k in range(NK)],
                    waits=[tw] + xn_tok))
                cm_mm.append(mm_group(
                    PS[2 + nn][:, 0:512],
                    [(lambda k=k, wb=wb: wb[:, k * 320 + 128:k * 320 + 256],
                      lambda k=k, nn=nn: rhs(k, nn)) for k in range(NK)],
                    waits=[tw] + xn_tok))
            tg = [P.op("se", lambda o, nn=nn: o.activation(
                g_cb[:, nn * 512:nn * 512 + 512], PS[0 + nn][:, 0:512], AF.Silu),
                waits=[z_mm[nn]]) for nn in range(2)]
            tcm = [P.op("ve", lambda o, nn=nn: o.tensor_copy(
                cm_cb[:, nn * 512:nn * 512 + 512], PS[2 + nn][:, 0:512]),
                waits=[cm_mm[nn]]) for nn in range(2)]
            txb = []
            for nn in range(2):
                tok = mm_group(
                    PS[4][0:64, 0:512],
                    [(lambda k=k, wb=wb: wb[:, k * 320 + 256:k * 320 + 320],
                      lambda k=k, nn=nn: rhs(k, nn)) for k in range(NK)],
                    waits=[tw] + xn_tok + txb)
                txb.append(P.op("ve", lambda o, nn=nn: o.tensor_copy(
                    xb_cb[0:64, nn * 512:nn * 512 + 512], PS[4][0:64, 0:512]),
                    waits=[tok]))
            if stop_after == "m2a":
                return _render(nc, P, top)
            dt_done = []
            for nn in range(2):
                tok = mm_group(PS[4][:, 0:512],
                               [(lambda cb=cb: wdt_s[:, cb * 128:(cb + 1) * 128],
                                 lambda nn=nn: dtin_sb[:, nn * 512:nn * 512 + 512])],
                               waits=txb + dtin_toks + dt_done)
                trm = P.op("se", lambda o, nn=nn, cb=cb: o.activation(
                    sp_m[:, nn * 512:nn * 512 + 512], PS[4][:, 0:512], AF.Relu,
                    bias=dtb_s[:, cb:cb + 1]), waits=[tok])
                ta = P.op("se", lambda o, nn=nn, cb=cb: o.activation(
                    sp_l[:, nn * 512:nn * 512 + 512], PS[4][:, 0:512], AF.Abs,
                    bias=dtb_s[:, cb:cb + 1]), waits=[tok])
                te_ = P.op("se", lambda o, nn=nn: o.activation(
                    sp_l[:, nn * 512:nn * 512 + 512], sp_l[:, nn * 512:nn * 512 + 512],
                    AF.Exp, scale=-1.0), waits=[ta])
                tl = P.op("se", lambda o, nn=nn: o.activation(
                    sp_l[:, nn * 512:nn * 512 + 512], sp_l[:, nn * 512:nn * 512 + 512],
                    AF.Ln, bias=one_col[:]), waits=[te_])
                tsp = P.op("ve", lambda o, nn=nn: o.tensor_add(
                    dtsp[:, nn * 512:nn * 512 + 512], sp_m[:, nn * 512:nn * 512 + 512],
                    sp_l[:, nn * 512:nn * 512 + 512]), waits=[trm, tl])
                dt_done.append(tsp)
            tspb = P.op("ve", lambda o: o.tensor_copy(dtspb[:], dtsp[:]), waits=dt_done)

            if stop_after == "m2b":
                return _render(nc, P, top)
            dtx_done, xf_done = [], []
            for nn in range(2):
                tok = mm_group(PS[4][:, 0:512],
                               [(lambda: sxf_s[:],
                                 lambda nn=nn: xb_cb[0:64, nn * 512:nn * 512 + 512])],
                               waits=txb + dt_done + dtx_done + xf_done)
                td = P.op("ve", lambda o, nn=nn: o.tensor_mul(
                    dtx[:, nn * 512:nn * 512 + 512], dtsp[:, nn * 512:nn * 512 + 512],
                    PS[4][:, 0:512]), waits=[tok] + dt_done)
                tf = P.op("se", lambda o, nn=nn: o.copy(
                    xfb[:, nn * 512:nn * 512 + 512], PS[4][:, 0:512]), waits=[tok, td])
                dtx_done.append(td)
                xf_done.append(tf)
            if stop_after == "m2b1":
                return _render(nc, P, top)
            tdxb = P.op("ve", lambda o: o.tensor_copy(dtxb[:], dtx[:]), waits=dtx_done)
            br_done = []
            for nn in range(2):
                tok = mm_group(PS[4][:, 0:512],
                               [(lambda: sbsel_s[:],
                                 lambda nn=nn: xb_cb[0:64, nn * 512:nn * 512 + 512])],
                               waits=dtx_done + xf_done + br_done)
                br_done.append(P.op("ve", lambda o, nn=nn: o.tensor_copy(
                    brep[:, nn * 512:nn * 512 + 512], PS[4][:, 0:512]), waits=[tok]))
            if stop_after == "m2b2":
                return _render(nc, P, top)
            tdg = P.op("ve", lambda o, cb=cb: o.tensor_scalar_mul(
                diagd[:], identb_s[:], darr_s[:, cb:cb + 1]), waits=CR + br_done)
            if stop_after == "m2b3":
                return _render(nc, P, top)

            if d_zx is not None and tc == 0:
                for j, (buf, wt) in enumerate(((g_cb, tg), (cm_cb, tcm), (dtsp, dt_done),
                                               (dtx, dtx_done), (brep, br_done))):
                    P.dma("sy", lambda o, cb=cb, j=j, buf=buf: o.dma_start(
                        out=dbg_aps["zx"][cb, j], in_=buf[:]), waits=list(wt))

            if stop_after == "m2c":
                return _render(nc, P, top)
            ys_mm = {}
            for nn in range(2):
                ys_mm[nn] = mm_group(
                    PS[2 + nn][:, 0:512],
                    [(lambda: diagd[:], lambda nn=nn: xfb[:, nn * 512:nn * 512 + 512])],
                    waits=[tdg] + xf_done + tcm, start=True, stop=False)
            prev_scan, prev_m = None, None
            for sl in range(16):
                st = cb * 16 + sl
                da_toks, db_toks = [], []
                for nn in range(2):
                    tka = mm_group(
                        PS[0 + nn][:, 0:512],
                        [(lambda sl=sl: sdt_s[:, sl * 128:(sl + 1) * 128],
                          lambda nn=nn: dtspb[:, nn * 512:nn * 512 + 512])],
                        waits=[tspb, tg[nn], prev_scan])
                    da_toks.append(P.op("se", lambda o, nn=nn, st=st: o.activation(
                        da_sb[:, nn * 512:nn * 512 + 512], PS[0 + nn][:, 0:512],
                        AF.Exp, scale=ascan[:, st:st + 1]),
                        waits=[tka, prev_scan]))
                for nn in range(2):
                    tkb = mm_group(
                        PS[0 + nn][:, 0:512],
                        [(lambda sl=sl: sdt_s[:, sl * 128:(sl + 1) * 128],
                          lambda nn=nn: dtxb[:, nn * 512:nn * 512 + 512])],
                        waits=[tdxb, da_toks[nn]])
                    db_toks.append(P.op("ve", lambda o, nn=nn: o.tensor_mul(
                        dbx_sb[:, nn * 512:nn * 512 + 512], PS[0 + nn][:, 0:512],
                        brep[:, nn * 512:nn * 512 + 512]),
                        waits=[tkb, prev_scan] + br_done))
                if tc == 0:
                    tscan = P.op("ve", lambda o: o.tensor_tensor_scan(
                        h_sb[:], da_sb[:], dbx_sb[:], 0.0, ALU.mult, ALU.add),
                        waits=da_toks + db_toks + [prev_m])
                else:
                    tscan = P.op("ve", lambda o, st=st: o.tensor_tensor_scan(
                        h_sb[:], da_sb[:], dbx_sb[:], carry[:, st:st + 1],
                        ALU.mult, ALU.add),
                        waits=da_toks + db_toks + [prev_m])
                tcar = P.op("ve", lambda o, st=st: o.tensor_copy(
                    carry[:, st:st + 1], h_sb[:, TCH - 1:TCH]), waits=[tscan])
                tm_ = P.op("ve", lambda o: o.tensor_mul(m_sb[:], h_sb[:], cm_cb[:]),
                           waits=[tscan, ys_mm[0], ys_mm[1]] + tcm)
                for nn in range(2):
                    ys_mm[nn] = mm_group(
                        PS[2 + nn][:, 0:512],
                        [(lambda sl=sl: sns_s[:, sl * 128:(sl + 1) * 128],
                          lambda nn=nn: m_sb[:, nn * 512:nn * 512 + 512])],
                        waits=[tm_, ys_mm[nn]], start=False, stop=(sl == 15))
                prev_scan, prev_m = tscan, tm_
                if d_hh is not None and tc == 0 and cb == 0 and sl == 0:
                    P.dma("sy", lambda o: o.dma_start(out=dbg_aps["hh"][0], in_=da_sb[:]),
                          waits=da_toks)
                    P.dma("sy", lambda o: o.dma_start(out=dbg_aps["hh"][1], in_=dbx_sb[:]),
                          waits=db_toks)
                    P.dma("sy", lambda o: o.dma_start(out=dbg_aps["hh"][2], in_=h_sb[:]),
                          waits=[tscan])

            if stop_after == "m2d":
                return _render(nc, P, top)
            yg_toks = []
            for nn in range(2):
                yg_toks.append(P.op("ve", lambda o, nn=nn, cb=cb: o.tensor_mul(
                    yg[:, cb * TCH + nn * 512:cb * TCH + nn * 512 + 512],
                    PS[2 + nn][:, 0:512], g_cb[:, nn * 512:nn * 512 + 512]),
                    waits=[ys_mm[nn]] + tg))
            if d_y is not None:
                P.dma("gp", lambda o, cb=cb, t0=t0: o.dma_start(
                    out=dbg_aps["y"][cb, :, t0:t0 + TCH],
                    in_=yg[:, cb * TCH:(cb + 1) * TCH]), waits=yg_toks + [PAR_TOK])
            P.barrier()

        if stop_after == "m2":
            return _render(nc, P, top)
        prev_store = None
        prev_ev3 = []
        for m in range(16):
            wb = wob[m % 2]
            tw = load_kpm("gp", wb, wout[:, m * 128:(m + 1) * 128], NK, 128,
                          [wob_busy[m % 2], PAR_TOK])
            ev = []
            mmt = None
            for nn in range(2):
                mmt = mm_group(
                    PS[nn][:, 0:512],
                    [(lambda k=k, wb=wb: wb[:, k * 128:(k + 1) * 128],
                      lambda k=k, nn=nn: yg[:, k * TCH + nn * 512:k * TCH + nn * 512 + 512])
                     for k in range(NCB)],
                    waits=[tw] + prev_ev3)
                ev.append(P.op("ve", lambda o, nn=nn: o.tensor_copy(
                    stage[:, nn * 512:nn * 512 + 512], PS[nn][:, 0:512]),
                    waits=[mmt, prev_store]))
            wob_busy[m % 2] = mmt
            prev_ev3 = list(ev)

            def store_partial(o, sem, m=m, tc=tc):
                with o.If_eq(par_reg, 0):
                    o.dma_start(out=ex4[tc, m * 128:(m + 1) * 128, :],
                                in_=stage[:]).then_inc(sem, 16)
                with o.Else():
                    o.dma_start(out=ex4[2 + tc, m * 128:(m + 1) * 128, :],
                                in_=stage[:]).then_inc(sem, 16)
            prev_store = P.dma_manual("gp", store_partial, waits=ev + [PAR_TOK])
            if d_op is not None:
                P.dma("sy", lambda o, m=m, t0=t0: o.dma_start(
                    out=dbg_aps["op"][m * 128:(m + 1) * 128, t0:t0 + TCH],
                    in_=stage[:]), waits=ev)
        gp_flag(tc, tc, waits=P.now())
        P.barrier()
    esm.close()

    if stop_after == "mamba":
        return _render(nc, P, top)

    esc = ExitStack()
    hbuf = sb("hbuf", [128, TH], F32, esc)
    abuf = sb("abuf", [128, TH], F32, esc)
    bbuf = sb("bbuf", [128, TH], F32, esc)
    xcbuf = sb("xcbuf", [128, TH], F32, esc)
    sqc = sb("sqc", [128, TH], BF16, esc)
    hn = sb("hn", [128, NK * TH], BF16, esc)
    rstd2_bc = sb("rstd2_bc", [128, TH], F32, esc)

    d_h = dbgt("h", [D, TH])
    d_hn = dbgt("hn", [D, TH])

    t_poll = gp_poll(((1, 0), 0), ((0, 1), 1), waits=P.now())

    prev_mm = {0: None, 1: None}
    hw_toks = []
    pt1 = pt2 = ptw = ptq = None
    for k in range(NK):
        tx = P.dma("sy", lambda o, k=k: o.dma_start(
            out=xcbuf[:], in_=x_h[k * 128:(k + 1) * 128, :]), waits=[t_poll, pt1])

        def ld_own(o, sem, k=k):
            with o.If_eq(par_reg, 0):
                o.dma_start(out=abuf[:], in_=ex4[0, k * 128:(k + 1) * 128, :]).then_inc(sem, 16)
            with o.Else():
                o.dma_start(out=abuf[:], in_=ex4[3, k * 128:(k + 1) * 128, :]).then_inc(sem, 16)

        def ld_partner(o, sem, k=k):
            with o.If_eq(par_reg, 0):
                o.dma_start(out=bbuf[:], in_=ex4[2, k * 128:(k + 1) * 128, :]).then_inc(sem, 16)
            with o.Else():
                o.dma_start(out=bbuf[:], in_=ex4[1, k * 128:(k + 1) * 128, :]).then_inc(sem, 16)
        ta = P.dma_manual("gp", ld_own, waits=[t_poll, pt1, PAR_TOK])
        tb = P.dma_manual("gp", ld_partner, waits=[t_poll, pt2, PAR_TOK])
        pt1 = P.op("ve", lambda o: o.tensor_add(hbuf[:], xcbuf[:], abuf[:]),
                   waits=[tx, ta, ptw, ptq])
        pt2 = P.op("ve", lambda o: o.tensor_add(hbuf[:], hbuf[:], bbuf[:]),
                   waits=[pt1, tb])
        ptw = P.dma("sy", lambda o, k=k: o.dma_start(
            out=h_scr[k * 128:(k + 1) * 128, :], in_=hbuf[:]), waits=[pt2])
        if d_h is not None:
            P.dma("sy", lambda o, k=k: o.dma_start(
                out=dbg_aps["h"][k * 128:(k + 1) * 128, :], in_=hbuf[:]), waits=[pt2])
        ptq = P.op("se", lambda o: o.activation(sqc[:], hbuf[:], AF.Square), waits=[pt2])
        for nn in range(2):
            prev_mm[nn] = mm_group(
                PROW[0:1, nn * 512:nn * 512 + 512],
                [(lambda: onesc_s[:], lambda nn=nn: sqc[:, nn * 512:nn * 512 + 512])],
                waits=[ptq, prev_mm[nn]], start=(k == 0), stop=(k == NK - 1))
        hw_toks.append(ptw)
    st2 = []
    for nn in range(2):
        tr1 = P.op("se", lambda o, nn=nn: o.activation(
            rowA[0:1, nn * 512:nn * 512 + 512], PROW[0:1, nn * 512:nn * 512 + 512],
            AF.Ln, bias=eps_col[0:1, 0:1], scale=1.0 / 2048.0), waits=[prev_mm[nn]])
        tr2 = P.op("se", lambda o, nn=nn: o.activation(
            rowA[0:1, nn * 512:nn * 512 + 512], rowA[0:1, nn * 512:nn * 512 + 512],
            AF.Exp, scale=-0.5), waits=[tr1])
        trb = P.op("ve", lambda o, nn=nn: o.tensor_copy(
            rowbf[0:1, nn * 512:nn * 512 + 512], rowA[0:1, nn * 512:nn * 512 + 512]),
            waits=[tr2])
        st2.append(bcast_f32(
            lambda nn=nn: rowbf[0:1, nn * 512:nn * 512 + 512],
            lambda nn=nn: rstd2_bc[:, nn * 512:nn * 512 + 512], [trb] + st2, PS[4]))
    hn_toks = []
    prev = None
    for k in range(NK):
        tr = P.dma("sy", lambda o, k=k: o.dma_start(
            out=hbuf[:], in_=h_scr[k * 128:(k + 1) * 128, :]),
            waits=st2 + [prev, hw_toks[k], ptq])
        prev = P.op("ve", lambda o, k=k: o.scalar_tensor_tensor(
            hn[:, k * TH:(k + 1) * TH], hbuf[:], w2_s[:, k:k + 1], rstd2_bc[:],
            ALU.mult, ALU.mult), waits=[tr] + st2)
        hn_toks.append(prev)
        if d_hn is not None:
            P.dma("gp", lambda o, k=k: o.dma_start(
                out=dbg_aps["hn"][k * 128:(k + 1) * 128, :],
                in_=hn[:, k * TH:(k + 1) * TH]), waits=[prev, PAR_TOK])
    P.barrier()
    if stop_after == "combine":
        return _render(nc, P, top)

    esl = ExitStack()
    wgb = [sb(f"wgb{i}", [128, 2048], BF16, esl) for i in range(2)]
    wub = [sb(f"wub{i}", [128, 2048], BF16, esl) for i in range(2)]
    wdb = [sb(f"wdb{i}", [128, 32 * 128], BF16, esl) for i in range(2)]
    act = sb("act", [128, 32 * TH], BF16, esl)
    sgb = sb("sgb", [128, TH], F32, esl)
    mstage = sb("mstage", [128, TH], F32, esl)
    mbuf = sb("mbuf", [128, TH], F32, esl)
    hbuf2 = sb("hbuf2", [128, TH], F32, esl)
    xbuf2 = sb("xbuf2", [128, TH], F32, esl)
    s1bf = sb("s1bf", [128, TH], BF16, esl)
    sq1 = sb("sq1", [128, TH], BF16, esl)

    d_s1 = dbgt("s1", [D, TH])

    mean_mm = {0: None, 1: None}
    ssq_mm = {0: None, 1: None}
    wg_busy = [None, None]
    wd_busy = [None, None]
    for ih in range(2):
        if stop_after == "mlpd" and ih == 1:
            return _render(nc, P, top)
        act_toks = []
        for ii in range(32):
            ig = ih * 32 + ii
            bsy = wg_busy[ii % 2]
            twg = load_kpm("gp", wgb[ii % 2], wg_d[:, ig * 128:(ig + 1) * 128],
                           NK, 128, [bsy, PAR_TOK] + hn_toks[-1:])
            twu = load_kpm("gp", wub[ii % 2], wu_d[:, ig * 128:(ig + 1) * 128],
                           NK, 128, [bsy, PAR_TOK])
            for nn in range(2):
                tgm = mm_group(
                    PS[0][:, 0:512],
                    [(lambda k=k, ii=ii: wgb[ii % 2][:, k * 128:(k + 1) * 128],
                      lambda k=k, nn=nn: hn[:, k * TH + nn * 512:k * TH + nn * 512 + 512])
                     for k in range(NK)],
                    waits=[twg] + hn_toks + act_toks[-1:])
                tum = mm_group(
                    PS[1][:, 0:512],
                    [(lambda k=k, ii=ii: wub[ii % 2][:, k * 128:(k + 1) * 128],
                      lambda k=k, nn=nn: hn[:, k * TH + nn * 512:k * TH + nn * 512 + 512])
                     for k in range(NK)],
                    waits=[twu] + hn_toks + act_toks[-1:])
                tsg = P.op("se", lambda o, nn=nn: o.activation(
                    sgb[:, nn * 512:nn * 512 + 512], PS[0][:, 0:512], AF.Silu),
                    waits=[tgm])
                tac = P.op("ve", lambda o, ii=ii, nn=nn: o.tensor_mul(
                    act[:, ii * TH + nn * 512:ii * TH + nn * 512 + 512],
                    PS[1][:, 0:512], sgb[:, nn * 512:nn * 512 + 512]),
                    waits=[tum, tsg])
                act_toks.append(tac)
            wg_busy[ii % 2] = act_toks[-1]
        if stop_after == "mlpa" and ih == 0:
            return _render(nc, P, top)
        if stop_after == "mlpc" and ih == 1:
            return _render(nc, P, top)
        import os as _os
        prev_evs = []
        prev_stg = None
        for m in range(int(_os.environ.get("N_MD", "16"))):
            twdn = load_kpm("gp", wdb[m % 2],
                            wd_d[ih * 4096:(ih + 1) * 4096, m * 128:(m + 1) * 128],
                            32, 128, [wd_busy[m % 2], PAR_TOK] + act_toks[-1:])
            evs = []
            mmt = None
            for nn in range(2):
                mmt = mm_group(
                    PS[2][:, 0:512],
                    [(lambda k=k, m=m: wdb[m % 2][:, k * 128:(k + 1) * 128],
                      lambda k=k, nn=nn: act[:, k * TH + nn * 512:k * TH + nn * 512 + 512])
                     for k in range(32)],
                    waits=[twdn] + act_toks + evs + prev_evs)
                if ih == 0:
                    evs.append(P.op("ve", lambda o, nn=nn: o.tensor_copy(
                        mstage[:, nn * 512:nn * 512 + 512], PS[2][:, 0:512]),
                        waits=[mmt, prev_stg]))
                else:
                    tmb = P.dma("sy", lambda o, m=m, nn=nn: o.dma_start(
                        out=mbuf[:, nn * 512:nn * 512 + 512],
                        in_=macc_scr[m * 128:(m + 1) * 128, nn * 512:nn * 512 + 512]),
                        waits=[mmt])
                    thb = P.dma("sy", lambda o, m=m, nn=nn: o.dma_start(
                        out=hbuf2[:, nn * 512:nn * 512 + 512],
                        in_=h_scr[m * 128:(m + 1) * 128, nn * 512:nn * 512 + 512]),
                        waits=[mmt])
                    txb2 = P.dma("sy", lambda o, m=m, nn=nn: o.dma_start(
                        out=xbuf2[:, nn * 512:nn * 512 + 512],
                        in_=x_h[m * 128:(m + 1) * 128, nn * 512:nn * 512 + 512]),
                        waits=[mmt])
                    e1 = P.op("ve", lambda o, nn=nn: o.tensor_add(
                        mstage[:, nn * 512:nn * 512 + 512], PS[2][:, 0:512],
                        mbuf[:, nn * 512:nn * 512 + 512]),
                        waits=[mmt, tmb, prev_stg])
                    e2 = P.op("ve", lambda o, nn=nn: o.tensor_add(
                        mstage[:, nn * 512:nn * 512 + 512],
                        mstage[:, nn * 512:nn * 512 + 512],
                        hbuf2[:, nn * 512:nn * 512 + 512]), waits=[e1, thb])
                    e3 = P.op("ve", lambda o, nn=nn: o.tensor_add(
                        mstage[:, nn * 512:nn * 512 + 512],
                        mstage[:, nn * 512:nn * 512 + 512],
                        xbuf2[:, nn * 512:nn * 512 + 512]), waits=[e2, txb2])
                    evs.append(e3)
            prev_evs = list(evs)
            if ih == 0:
                prev_stg = P.dma("sy", lambda o, m=m: o.dma_start(
                    out=macc_scr[m * 128:(m + 1) * 128, :], in_=mstage[:]), waits=evs)
                wd_busy[m % 2] = prev_stg
            else:
                tws = P.dma("sy", lambda o, m=m: o.dma_start(
                    out=s1_scr[m * 128:(m + 1) * 128, :], in_=mstage[:]), waits=evs)
                prev_stg = tws
                if d_s1 is not None:
                    P.dma("sy", lambda o, m=m: o.dma_start(
                        out=dbg_aps["s1"][m * 128:(m + 1) * 128, :], in_=mstage[:]),
                        waits=evs)
                tb1 = P.op("ve", lambda o: o.tensor_copy(s1bf[:], mstage[:]), waits=evs)
                tq1 = P.op("se", lambda o: o.activation(sq1[:], mstage[:], AF.Square),
                           waits=evs)
                for nn in range(2):
                    psm = PS[3] if nn == 0 else PS[4]
                    mean_mm[nn] = mm_group(
                        psm[0:1, 0:512],
                        [(lambda: onesc_s[:],
                          lambda nn=nn: s1bf[:, nn * 512:nn * 512 + 512])],
                        waits=[tb1, mean_mm[nn]], start=(m == 0), stop=(m == 15))
                    ssq_mm[nn] = mm_group(
                        PROW[0:1, nn * 512:nn * 512 + 512],
                        [(lambda: onesc_s[:],
                          lambda nn=nn: sq1[:, nn * 512:nn * 512 + 512])],
                        waits=[tq1, ssq_mm[nn]], start=(m == 0), stop=(m == 15))
                wd_busy[m % 2] = tws

    if stop_after == "mlpb":
        return _render(nc, P, top)
    r_toks = []
    for nn in range(2):
        psm = PS[3] if nn == 0 else PS[4]
        t1 = P.op("se", lambda o, nn=nn, psm=psm: o.activation(
            rowA[0:1, nn * 512:nn * 512 + 512], psm[0:1, 0:512], AF.Copy,
            scale=1.0 / 2048.0), waits=[mean_mm[nn]])
        t2 = P.op("se", lambda o, nn=nn: o.activation(
            rowB[0:1, nn * 512:nn * 512 + 512], rowA[0:1, nn * 512:nn * 512 + 512],
            AF.Square), waits=[t1])
        t3 = P.op("se", lambda o, nn=nn: o.activation(
            rowC[0:1, nn * 512:nn * 512 + 512], PROW[0:1, nn * 512:nn * 512 + 512],
            AF.Copy, scale=1.0 / 2048.0), waits=[ssq_mm[nn]])
        t4 = P.op("ve", lambda o, nn=nn: o.tensor_sub(
            rowC[0:1, nn * 512:nn * 512 + 512], rowC[0:1, nn * 512:nn * 512 + 512],
            rowB[0:1, nn * 512:nn * 512 + 512]), waits=[t2, t3])
        t5 = P.op("se", lambda o, nn=nn: o.activation(
            rowB[0:1, nn * 512:nn * 512 + 512], rowC[0:1, nn * 512:nn * 512 + 512],
            AF.Ln, bias=eps_col[0:1, 0:1]), waits=[t4])
        t6 = P.op("se", lambda o, nn=nn: o.activation(
            rowD[0:1, nn * 512:nn * 512 + 512], rowB[0:1, nn * 512:nn * 512 + 512],
            AF.Exp, scale=-0.5), waits=[t5])
        t7 = P.op("se", lambda o, nn=nn: o.activation(
            rowC[0:1, nn * 512:nn * 512 + 512], rowC[0:1, nn * 512:nn * 512 + 512],
            AF.Ln, scale=2048.0), waits=[t5, t6])
        t8 = P.op("ve", lambda o, nn=nn: o.tensor_sub(
            rowB[0:1, nn * 512:nn * 512 + 512], rowB[0:1, nn * 512:nn * 512 + 512],
            rowC[0:1, nn * 512:nn * 512 + 512]), waits=[t7])
        t9 = P.op("se", lambda o, nn=nn: o.activation(
            rowC[0:1, nn * 512:nn * 512 + 512], rowB[0:1, nn * 512:nn * 512 + 512],
            AF.Exp, scale=0.5), waits=[t8])
        tmb_ = P.op("ve", lambda o, nn=nn: o.tensor_copy(
            rowbf[0:1, nn * 512:nn * 512 + 512], rowA[0:1, nn * 512:nn * 512 + 512]),
            waits=[t2, t9])
        trb_ = P.op("ve", lambda o, nn=nn: o.tensor_copy(
            rowbf2[0:1, nn * 512:nn * 512 + 512], rowD[0:1, nn * 512:nn * 512 + 512]),
            waits=[t6])
        r_toks += [tmb_, trb_, t9]
    INV_TOK = r_toks[-1]
    P.barrier()
    esl.close()
    esc.close()
    if stop_after == "mlp":
        return _render(nc, P, top)

    esf = ExitStack()
    x1bf = sb("x1bf", [128, NK * TH], BF16, esf)
    mu1_bc = sb("mu1_bc", [128, TH], F32, esf)
    rstd1_bc = sb("rstd1_bc", [128, TH], F32, esf)
    x1stage = sb("x1stage", [128, TH], F32, esf)
    s1buf = sb("s1buf", [128, TH], F32, esf)
    d_x1 = dbgt("x1", [D, TH])
    bt = []
    for nn in range(2):
        bt.append(bcast_f32(lambda nn=nn: rowbf[0:1, nn * 512:nn * 512 + 512],
                            lambda nn=nn: mu1_bc[:, nn * 512:nn * 512 + 512],
                            r_toks + bt, PS[3]))
        bt.append(bcast_f32(lambda nn=nn: rowbf2[0:1, nn * 512:nn * 512 + 512],
                            lambda nn=nn: rstd1_bc[:, nn * 512:nn * 512 + 512],
                            r_toks + bt, PS[4]))
    x1_toks = []
    prev = None
    for k in range(NK):
        trd = P.dma("sy", lambda o, k=k: o.dma_start(
            out=s1buf[:], in_=s1_scr[k * 128:(k + 1) * 128, :]),
            waits=bt + [prev])
        tsb = P.op("ve", lambda o: o.tensor_sub(x1stage[:], s1buf[:], mu1_bc[:]),
                   waits=[trd, prev] + bt)
        tmu = P.op("ve", lambda o: o.tensor_mul(x1stage[:], x1stage[:], rstd1_bc[:]),
                   waits=[tsb])
        tbf = P.op("ve", lambda o, k=k: o.tensor_copy(
            x1bf[:, k * TH:(k + 1) * TH], x1stage[:]), waits=[tmu])
        tst = P.dma("sy", lambda o, k=k: o.dma_start(
            out=x1_scr[k * 128:(k + 1) * 128, :], in_=x1stage[:]), waits=[tmu])
        if d_x1 is not None:
            P.dma("sy", lambda o, k=k: o.dma_start(
                out=dbg_aps["x1"][k * 128:(k + 1) * 128, :], in_=x1stage[:]),
                waits=[tmu])
        prev = tst
        x1_toks.append(tbf)

    if stop_after == "ffa":
        return _render(nc, P, top)
    invn_cols = sb("invn_cols", [128, 8], F32, esf)
    simb = sb("simb", [128, 16 * NE], BF16, esf)
    sqe = sb("sqe", [128, NE], BF16, esf)
    sgate_bc = sb("sgate_bc", [128, NE], F32, esf)
    cnorm_bc = sb("cnorm_bc", [128, NE], F32, esf)
    wtm = sb("wtm", [128, 8 * NE], F32, esf)
    wfm = sb("wfm", [NE, TH], F32, esf)
    wfmb = sb("wfmb", [NE, TH], BF16, esf)
    rt1 = sb("rt1", [128, NE], F32, esf)
    rt2 = sb("rt2", [128, NE], F32, esf)
    rt3 = sb("rt3", [128, NE], F32, esf)
    rt4 = sb("rt4", [128, NE], F32, esf)
    rt5 = sb("rt5", [128, NE], F32, esf)
    rcol = sb("rcol", [128, 1], F32, esf)
    rcol2 = sb("rcol2", [128, 1], F32, esf)
    rcol3 = sb("rcol3", [128, 1], F32, esf)
    negb = sb("negb", [128, 1], F32, esf)
    posb = sb("posb", [128, 1], F32, esf)
    qwb = [sb("qwb0", [128, 2048], BF16, esf)] * 2
    kwb = [sb("kwb0", [128, 2048], BF16, esf)] * 2
    vwb = [sb("vwb0", [128, 2048], BF16, esf)] * 2
    wbc = sb("wbc", [128, TH], F32, esf)
    qacc = [sb(f"qacc{i}", [128, TH], F32, esf) for i in range(2)]
    kacc = [sb(f"kacc{i}", [128, TH], F32, esf) for i in range(2)]
    vacc = [sb(f"vacc{i}", [128, TH], F32, esf) for i in range(2)]
    tmpe = sb("tmpe", [128, TH], F32, esf)
    qbf = sb("qbf", [128, TH], BF16, esf)
    kownbf = sb("kownbf", [128, TH], BF16, esf)
    vownbf = sb("vownbf", [128, TH], BF16, esf)
    vstg = sb("vstg", [128, TH], BF16, esf)
    kfm = sb("kfm", [128, T], BF16, esf)
    vtm = sb("vtm", [128, T], BF16, esf)
    pbf = sb("pbf", [128, T], BF16, esf)
    ptr_sb = sb("ptr_sb", [128, 128], BF16, esf)
    aofm = sb("aofm", [128, TH], BF16, esf)
    maskb = sb("maskb", [128, 16 * 128], BF16, esf)
    zall = sb("zall", [128, NE * TH], BF16, esf)
    owb = [sb(f"owb{i}", [128, D], BF16, esf) for i in range(2)]
    s2stage = sb("s2stage", [128, TH], F32, esf)
    s2bf = sb("s2bf", [128, TH], BF16, esf)
    sq2 = sb("sq2", [128, TH], BF16, esf)
    s2tok = sb("s2tok", [128, D], F32, esf)
    vcol = sb("vcol", [128, 8], F32, esf)
    mu2c = sb("mu2c", [128, 8], F32, esf)
    rstd2c = sb("rstd2c", [128, 8], F32, esf)

    d_w = dbgt("w", [TH, NE])
    d_qkv = dbgt("qkv_o", [3, 128, TH])
    d_ao = dbgt("ao", [128, TH])
    d_s2 = dbgt("s2", [D, TH])

    inv_toks = []
    for qc in range(8):
        tok = P.op("te", lambda o, qc=qc: o.transpose(
            PS[0][0:128, 0:1], rowC[0:1, qc * 128:(qc + 1) * 128], identf_s[0:1, 0:1]),
            waits=[INV_TOK] + inv_toks[-1:])
        inv_toks.append(P.op("se", lambda o, qc=qc: o.copy(
            invn_cols[:, qc:qc + 1], PS[0][0:128, 0:1]), waits=[tok]))

    tsim = P.dma2("gp", [
        lambda o: o.dma_start(out=simb[:, 0:8 * NE].rearrange("p (k e) -> p k e", k=8),
                              in_=sim_d[0:1024, :].rearrange("(k p) e -> p k e", p=128)),
        lambda o: o.dma_start(out=simb[:, 8 * NE:16 * NE].rearrange("p (k e) -> p k e", k=8),
                              in_=sim_d[1024:2048, :].rearrange("(k p) e -> p k e", p=128)),
    ], waits=[PAR_TOK])
    prev2 = None
    for k in range(NK):
        tq = P.op("se", lambda o, k=k: o.activation(
            sqe[:], simb[:, k * NE:(k + 1) * NE], AF.Square), waits=[tsim, prev2])
        prev2 = mm_group(PROW[0:1, 0:NE],
                         [(lambda: onesc_s[:], lambda: sqe[:])],
                         waits=[tq, prev2], start=(k == 0), stop=(k == NK - 1))
    tn1 = P.op("se", lambda o: o.activation(rowA[0:1, 0:NE], PROW[0:1, 0:NE], AF.Ln),
               waits=[prev2])
    tn2 = P.op("se", lambda o: o.activation(rowA[0:1, 0:NE], rowA[0:1, 0:NE],
                                            AF.Exp, scale=-0.5), waits=[tn1])
    tn3 = P.op("ve", lambda o: o.tensor_copy(rowbf[0:1, 0:NE], rowA[0:1, 0:NE]),
               waits=[tn2] + bt)
    tn5 = bcast_f32(lambda: rowbf[0:1, 0:NE], lambda: cnorm_bc[:], [tn3], PS[0], width=NE)
    tgd = P.dma("sy", lambda o: o.dma_start(out=rowB[0:1, 0:NE], in_=gates_d),
                waits=inv_toks)
    tg1 = P.op("se", lambda o: o.activation(rowB[0:1, 0:NE], rowB[0:1, 0:NE],
                                            AF.Sigmoid), waits=[tgd])
    tg2 = P.op("ve", lambda o: o.tensor_copy(rowbf2[0:1, 0:NE], rowB[0:1, 0:NE]),
               waits=[tg1] + bt)
    tg4 = bcast_f32(lambda: rowbf2[0:1, 0:NE], lambda: sgate_bc[:], [tg2], PS[1], width=NE)
    tnb = P.op("ve", lambda o: o.memset(negb[:], NEG))
    tpb = P.op("ve", lambda o: o.memset(posb[:], -NEG))

    w_toks = []
    for tcn in range(8):
        lmm = mm_group(
            PS[2][:, 0:NE],
            [(lambda k=k, tcn=tcn: x1bf[:, k * TH + tcn * 128:k * TH + tcn * 128 + 128],
              lambda k=k: simb[:, k * NE:(k + 1) * NE]) for k in range(NK)],
            waits=[tsim, tn5, tg4] + x1_toks + w_toks[-1:])
        tl1 = P.op("ve", lambda o, tcn=tcn: o.scalar_tensor_tensor(
            rt1[:], PS[2][:, 0:NE], invn_cols[:, tcn:tcn + 1], cnorm_bc[:],
            ALU.mult, ALU.mult), waits=[lmm, tn5] + inv_toks)
        tl2 = P.op("ve", lambda o: o.tensor_sub(rt1[:], rt1[:], sgate_bc[:]),
                   waits=[tl1, tg4])
        tmk = P.op("ve", lambda o: o.tensor_single_scalar(rt3[:], rt1[:], 0.0, ALU.is_gt),
                   waits=[tl2])
        tnp = P.op("ve", lambda o: o.reduce_sum(rcol[:], rt3[:], axis=AX.X), waits=[tmk])
        tmx = P.op("ve", lambda o: o.reduce_max(rcol2[:], rt1[:], axis=AX.X), waits=[tnp])
        teq = P.op("ve", lambda o: o.tensor_tensor(
            rt2[:], rt1[:], rcol2[:].broadcast_to([128, NE]), ALU.is_equal), waits=[tmx])
        tl2b = P.op("ve", lambda o: o.scalar_tensor_tensor(
            rt4[:], rt2[:], negb[:], rt1[:], ALU.mult, ALU.add), waits=[teq, tnb])
        tmx2 = P.op("ve", lambda o: o.reduce_max(rcol2[:], rt4[:], axis=AX.X),
                    waits=[tl2b])
        teq2 = P.op("ve", lambda o: o.tensor_tensor(
            rt5[:], rt4[:], rcol2[:].broadcast_to([128, NE]), ALU.is_equal),
            waits=[tmx2])
        tfb = P.op("ve", lambda o: o.tensor_add(rt2[:], rt2[:], rt5[:]), waits=[teq2])
        tin = P.op("ve", lambda o: o.tensor_single_scalar(rcol3[:], rcol[:], 0.5, ALU.is_lt),
                   waits=[tfb])
        tsc = P.op("ve", lambda o: o.tensor_scalar_mul(rt4[:], rt2[:], rcol3[:]),
                   waits=[tin])
        tmf = P.op("ve", lambda o: o.tensor_max(rt2[:], rt4[:], rt3[:]), waits=[tsc])
        tga = P.op("ve", lambda o: o.tensor_scalar_max(rt5[:], rt1[:], 0.0), waits=[tmf])
        tml = P.op("ve", lambda o: o.tensor_mul(rt5[:], rt5[:], rt2[:])
                   if False else o.tensor_mul(rt3[:], rt5[:], rt2[:]), waits=[tga])
        tm1 = P.op("ve", lambda o: o.tensor_single_scalar(
            rt5[:], rt2[:], 1.0, ALU.subtract), waits=[tml])
        tm2 = P.op("ve", lambda o: o.scalar_tensor_tensor(
            rt4[:], rt5[:], posb[:], rt3[:], ALU.mult, ALU.add), waits=[tm1, tpb])
        tsm = P.op("ve", lambda o: o.reduce_max(rcol2[:], rt4[:], axis=AX.X), waits=[tm2])
        tng = P.op("ve", lambda o: o.tensor_scalar_mul(rcol[:], rcol2[:], -1.0),
                   waits=[tsm])
        tex = P.op("se", lambda o: o.activation(rt5[:], rt4[:], AF.Exp, bias=rcol[:]),
                   waits=[tng])
        tss = P.op("ve", lambda o: o.reduce_sum(rcol2[:], rt5[:], axis=AX.X), waits=[tex])
        trc = P.op("ve", lambda o: o.reciprocal(rcol3[:], rcol2[:]), waits=[tss])
        tw_ = P.op("ve", lambda o, tcn=tcn: o.scalar_tensor_tensor(
            wtm[:, tcn * NE:(tcn + 1) * NE], rt5[:], rcol3[:], rt2[:],
            ALU.mult, ALU.mult), waits=[trc])
        w_toks.append(tw_)
        if d_w is not None:
            P.dma("sy", lambda o, tcn=tcn: o.dma_start(
                out=dbg_aps["w"][tcn * 128:(tcn + 1) * 128, :],
                in_=wtm[:, tcn * NE:(tcn + 1) * NE]), waits=[tw_])
    if stop_after == "ffb":
        return _render(nc, P, top)
    wfm_toks = []
    for tcn in range(8):
        tok = P.op("te", lambda o, tcn=tcn: o.transpose(
            PS[3][0:NE, 0:128], wtm[:, tcn * NE:(tcn + 1) * NE], identf_s[:]),
            waits=[w_toks[tcn]] + wfm_toks[-1:])
        wfm_toks.append(P.op("se", lambda o, tcn=tcn: o.copy(
            wfm[0:NE, tcn * 128:(tcn + 1) * 128], PS[3][0:NE, 0:128]), waits=[tok]))
    twfb = P.op("ve", lambda o: o.tensor_copy(wfmb[0:NE, :], wfm[0:NE, :]),
                waits=wfm_toks)

    if stop_after == "ffc":
        return _render(nc, P, top)
    qw_busy = [None, None]
    cur = {"q": None, "k": None, "v": None}
    prev_mm3 = {"q": None, "k": None, "v": None}
    import os as _os2
    for e in range(int(_os2.environ.get("N_E", str(NE)))):
        tqw = P.dma2("gp", [
            lambda o, e=e: o.dma_start(
                out=qwb[e % 2][:, 0:1024].rearrange("p (k h) -> p k h", k=8),
                in_=qkv_d[0, e, 0:1024, :].rearrange("(k p) h -> p k h", p=128)),
            lambda o, e=e: o.dma_start(
                out=qwb[e % 2][:, 1024:2048].rearrange("p (k h) -> p k h", k=8),
                in_=qkv_d[0, e, 1024:2048, :].rearrange("(k p) h -> p k h", p=128)),
        ], waits=[qw_busy[e % 2], PAR_TOK])
        tkw = P.dma2("gp", [
            lambda o, e=e: o.dma_start(
                out=kwb[e % 2][:, 0:1024].rearrange("p (k h) -> p k h", k=8),
                in_=qkv_d[1, e, 0:1024, :].rearrange("(k p) h -> p k h", p=128)),
            lambda o, e=e: o.dma_start(
                out=kwb[e % 2][:, 1024:2048].rearrange("p (k h) -> p k h", k=8),
                in_=qkv_d[1, e, 1024:2048, :].rearrange("(k p) h -> p k h", p=128)),
        ], waits=[qw_busy[e % 2], PAR_TOK])
        tvw = P.dma2("gp", [
            lambda o, e=e: o.dma_start(
                out=vwb[e % 2][:, 0:1024].rearrange("p (k h) -> p k h", k=8),
                in_=qkv_d[2, e, 0:1024, :].rearrange("(k p) h -> p k h", p=128)),
            lambda o, e=e: o.dma_start(
                out=vwb[e % 2][:, 1024:2048].rearrange("p (k h) -> p k h", k=8),
                in_=qkv_d[2, e, 1024:2048, :].rearrange("(k p) h -> p k h", p=128)),
        ], waits=[qw_busy[e % 2], PAR_TOK])
        tb1 = mm_group(PS[4][:, 0:512],
                       [(lambda e=e: sel16_s[:, e * 128:(e + 1) * 128],
                         lambda: wfmb[0:NE, 0:512])], waits=[twfb, cur["v"]])
        tb2 = P.op("se", lambda o: o.copy(wbc[:, 0:512], PS[4][:, 0:512]), waits=[tb1])
        tb3 = mm_group(PS[4][:, 0:512],
                       [(lambda e=e: sel16_s[:, e * 128:(e + 1) * 128],
                         lambda: wfmb[0:NE, 512:1024])], waits=[tb2])
        tb4 = P.op("se", lambda o: o.copy(wbc[:, 512:1024], PS[4][:, 0:512]), waits=[tb3])
        wbc_tok = [tb2, tb4]
        last_mm = None
        for name, wgt, tload, psix in (("q", qwb, tqw, 0), ("k", kwb, tkw, 1),
                                       ("v", vwb, tvw, 2)):
            src = {"q": qacc, "k": kacc, "v": vacc}[name]
            prev_add = None
            for nn in range(2):
                mmt = mm_group(
                    PS[psix][:, 0:512],
                    [(lambda k=k, e=e, wgt=wgt: wgt[e % 2][:, k * 128:(k + 1) * 128],
                      lambda k=k, nn=nn: x1bf[:, k * TH + nn * 512:k * TH + nn * 512 + 512])
                     for k in range(NK)],
                    waits=[tload, prev_mm3[name], twfb] + x1_toks
                          + ([prev_add] if prev_add else []))
                tmm = P.op("ve", lambda o, nn=nn, psix=psix: o.tensor_mul(
                    tmpe[:, nn * 512:nn * 512 + 512], PS[psix][:, 0:512],
                    wbc[:, nn * 512:nn * 512 + 512]), waits=[mmt] + wbc_tok)
                if e == 0:
                    tad = P.op("ve", lambda o, nn=nn, src=src: o.tensor_copy(
                        src[0][:, nn * 512:nn * 512 + 512],
                        tmpe[:, nn * 512:nn * 512 + 512]), waits=[tmm])
                else:
                    tad = P.op("ve", lambda o, nn=nn, src=src, e=e: o.tensor_add(
                        src[e % 2][:, nn * 512:nn * 512 + 512],
                        src[(e + 1) % 2][:, nn * 512:nn * 512 + 512],
                        tmpe[:, nn * 512:nn * 512 + 512]), waits=[tmm, cur[name]])
                prev_add = tad
            cur[name] = prev_add
            prev_mm3[name] = prev_add
            last_mm = prev_add
        qw_busy[0] = qw_busy[1] = last_mm
    last = (NE - 1) % 2
    tqb = P.op("ve", lambda o: o.tensor_copy(qbf[:], qacc[last][:]), waits=[cur["q"]])
    tkb = P.op("ve", lambda o: o.tensor_copy(kownbf[:], kacc[last][:]), waits=[cur["k"]])
    tvb = P.op("ve", lambda o: o.tensor_copy(vownbf[:], vacc[last][:]), waits=[cur["v"]])
    if d_qkv is not None:
        P.dma("sy", lambda o: o.dma_start(out=dbg_aps["qkv_o"][0], in_=qacc[last][:]),
              waits=[cur["q"]])
        P.dma("sy", lambda o: o.dma_start(out=dbg_aps["qkv_o"][1], in_=kacc[last][:]),
              waits=[cur["k"]])
        P.dma("sy", lambda o: o.dma_start(out=dbg_aps["qkv_o"][2], in_=vacc[last][:]),
              waits=[cur["v"]])

    if stop_after == "ffd":
        return _render(nc, P, top)
    vt_toks = []
    for tcn in range(8):
        tok = P.op("te", lambda o, tcn=tcn: o.transpose(
            PTB[:, 0:128], vownbf[:, tcn * 128:(tcn + 1) * 128], identb_s[:]),
            waits=[tvb] + vt_toks[-1:])
        vt_toks.append(P.op("ve", lambda o, tcn=tcn: o.tensor_copy(
            vstg[:, tcn * 128:(tcn + 1) * 128], PTB[:, 0:128]), waits=[tok]))

    if stop_after == "ffe":
        return _render(nc, P, top)
    def kv_store_k(o, sem):
        with o.If_eq(par_reg, 0):
            o.dma_start(out=kvx[0, 0], in_=kownbf[:]).then_inc(sem, 16)
        with o.Else():
            o.dma_start(out=kvx[1, 0], in_=kownbf[:]).then_inc(sem, 16)

    def kv_store_v(o, sem):
        with o.If_eq(par_reg, 0):
            o.dma_start(out=kvx[0, 1], in_=vstg[:]).then_inc(sem, 16)
        with o.Else():
            o.dma_start(out=kvx[1, 1], in_=vstg[:]).then_inc(sem, 16)
    tks = P.dma_manual("gp", kv_store_k, waits=[tkb, PAR_TOK])
    tvs = P.dma_manual("gp", kv_store_v, waits=vt_toks + [PAR_TOK])
    if stop_after == "kva":
        return _render(nc, P, top)
    gp_flag(2, 2, waits=[tks, tvs])
    if stop_after == "kvb":
        return _render(nc, P, top)
    t_pollkv = gp_poll(((1, 2), 2), ((0, 2), 2), waits=P.now())
    if stop_after == "kvc":
        return _render(nc, P, top)
    kv_toks = [
        P.dma("gp", lambda o: o.dma_start(out=kfm[:, 0:TH], in_=kvx[0, 0]),
              waits=[t_pollkv]),
        P.dma("gp", lambda o: o.dma_start(out=kfm[:, TH:T], in_=kvx[1, 0]),
              waits=[t_pollkv]),
        P.dma("gp", lambda o: o.dma_start(out=vtm[:, 0:TH], in_=kvx[0, 1]),
              waits=[t_pollkv]),
        P.dma("gp", lambda o: o.dma_start(out=vtm[:, TH:T], in_=kvx[1, 1]),
              waits=[t_pollkv]),
    ]

    if stop_after == "mix":
        return _render(nc, P, top)

    ps_stack.close()
    esa = ExitStack()
    PS2 = [pst(f"ps2_{i}", [128, 512], F32, esa) for i in range(3)]
    psA = pst("psA", [128, 2048], F32, esa)
    PTB2 = pst("ptb2", [128, 512], BF16, esa)

    tmask_busy = None
    ao_toks = []
    for qc in range(8):
        nkk = 9 + qc
        tml_ = P.dma2("gp", [
            lambda o, qc=qc: o.dma_start(
                out=maskb[:, 0:8 * 128].rearrange("p (c q) -> p c q", c=8),
                in_=masks_d[qc, 0:8].rearrange("c p q -> p c q")),
            lambda o, qc=qc: o.dma_start(
                out=maskb[:, 8 * 128:16 * 128].rearrange("p (c q) -> p c q", c=8),
                in_=masks_d[qc, 8:16].rearrange("c p q -> p c q")),
        ], waits=[tmask_busy, PAR_TOK])
        sc_toks = []
        for kkc in range(nkk):
            sc_toks.append(mm_group(
                psA[:, kkc * 128:(kkc + 1) * 128],
                [(lambda qc=qc: qbf[:, qc * 128:(qc + 1) * 128],
                  lambda kkc=kkc: kfm[:, kkc * 128:(kkc + 1) * 128]),
                 (lambda: identb_s[:],
                  lambda kkc=kkc: maskb[:, kkc * 128:(kkc + 1) * 128])],
                waits=[tml_, tqb] + kv_toks + ao_toks[-1:]))
        tmask_busy = sc_toks[-1]
        tmx_ = P.op("ve", lambda o, nkk=nkk: o.reduce_max(
            rcol[:], psA[:, 0:nkk * 128], axis=AX.X), waits=sc_toks)
        tng_ = P.op("ve", lambda o: o.tensor_scalar_mul(rcol[:], rcol[:], -ATT_SCALE),
                    waits=[tmx_])
        tex_ = P.op("se", lambda o, nkk=nkk: o.activation(
            pbf[:, 0:nkk * 128], psA[:, 0:nkk * 128], AF.Exp,
            bias=rcol[:], scale=ATT_SCALE), waits=[tng_])
        tsm_ = P.op("ve", lambda o, nkk=nkk: o.reduce_sum(
            rcol2[:], pbf[:, 0:nkk * 128], axis=AX.X), waits=[tex_])
        trc_ = P.op("ve", lambda o: o.reciprocal(rcol[:], rcol2[:]), waits=[tsm_])
        tp2_ = P.op("ve", lambda o, nkk=nkk: o.tensor_scalar_mul(
            pbf[:, 0:nkk * 128], pbf[:, 0:nkk * 128], rcol[:]), waits=[trc_])
        avt = None
        for kkc in range(nkk):
            ttr = P.op("te", lambda o, kkc=kkc: o.transpose(
                PTB2[:, 0:128], pbf[:, kkc * 128:(kkc + 1) * 128], identb_s[:]),
                waits=[tp2_] + ([avt] if avt else []))
            tpc = P.op("ve", lambda o: o.tensor_copy(ptr_sb[:], PTB2[:, 0:128]),
                       waits=[ttr])
            avt = mm_group(
                PS2[0][:, 0:128],
                [(lambda kkc=kkc: vtm[:, kkc * 128:(kkc + 1) * 128],
                  lambda: ptr_sb[:])],
                waits=[tpc] + kv_toks, start=(kkc == 0), stop=(kkc == nkk - 1))
        ao_toks.append(P.op("ve", lambda o, qc=qc: o.tensor_copy(
            aofm[:, qc * 128:(qc + 1) * 128], PS2[0][:, 0:128]), waits=[avt]))
    if d_ao is not None:
        P.dma("gp", lambda o: o.dma_start(out=dbg_aps["ao"], in_=aofm[:]),
              waits=ao_toks + [PAR_TOK])

    z_toks = []
    prev3 = None
    for e in range(NE):
        tb1 = mm_group(PS2[1][:, 0:512],
                       [(lambda e=e: sel16_s[:, e * 128:(e + 1) * 128],
                         lambda: wfmb[0:NE, 0:512])], waits=[twfb, prev3])
        tz1 = P.op("ve", lambda o, e=e: o.tensor_mul(
            zall[:, e * TH:e * TH + 512], aofm[:, 0:512], PS2[1][:, 0:512]),
            waits=[tb1] + ao_toks)
        tb2 = mm_group(PS2[1][:, 0:512],
                       [(lambda e=e: sel16_s[:, e * 128:(e + 1) * 128],
                         lambda: wfmb[0:NE, 512:1024])], waits=[tz1])
        tz2 = P.op("ve", lambda o, e=e: o.tensor_mul(
            zall[:, e * TH + 512:(e + 1) * TH], aofm[:, 512:1024], PS2[1][:, 0:512]),
            waits=[tb2] + ao_toks)
        prev3 = tz2
        z_toks += [tz1, tz2]

    esa.close()
    esb = ExitStack()
    PS3 = [pst(f"ps3_{i}", [128, 512], F32, esb) for i in range(5)]
    PROW2 = pst("prow2", [1, 1024], F32, esb)
    PTF = pst("ptf", [128, 512], F32, esb)

    ow_busy = [None, None]
    mean2_mm = {0: None, 1: None}
    ssq2_mm = {0: None, 1: None}
    s2_toks = []
    prev_ev = []
    for m in range(16):
        tow = P.dma2("gp", [
            lambda o, m=m: o.dma_start(
                out=owb[m % 2][:, 0:1024].rearrange("p (e c) -> p e c", e=8),
                in_=oproj_d[0:8, :, m * 128:(m + 1) * 128].rearrange("e p c -> p e c")),
            lambda o, m=m: o.dma_start(
                out=owb[m % 2][:, 1024:2048].rearrange("p (e c) -> p e c", e=8),
                in_=oproj_d[8:16, :, m * 128:(m + 1) * 128].rearrange("e p c -> p e c")),
        ], waits=[ow_busy[m % 2], PAR_TOK] + z_toks[-1:])
        evs = []
        mmt = None
        for nn in range(2):
            mmt = mm_group(
                PS3[0 + nn][:, 0:512],
                [(lambda e2=e2, m=m: owb[m % 2][:, e2 * 128:(e2 + 1) * 128],
                  lambda e2=e2, nn=nn: zall[:, e2 * TH + nn * 512:e2 * TH + nn * 512 + 512])
                 for e2 in range(NE)],
                waits=[tow] + z_toks + prev_ev)
            tx1r = P.dma("sy", lambda o, m=m, nn=nn: o.dma_start(
                out=x1stage[:, nn * 512:nn * 512 + 512],
                in_=x1_scr[m * 128:(m + 1) * 128, nn * 512:nn * 512 + 512]),
                waits=[mmt] + prev_ev)
            evs.append(P.op("ve", lambda o, nn=nn: o.tensor_add(
                s2stage[:, nn * 512:nn * 512 + 512], PS3[nn][:, 0:512],
                x1stage[:, nn * 512:nn * 512 + 512]), waits=[mmt, tx1r] + prev_ev))
        ow_busy[m % 2] = mmt
        tws2 = P.dma("sy", lambda o, m=m: o.dma_start(
            out=s2_scr[m * 128:(m + 1) * 128, :], in_=s2stage[:]), waits=evs)
        s2_toks.append(tws2)
        if d_s2 is not None:
            P.dma("sy", lambda o, m=m: o.dma_start(
                out=dbg_aps["s2"][m * 128:(m + 1) * 128, :], in_=s2stage[:]), waits=evs)
        tbf2 = P.op("ve", lambda o: o.tensor_copy(s2bf[:], s2stage[:]), waits=evs)
        tq2 = P.op("se", lambda o: o.activation(sq2[:], s2stage[:], AF.Square), waits=evs)
        for nn in range(2):
            psm = PS3[2] if nn == 0 else PS3[3]
            mean2_mm[nn] = mm_group(
                psm[0:1, 0:512],
                [(lambda: onesc_s[:], lambda nn=nn: s2bf[:, nn * 512:nn * 512 + 512])],
                waits=[tbf2, mean2_mm[nn]], start=(m == 0), stop=(m == 15))
            ssq2_mm[nn] = mm_group(
                PROW2[0:1, nn * 512:nn * 512 + 512],
                [(lambda: onesc_s[:], lambda nn=nn: sq2[:, nn * 512:nn * 512 + 512])],
                waits=[tq2, ssq2_mm[nn]], start=(m == 0), stop=(m == 15))
        prev_ev = [tws2, tbf2, tq2]

    fr = []
    for nn in range(2):
        psm = PS3[2] if nn == 0 else PS3[3]
        t1 = P.op("se", lambda o, nn=nn, psm=psm: o.activation(
            rowA[0:1, nn * 512:nn * 512 + 512], psm[0:1, 0:512], AF.Copy,
            scale=1.0 / 2048.0), waits=[mean2_mm[nn]])
        t2 = P.op("se", lambda o, nn=nn: o.activation(
            rowB[0:1, nn * 512:nn * 512 + 512], rowA[0:1, nn * 512:nn * 512 + 512],
            AF.Square), waits=[t1])
        t3 = P.op("se", lambda o, nn=nn: o.activation(
            rowC[0:1, nn * 512:nn * 512 + 512], PROW2[0:1, nn * 512:nn * 512 + 512],
            AF.Copy, scale=1.0 / 2048.0), waits=[ssq2_mm[nn]])
        t4 = P.op("ve", lambda o, nn=nn: o.tensor_sub(
            rowC[0:1, nn * 512:nn * 512 + 512], rowC[0:1, nn * 512:nn * 512 + 512],
            rowB[0:1, nn * 512:nn * 512 + 512]), waits=[t2, t3])
        fr += [t1, t4]
    col_toks = []
    for tcn in range(8):
        tok = P.op("te", lambda o, tcn=tcn: o.transpose(
            PTF[0:128, 0:1], rowA[0:1, tcn * 128:(tcn + 1) * 128], identf_s[0:1, 0:1]),
            waits=fr + col_toks[-1:])
        tmu = P.op("se", lambda o, tcn=tcn: o.copy(mu2c[:, tcn:tcn + 1],
                                                   PTF[0:128, 0:1]), waits=[tok])
        tok2 = P.op("te", lambda o, tcn=tcn: o.transpose(
            PTF[0:128, 0:1], rowC[0:1, tcn * 128:(tcn + 1) * 128], identf_s[0:1, 0:1]),
            waits=[tmu])
        tvc = P.op("se", lambda o, tcn=tcn: o.copy(vcol[:, tcn:tcn + 1],
                                                   PTF[0:128, 0:1]), waits=[tok2])
        tl_ = P.op("se", lambda o, tcn=tcn: o.activation(
            rstd2c[:, tcn:tcn + 1], vcol[:, tcn:tcn + 1], AF.Ln, bias=eps_col[:]),
            waits=[tvc])
        col_toks.append(P.op("se", lambda o, tcn=tcn: o.activation(
            rstd2c[:, tcn:tcn + 1], rstd2c[:, tcn:tcn + 1], AF.Exp, scale=-0.5),
            waits=[tl_]))

    P.barrier()
    out_toks = []
    for tcn in range(8):
        tr_toks = []
        for k in range(NK):
            trd = P.dma("sy", lambda o, k=k, tcn=tcn: o.dma_start(
                out=s2stage[:, 0:128],
                in_=s2_scr[k * 128:(k + 1) * 128, tcn * 128:(tcn + 1) * 128]),
                waits=tr_toks[-1:] + out_toks[-1:])
            ttr = P.op("te", lambda o: o.transpose(
                PTF[:, 0:128], s2stage[:, 0:128], identf_s[:]),
                waits=[trd] + tr_toks[-1:])
            tr_toks.append(P.op("ve", lambda o, k=k: o.tensor_copy(
                s2tok[:, k * 128:(k + 1) * 128], PTF[:, 0:128]), waits=[ttr]))
        tno = P.op("ve", lambda o, tcn=tcn: o.tensor_scalar(
            s2tok[:], s2tok[:], mu2c[:, tcn:tcn + 1], rstd2c[:, tcn:tcn + 1],
            ALU.subtract, ALU.mult), waits=tr_toks + col_toks)
        out_toks.append(P.dma("sy", lambda o, tcn=tcn: o.dma_start(
            out=out_d[tcn * 128:(tcn + 1) * 128, :], in_=s2tok[:]), waits=[tno]))
    esb.close()
    esf.close()

    P.barrier()
    return _render(nc, P, top)


def _render(nc, P, top):
    P.barrier()
    P.check_deadlock()
    semd = {}
    semstack = ExitStack()
    for s in ("te", "ve", "se", "gp", "d_sy", "d_gp", "gpx", "pf1", "pf2"):
        semd[s] = semstack.enter_context(nc.semaphore(s))
    with nc.Block() as block:
        @block.tensor
        def _(o):
            for f in P.fns["te"]:
                f(o, semd)

        @block.vector
        def _(o):
            for f in P.fns["ve"]:
                f(o, semd)

        @block.scalar
        def _(o):
            for f in P.fns["se"]:
                f(o, semd)

        @block.gpsimd
        def _(o):
            for f in P.fns["gp"]:
                f(o, semd)

        @block.sync
        def _(o):
            for f in P.fns["sy"]:
                f(o, semd)
    semstack.close()
    return nc


def kernel(**inputs):
    ins = host_prepare(inputs)
    nc = build_program()
    res = run_bass_kernel_spmd(nc, ins, core_ids=list(range(8)))
    out = np.empty((B, T, D), np.float32)
    for core in range(8):
        b, par = core // 2, core % 2
        out[b, par * TH:(par + 1) * TH, :] = res.results[core]["out"]
    return out


# version 02
# speedup vs baseline: 1.0073x; 1.0073x over previous
import numpy as np
import ml_dtypes
from contextlib import ExitStack

import concourse.bass as bass
import concourse.mybir as mybir
from concourse.bass_utils import run_bass_kernel_spmd

F32 = mybir.dt.float32
BF16 = mybir.dt.bfloat16
I32 = mybir.dt.int32
AF = mybir.ActivationFunctionType
ALU = mybir.AluOpType
AX = mybir.AxisListType
BF = ml_dtypes.bfloat16

B, T, D = 4, 2048, 2048
DI, DXB, DTR = 4096, 1024, 128
INTER = 8192
NE, HD = 16, 128
HALF = DI // 2
NCB, CBW = 16, 128
TCH = 1024
NK = D // 128
TH = T // 2
EPS = 1e-5
NEG = -1e9
ATT_SCALE = float(1.0 / np.sqrt(128.0))

ENGINES = ("te", "ve", "se", "gp", "sy")
PROG_SEMS = ("te", "ve", "se", "gp", "d_sy", "d_gp", "gpx")


class Prog:

    def __init__(self):
        self.fns = {e: [] for e in ENGINES}
        self.cnt = {s: 0 for s in PROG_SEMS}
        self.meta = []

    @staticmethod
    def merge(waits):
        m = {}
        for w in waits:
            if w is None:
                continue
            s, v = w
            if v > m.get(s, 0):
                m[s] = v
        return tuple(sorted(m.items()))

    def _emit(self, eng, f, waits, inc):
        waits = Prog.merge(waits)
        self.meta.append((eng, waits, inc))

        def g(ob, semd):
            for s, v in waits:
                ob.wait_ge(semd[s], v)
            inst = f(ob, semd) if getattr(f, "wants_semd", False) else f(ob)
            if inc is not None:
                s, _ = inc
                amt = 16 if s.startswith("d_") else 1
                if inst is not None:
                    inst.then_inc(semd[s], amt)
                else:
                    ob.sem_inc(semd[s], amt)
        self.fns[eng].append(g)

    def op(self, eng, f, waits=()):
        assert eng in ("te", "ve", "se", "gp")
        self.cnt[eng] += 1
        tok = (eng, self.cnt[eng])
        self._emit(eng, f, waits, tok)
        return tok

    def raw(self, eng, f, waits=()):
        self._emit(eng, f, waits, None)

    CHAIN_DMA = True

    def dma(self, eng, f, waits=()):
        sem = {"sy": "d_sy", "gp": "d_gp"}[eng]
        if Prog.CHAIN_DMA:
            waits = list(waits) + [(sem, self.cnt[sem])]
        self.cnt[sem] += 16
        tok = (sem, self.cnt[sem])
        self._emit(eng, f, waits, tok)
        return tok

    def dma_manual(self, eng, f, waits=()):
        sem = {"sy": "d_sy", "gp": "d_gp"}[eng]
        if Prog.CHAIN_DMA:
            waits = list(waits) + [(sem, self.cnt[sem])]
        self.cnt[sem] += 16
        tok = (sem, self.cnt[sem])

        @wants_semd
        def g(ob, semd):
            f(ob, semd[sem])
            return None
        self._emit(eng, g, waits, None)
        self.meta[-1] = (eng, self.meta[-1][1], tok)
        return tok

    def dma2(self, eng, fs, waits=()):
        tok = None
        for f in fs:
            tok = self.dma(eng, f, waits)
        return tok

    def mark(self, eng, f=None, waits=()):
        self.cnt["gpx"] += 1
        tok = ("gpx", self.cnt["gpx"])
        self._emit(eng, f or (lambda ob: None), waits, tok)
        return tok

    def now(self):
        return [(s, self.cnt[s]) for s in PROG_SEMS]

    def barrier(self):
        toks = self.now()
        for e in ENGINES:
            self.raw(e, lambda ob: None, waits=toks)
        return toks

    def check_deadlock(self):
        streams = {e: [] for e in ENGINES}
        for eng, waits, inc in self.meta:
            streams[eng].append((waits, inc))
        ptr = {e: 0 for e in ENGINES}
        vals = {s: 0 for s in PROG_SEMS}
        progress = True
        while progress:
            progress = False
            for e in ENGINES:
                while ptr[e] < len(streams[e]):
                    waits, inc = streams[e][ptr[e]]
                    if all(vals[s] >= v for s, v in waits):
                        if inc is not None:
                            s, n = inc
                            vals[s] = max(vals[s], n)
                        ptr[e] += 1
                        progress = True
                    else:
                        break
        stuck = {e: ptr[e] for e in ENGINES if ptr[e] < len(streams[e])}
        if stuck:
            lines = []
            for e, p in stuck.items():
                waits, _ = streams[e][p]
                unmet = [(s, v, vals[s]) for s, v in waits if vals[s] < v]
                lines.append(f"  {e}: op {p}/{len(streams[e])} unmet={unmet}")
            raise RuntimeError("schedule deadlock:\n" + "\n".join(lines))


def wants_semd(f):
    f.wants_semd = True
    return f



def host_prepare(inputs):
    x = np.asarray(inputs["L0_attn_hidden_states"], np.float32)
    w_rms1 = np.asarray(inputs["L0_attn_rms1_w"], np.float32)
    Win = np.asarray(inputs["L0_attn_in_proj_w"], np.float32)
    Wdt = np.asarray(inputs["L0_attn_dt_proj_w"], np.float32)
    dtb = np.asarray(inputs["L0_attn_dt_proj_b"], np.float32)
    Alog = np.asarray(inputs["L0_attn_A_log"], np.float32)
    Dsk = np.asarray(inputs["L0_attn_D_skip"], np.float32)
    Wout = np.asarray(inputs["L0_attn_out_proj_w"], np.float32)
    w_rms2 = np.asarray(inputs["L0_attn_rms2_w"], np.float32)
    WgT = np.ascontiguousarray(np.asarray(inputs["L0_attn_gate_w"], np.float32).T)
    WuT = np.ascontiguousarray(np.asarray(inputs["L0_attn_up_w"], np.float32).T)
    WdT = np.ascontiguousarray(np.asarray(inputs["L0_attn_down_w"], np.float32).T)
    sim = np.asarray(inputs["L0_ffn_sim_matrix"], np.float32)
    gates = np.asarray(inputs["L0_ffn_gates"], np.float32)
    Q = np.asarray(inputs["L0_ffn_q_proj"], np.float32)
    K = np.asarray(inputs["L0_ffn_k_proj"], np.float32)
    V = np.asarray(inputs["L0_ffn_v_proj"], np.float32)
    O = np.asarray(inputs["L0_ffn_o_proj"], np.float32)

    WinT = np.ascontiguousarray(Win.T)
    WdtT = np.ascontiguousarray(Wdt.T)
    WoutT = np.ascontiguousarray(Wout.T)

    per_half = {}
    for half in (0, 1):
        zoff = half * HALF
        cols = []
        for cb in range(NCB):
            base = zoff + cb * CBW
            cols.extend(range(base, base + CBW))
            cols.extend(range(6144 + base, 6144 + base + CBW))
            xvrows = sorted({((base + c) // 64) * 16 + ((base + c) % 16) for c in range(CBW)})
            cols.extend(4096 + r for r in xvrows)
            cols.extend(5120 + r for r in xvrows)
        cols.extend(range(10240, 10240 + DTR))
        win_slice = np.ascontiguousarray(WinT[:, cols])

        Aperm = np.empty((128, 256), np.float32)
        cc_i = np.arange(128) // 16
        n_i = np.arange(128) % 16
        for st in range(256):
            cb, sl = st // 16, st % 16
            Aperm[:, st] = Alog[zoff + cb * CBW + cc_i * 16 + sl, n_i]
        per_half[half] = dict(
            win=win_slice,
            wout=np.ascontiguousarray(WoutT[zoff:zoff + HALF, :]),
            wdt=np.ascontiguousarray(WdtT[:, zoff:zoff + HALF]).astype(BF),
            aperm=Aperm,
            d_arr=Dsk[zoff:zoff + HALF].reshape(NCB, CBW).T.copy(),
            dtb_arr=dtb[zoff:zoff + HALF].reshape(NCB, CBW).T.copy(),
        )

    S_dt = np.zeros((128, 16 * 128), np.float32)
    S_ns = np.zeros((128, 16 * 128), np.float32)
    for sl in range(16):
        for p in range(128):
            S_dt[(p // 16) * 16 + sl, sl * 128 + p] = 1.0
            S_ns[p, sl * 128 + (p // 16) * 16 + sl] = 1.0
    S_xf = np.zeros((64, 128), np.float32)
    S_B = np.zeros((64, 128), np.float32)
    for c in range(128):
        S_xf[(c // 64) * 16 + (c % 16), c] = 1.0
        S_B[32 + ((c // 16) // 4) * 16 + (c % 16), c] = 1.0

    tri = np.triu(np.full((128, 128), NEG, np.float32), 1)
    fullm = np.full((128, 128), NEG, np.float32)
    masks = {}
    for par in (0, 1):
        m = np.zeros((8, 16, 128, 128), np.float32)
        for qc in range(8):
            dg = par * 8 + qc
            for kkc in range(16):
                if kkc == dg:
                    m[qc, kkc] = tri
                elif kkc > dg:
                    m[qc, kkc] = fullm
        masks[par] = np.ascontiguousarray(m.astype(BF))

    nonces = np.random.default_rng().integers(1, 2**30, size=16).astype(np.int32)
    sel16 = np.zeros((16, 16 * 128), np.float32)
    for e in range(16):
        sel16[e, e * 128:(e + 1) * 128] = 1.0
    w1_arr = w_rms1.reshape(NK, 128).T.copy()
    w2_arr = w_rms2.reshape(NK, 128).T.copy()
    qkv = np.ascontiguousarray(np.stack([Q, K, V], 0))

    ins = []
    for core in range(8):
        b, par = core // 2, core % 2
        meta = np.zeros((1, 32), np.int32)
        meta[0, 0] = par
        meta[0, 1:17] = nonces
        ph = per_half[par]
        xt = np.ascontiguousarray(x[b].T)
        ins.append({
            "x_t": xt,
            "x_h": np.ascontiguousarray(xt[:, par * TH:(par + 1) * TH]),
            "win": ph["win"], "wdt": ph["wdt"], "wout": ph["wout"],
            "aperm": ph["aperm"], "d_arr": ph["d_arr"], "dtb_arr": ph["dtb_arr"],
            "w1_arr": w1_arr, "w2_arr": w2_arr,
            "wg": WgT, "wu": WuT, "wd": WdT,
            "sim": sim, "gates": np.ascontiguousarray(gates.reshape(1, NE)),
            "qkv": qkv, "oproj": O,
            "sdt": S_dt.astype(BF), "sns": S_ns.astype(BF),
            "sxf": S_xf.astype(BF), "sbsel": S_B.astype(BF),
            "identb": np.eye(128, dtype=np.float32).astype(BF),
            "sel16": sel16.astype(BF),
            "identf": np.eye(128, dtype=np.float32),
            "ones_col": np.ones((128, 1), np.float32).astype(BF),
            "ones_row": np.ones((1, 128), np.float32).astype(BF),
            "masks": masks[par],
            "meta": meta,
        })
    return ins



def build_program(dbg=(), stop_after=None, n_cb=NCB, n_tc=2):
    nc = bass.Bass("TRN2", target_bir_lowering=False)
    P = Prog()
    top = ExitStack()

    def sb(name, shape, dt, stack=None):
        return (stack or top).enter_context(nc.sbuf_tensor(name, shape, dt))

    def pst(name, shape, dt=F32, stack=None):
        return (stack or top).enter_context(nc.psum_tensor(name, shape, dt))

    def din(name, shape, dt):
        return nc.dram_tensor(name, shape, dt, kind="ExternalInput").ap()

    x_t = din("x_t", [D, T], F32)
    x_h = din("x_h", [D, TH], F32)
    win = din("win", [D, NCB * 320 + 128], F32)
    wdt = din("wdt", [128, HALF], BF16)
    wout = din("wout", [HALF, D], F32)
    aperm = din("aperm", [128, 256], F32)
    d_arr_d = din("d_arr", [128, 16], F32)
    dtb_d = din("dtb_arr", [128, 16], F32)
    w1_d = din("w1_arr", [128, 16], F32)
    w2_d = din("w2_arr", [128, 16], F32)
    wg_d = din("wg", [D, INTER], F32)
    wu_d = din("wu", [D, INTER], F32)
    wd_d = din("wd", [INTER, D], F32)
    sim_d = din("sim", [D, NE], F32)
    gates_d = din("gates", [1, NE], F32)
    qkv_d = din("qkv", [3, NE, D, HD], F32)
    oproj_d = din("oproj", [NE, HD, D], F32)
    sdt_d = din("sdt", [128, 2048], BF16)
    sns_d = din("sns", [128, 2048], BF16)
    sxf_d = din("sxf", [64, 128], BF16)
    sbsel_d = din("sbsel", [64, 128], BF16)
    identb_d = din("identb", [128, 128], BF16)
    sel16_d = din("sel16", [16, 16 * 128], BF16)
    identf_d = din("identf", [128, 128], F32)
    onesc_d = din("ones_col", [128, 1], BF16)
    onesr_d = din("ones_row", [1, 128], BF16)
    masks_d = din("masks", [8, 16, 128, 128], BF16)
    meta_d = din("meta", [1, 32], I32)

    out_d = nc.dram_tensor("out", [TH, D], F32, kind="ExternalOutput").ap()

    dbg_aps = {}
    def dbgt(name, shape):
        if name in dbg and name not in dbg_aps:
            dbg_aps[name] = nc.dram_tensor("dbg_" + name, shape, F32,
                                           kind="ExternalOutput").ap()
        return dbg_aps.get(name)

    h_scr = nc.dram_tensor("h_scr", [D, TH], F32).ap()
    s1_scr = nc.dram_tensor("s1_scr", [D, TH], F32).ap()
    x1_scr = nc.dram_tensor("x1_scr", [D, TH], F32).ap()
    s2_scr = nc.dram_tensor("s2_scr", [D, TH], F32).ap()
    macc_scr = nc.dram_tensor("macc_scr", [D, TH], F32).ap()
    ex4 = nc.dram_tensor("ex4", [4, D, TCH], F32, addr_space="Shared").ap()
    kvx = nc.dram_tensor("kvx", [2, 2, 128, TH], BF16, addr_space="Shared").ap()
    flg = nc.dram_tensor("flg", [2, 8, 16], I32, addr_space="Shared").ap()

    smeta = sb("smeta", [1, 32], I32)
    spoll = sb("spoll", [1, 16], I32)
    identb_s = sb("identb_s", [128, 128], BF16)
    sel16_s = sb("sel16_s", [16, 16 * 128], BF16)
    identf_s = sb("identf_s", [128, 128], F32)
    onesc_s = sb("onesc_s", [128, 1], BF16)
    onesr_s = sb("onesr_s", [1, 128], BF16)
    w1_s = sb("w1_s", [128, 16], F32)
    w2_s = sb("w2_s", [128, 16], F32)
    rowA = sb("rowA", [1, TH], F32)
    rowB = sb("rowB", [1, TH], F32)
    rowC = sb("rowC", [1, TH], F32)
    rowD = sb("rowD", [1, TH], F32)
    rowbf = sb("rowbf", [1, TH], BF16)
    rowbf2 = sb("rowbf2", [1, TH], BF16)
    eps_col = sb("eps_col", [128, 1], F32)
    one_col = sb("one_col", [128, 1], F32)
    esm = ExitStack()
    sdt_s = sb("sdt_s", [128, 2048], BF16, esm)
    sns_s = sb("sns_s", [128, 2048], BF16, esm)
    sxf_s = sb("sxf_s", [64, 128], BF16, esm)
    sbsel_s = sb("sbsel_s", [64, 128], BF16, esm)
    ascan = sb("ascan", [128, 256], F32, esm)
    darr_s = sb("darr_s", [128, 16], F32, esm)
    dtb_s = sb("dtb_s", [128, 16], F32, esm)
    carry = sb("carry", [128, 256], F32, esm)
    wdt_s = sb("wdt_s", [128, HALF], BF16, esm)

    t_meta = P.dma("sy", lambda o: o.dma_start(out=smeta[:], in_=meta_d))
    ct = [t_meta]
    for s, dd in ((sdt_s, sdt_d), (sns_s, sns_d), (sxf_s, sxf_d), (sbsel_s, sbsel_d),
                  (identb_s, identb_d), (sel16_s, sel16_d), (identf_s, identf_d),
                  (onesc_s, onesc_d), (onesr_s, onesr_d), (darr_s, d_arr_d),
                  (dtb_s, dtb_d), (w1_s, w1_d), (w2_s, w2_d), (wdt_s, wdt)):
        ct.append(P.dma("sy", lambda o, s=s, dd=dd: o.dma_start(out=s[:], in_=dd)))
    ct.append(P.op("ve", lambda o: o.memset(eps_col[:], EPS)))
    ct.append(P.op("ve", lambda o: o.memset(one_col[:], 1.0)))
    t_ap = P.dma("sy", lambda o: o.dma_start(out=ascan[:], in_=aperm))
    t_a1 = P.op("se", lambda o: o.activation(carry[:], ascan[:], AF.Exp), waits=[t_ap])
    ct.append(P.op("se", lambda o: o.mul(ascan[:], carry[:], -1.0), waits=[t_a1]))
    CR = list(Prog.merge(ct))

    par_reg = nc.gpsimd.alloc_register("par_reg")
    PAR_TOK = P.mark("gp", lambda o: o.reg_load(par_reg, smeta[0:1, 0:1]),
                     waits=[t_meta])

    ps_stack = ExitStack()
    PS = [pst(f"ps{i}", [128, 512], F32, ps_stack) for i in range(5)]
    PROW = pst("prow", [1, 1024], F32, ps_stack)
    PTB = pst("ptb", [128, 512], BF16, ps_stack)

    def mm_group(psum_ap, tiles, waits, start=True, stop=True):
        n = len(tiles)
        def f(o):
            inst = None
            for i, (lf, rf) in enumerate(tiles):
                inst = o.matmul(psum_ap, lf(), rf(),
                                start=(start and i == 0), stop=(stop and i == n - 1))
            return inst
        return P.op("te", f, waits=waits)

    def bcast_f32(row_bf_fn, dst_fn, waits, ps_tile, width=512):
        t1 = mm_group(ps_tile[:, 0:width], [(lambda: onesr_s[:], row_bf_fn)], waits=waits)
        return P.op("se", lambda o: o.copy(dst_fn(), ps_tile[:, 0:width]), waits=[t1])

    poll_id = [0]

    def gp_poll(spec_p0, spec_p1, waits):
        (s0, n0), (s1, n1) = spec_p0, spec_p1
        poll_id[0] += 1
        pid = poll_id[0]
        psem = f"pf{pid}"

        @wants_semd
        def f(o, semd):
            with o.register(f"rn{pid}") as rn, o.register(f"fv{pid}") as fv, \
                 o.register(f"cnt{pid}") as cnt, o.register(f"nd{pid}") as nd:
                with o.If_eq(par_reg, 0):
                    o.reg_load(rn, smeta[0:1, 1 + n0:2 + n0])
                with o.Else():
                    o.reg_load(rn, smeta[0:1, 1 + n1:2 + n1])
                o.reg_mov(nd, 1)
                o.reg_mov(cnt, 0)
                with o.While(nd):
                    with o.If_eq(par_reg, 0):
                        o.dma_start(out=spoll[0:1, 0:1],
                                    in_=flg[s0[0], s0[1], 0:1]).then_inc(semd[psem], 16)
                    with o.Else():
                        o.dma_start(out=spoll[0:1, 0:1],
                                    in_=flg[s1[0], s1[1], 0:1]).then_inc(semd[psem], 16)
                    o.reg_add(cnt, cnt, 16)
                    o.wait_ge(semd[psem], cnt)
                    o.reg_load(fv, spoll[0:1, 0:1])
                    o.reg_alu(nd, fv, rn, ALU.not_equal)
            return None
        return P.mark("gp", f, waits=list(waits) + [PAR_TOK])

    def gp_flag(slot, nidx, waits):
        def f(o, sem):
            with o.If_eq(par_reg, 0):
                o.dma_start(out=flg[0, slot, 0:1],
                            in_=smeta[0:1, 1 + nidx:2 + nidx]).then_inc(sem, 16)
            with o.Else():
                o.dma_start(out=flg[1, slot, 0:1],
                            in_=smeta[0:1, 1 + nidx:2 + nidx]).then_inc(sem, 16)
        return P.dma_manual("gp", f, waits=list(waits) + [PAR_TOK])

    xnorm = sb("xnorm", [128, NK * TCH], BF16, esm)
    yg = sb("yg", [128, NCB * TCH], BF16, esm)
    dtin_sb = sb("dtin_sb", [128, TCH], BF16, esm)
    rstd_bc = sb("rstd_bc", [128, TCH], F32, esm)
    xtl = [sb(f"xtl{i}", [128, TCH], F32, esm) for i in range(2)]
    sqb = sb("sqb", [128, 2 * TCH], BF16, esm)
    g_cb = sb("g_cb", [128, TCH], F32, esm)
    cm_cb = sb("cm_cb", [128, TCH], F32, esm)
    xb_cb = sb("xb_cb", [64, TCH], BF16, esm)
    dtsp = sb("dtsp", [128, TCH], F32, esm)
    dtspb = sb("dtspb", [128, TCH], BF16, esm)
    dtx = sb("dtx", [128, TCH], F32, esm)
    dtxb = sb("dtxb", [128, TCH], BF16, esm)
    xfb = sb("xfb", [128, TCH], BF16, esm)
    brep = sb("brep", [128, TCH], F32, esm)
    diagd = sb("diagd", [128, 128], BF16, esm)
    sp_l = sb("sp_l", [128, TCH], F32, esm)
    da_sbs = [sb(f"da_sb{i}", [128, TCH], F32, esm) for i in range(2)]
    dbx_sb = sb("dbx_sb", [128, TCH], F32, esm)
    h_sb = sb("h_sb", [128, TCH], F32, esm)
    m_sb = sb("m_sb", [128, TCH], BF16, esm)
    wcb = [sb(f"wcb{i}", [128, 16 * 320], BF16, esm) for i in range(2)]
    wob = [sb(f"wob{i}", [128, 16 * 128], BF16, esm) for i in range(2)]
    stage = sb("stage", [128, TCH], F32, esm)

    d_xn = dbgt("xn", [NK, 128, T])
    d_zx = dbgt("zx", [NCB, 5, 128, TCH])
    d_hh = dbgt("hh", [3, 128, TCH])
    d_y = dbgt("y", [NCB, 128, T])
    d_op = dbgt("op", [D, T])

    wcb_busy = [None, None]
    wob_busy = [None, None]

    def load_kpm(eng, dst, src_cols, kcnt, mw, waits):
        tok = None
        for k0 in range(0, kcnt, 8):
            kc = min(8, kcnt - k0)
            def f(o, k0=k0, kc=kc):
                return o.dma_start(
                    out=dst[:, k0 * mw:(k0 + kc) * mw].rearrange("p (k m) -> p k m", k=kc),
                    in_=src_cols[k0 * 128:(k0 + kc) * 128, :].rearrange("(k p) m -> p k m", p=128))
            tok = P.dma(eng, f, waits)
        return tok

    for tc in range(n_tc):
        t0 = tc * TCH

        prev_mm = {0: None, 1: None}
        sq_of = {}
        for k in range(NK):
            tx = P.dma("sy", lambda o, k=k, t0=t0: o.dma_start(
                out=xtl[k % 2][:], in_=x_t[k * 128:(k + 1) * 128, t0:t0 + TCH]),
                waits=CR + [sq_of.get(k - 2)])
            tq = P.op("se", lambda o, k=k: o.activation(
                sqb[:, (k % 2) * TCH:(k % 2) * TCH + TCH], xtl[k % 2][:], AF.Square),
                waits=[tx, prev_mm[1]])
            sq_of[k] = tq
            for nn in range(2):
                prev_mm[nn] = mm_group(
                    PROW[0:1, nn * 512:nn * 512 + 512],
                    [(lambda: onesc_s[:],
                      lambda k=k, nn=nn: sqb[:, (k % 2) * TCH + nn * 512:(k % 2) * TCH + nn * 512 + 512])],
                    waits=[tq, prev_mm[nn]], start=(k == 0), stop=(k == NK - 1))
        st_toks = []
        for nn in range(2):
            tr1 = P.op("se", lambda o, nn=nn: o.activation(
                rowA[0:1, nn * 512:nn * 512 + 512], PROW[0:1, nn * 512:nn * 512 + 512],
                AF.Ln, bias=eps_col[0:1, 0:1], scale=1.0 / 2048.0), waits=[prev_mm[nn]])
            tr2 = P.op("se", lambda o, nn=nn: o.activation(
                rowA[0:1, nn * 512:nn * 512 + 512], rowA[0:1, nn * 512:nn * 512 + 512],
                AF.Exp, scale=-0.5), waits=[tr1])
            trb = P.op("ve", lambda o, nn=nn: o.tensor_copy(
                rowbf[0:1, nn * 512:nn * 512 + 512], rowA[0:1, nn * 512:nn * 512 + 512]),
                waits=[tr2])
            st_toks.append(bcast_f32(
                lambda nn=nn: rowbf[0:1, nn * 512:nn * 512 + 512],
                lambda nn=nn: rstd_bc[:, nn * 512:nn * 512 + 512],
                [trb] + st_toks, PS[4]))
        xn_tok = [None] * NK
        for k in range(NK):
            tx = P.dma("sy", lambda o, k=k, t0=t0: o.dma_start(
                out=xtl[k % 2][:], in_=x_t[k * 128:(k + 1) * 128, t0:t0 + TCH]),
                waits=st_toks + [xn_tok[k - 2] if k >= 2 else None])
            xn_tok[k] = P.op("ve", lambda o, k=k: o.scalar_tensor_tensor(
                xnorm[:, k * TCH:(k + 1) * TCH], xtl[k % 2][:],
                w1_s[:, k:k + 1], rstd_bc[:], ALU.mult, ALU.mult),
                waits=[tx] + st_toks)
            if d_xn is not None:
                P.dma("gp", lambda o, k=k, t0=t0: o.dma_start(
                    out=dbg_aps["xn"][k, :, t0:t0 + TCH],
                    in_=xnorm[:, k * TCH:(k + 1) * TCH]), waits=[xn_tok[k], PAR_TOK])

        if stop_after == "m0":
            return _render(nc, P, top)
        twd = load_kpm("gp", wcb[0], win[:, NCB * 320:NCB * 320 + 128], NK, 128,
                       [wcb_busy[0], PAR_TOK])
        dtin_toks = []
        prev = None
        for nn in range(2):
            tok = mm_group(PS[0][:, 0:512],
                           [(lambda k=k: wcb[0][:, k * 128:(k + 1) * 128],
                             lambda k=k, nn=nn: xnorm[:, k * TCH + nn * 512:k * TCH + nn * 512 + 512])
                            for k in range(NK)],
                           waits=[twd, prev] + xn_tok)
            prev = P.op("ve", lambda o, nn=nn: o.tensor_copy(
                dtin_sb[:, nn * 512:nn * 512 + 512], PS[0][:, 0:512]), waits=[tok])
            dtin_toks.append(prev)
        wcb_busy[0] = dtin_toks[-1]

        if stop_after == "m1":
            return _render(nc, P, top)
        for cb in range(n_cb):
            wb = wcb[cb % 2]
            tw = load_kpm("gp", wb, win[:, cb * 320:(cb + 1) * 320], NK, 320,
                          [wcb_busy[cb % 2], PAR_TOK])

            def rhs(k, nn):
                return xnorm[:, k * TCH + nn * 512:k * TCH + nn * 512 + 512]

            z_mm, cm_mm = [], []
            for nn in range(2):
                z_mm.append(mm_group(
                    PS[0 + nn][:, 0:512],
                    [(lambda k=k, wb=wb: wb[:, k * 320:k * 320 + 128],
                      lambda k=k, nn=nn: rhs(k, nn)) for k in range(NK)],
                    waits=[tw] + xn_tok))
                cm_mm.append(mm_group(
                    PS[2 + nn][:, 0:512],
                    [(lambda k=k, wb=wb: wb[:, k * 320 + 128:k * 320 + 256],
                      lambda k=k, nn=nn: rhs(k, nn)) for k in range(NK)],
                    waits=[tw] + xn_tok))
            tg = [P.op("se", lambda o, nn=nn: o.activation(
                g_cb[:, nn * 512:nn * 512 + 512], PS[0 + nn][:, 0:512], AF.Silu),
                waits=[z_mm[nn]]) for nn in range(2)]
            tcm = [P.op("ve", lambda o, nn=nn: o.tensor_copy(
                cm_cb[:, nn * 512:nn * 512 + 512], PS[2 + nn][:, 0:512]),
                waits=[cm_mm[nn]]) for nn in range(2)]
            txb = []
            for nn in range(2):
                tok = mm_group(
                    PS[4][0:64, 0:512],
                    [(lambda k=k, wb=wb: wb[:, k * 320 + 256:k * 320 + 320],
                      lambda k=k, nn=nn: rhs(k, nn)) for k in range(NK)],
                    waits=[tw] + xn_tok + txb)
                txb.append(P.op("ve", lambda o, nn=nn: o.tensor_copy(
                    xb_cb[0:64, nn * 512:nn * 512 + 512], PS[4][0:64, 0:512]),
                    waits=[tok]))
            if stop_after == "m2a":
                return _render(nc, P, top)
            dt_done = []
            for nn in range(2):
                tok = mm_group(PS[4][:, 0:512],
                               [(lambda cb=cb: wdt_s[:, cb * 128:(cb + 1) * 128],
                                 lambda nn=nn: dtin_sb[:, nn * 512:nn * 512 + 512])],
                               waits=txb + dtin_toks + dt_done)
                trm = P.op("se", lambda o, nn=nn, cb=cb: o.activation(
                    dtsp[:, nn * 512:nn * 512 + 512], PS[4][:, 0:512], AF.Relu,
                    bias=dtb_s[:, cb:cb + 1]), waits=[tok])
                ta = P.op("se", lambda o, nn=nn, cb=cb: o.activation(
                    sp_l[:, nn * 512:nn * 512 + 512], PS[4][:, 0:512], AF.Abs,
                    bias=dtb_s[:, cb:cb + 1]), waits=[tok])
                te_ = P.op("se", lambda o, nn=nn: o.activation(
                    sp_l[:, nn * 512:nn * 512 + 512], sp_l[:, nn * 512:nn * 512 + 512],
                    AF.Exp, scale=-1.0), waits=[ta])
                tl = P.op("se", lambda o, nn=nn: o.activation(
                    sp_l[:, nn * 512:nn * 512 + 512], sp_l[:, nn * 512:nn * 512 + 512],
                    AF.Ln, bias=one_col[:]), waits=[te_])
                tsp = P.op("ve", lambda o, nn=nn: o.tensor_add(
                    dtsp[:, nn * 512:nn * 512 + 512], dtsp[:, nn * 512:nn * 512 + 512],
                    sp_l[:, nn * 512:nn * 512 + 512]), waits=[trm, tl])
                dt_done.append(tsp)
            tspb = P.op("ve", lambda o: o.tensor_copy(dtspb[:], dtsp[:]), waits=dt_done)

            if stop_after == "m2b":
                return _render(nc, P, top)
            dtx_done, xf_done = [], []
            for nn in range(2):
                tok = mm_group(PS[4][:, 0:512],
                               [(lambda: sxf_s[:],
                                 lambda nn=nn: xb_cb[0:64, nn * 512:nn * 512 + 512])],
                               waits=txb + dt_done + dtx_done + xf_done)
                td = P.op("ve", lambda o, nn=nn: o.tensor_mul(
                    dtx[:, nn * 512:nn * 512 + 512], dtsp[:, nn * 512:nn * 512 + 512],
                    PS[4][:, 0:512]), waits=[tok] + dt_done)
                tf = P.op("se", lambda o, nn=nn: o.copy(
                    xfb[:, nn * 512:nn * 512 + 512], PS[4][:, 0:512]), waits=[tok, td])
                dtx_done.append(td)
                xf_done.append(tf)
            if stop_after == "m2b1":
                return _render(nc, P, top)
            tdxb = P.op("ve", lambda o: o.tensor_copy(dtxb[:], dtx[:]), waits=dtx_done)
            br_done = []
            for nn in range(2):
                tok = mm_group(PS[4][:, 0:512],
                               [(lambda: sbsel_s[:],
                                 lambda nn=nn: xb_cb[0:64, nn * 512:nn * 512 + 512])],
                               waits=dtx_done + xf_done + br_done)
                br_done.append(P.op("ve", lambda o, nn=nn: o.tensor_copy(
                    brep[:, nn * 512:nn * 512 + 512], PS[4][:, 0:512]), waits=[tok]))
            if stop_after == "m2b2":
                return _render(nc, P, top)
            tdg = P.op("ve", lambda o, cb=cb: o.tensor_scalar_mul(
                diagd[:], identb_s[:], darr_s[:, cb:cb + 1]), waits=CR + br_done)
            if stop_after == "m2b3":
                return _render(nc, P, top)

            if d_zx is not None and tc == 0:
                for j, (buf, wt) in enumerate(((g_cb, tg), (cm_cb, tcm), (dtsp, dt_done),
                                               (dtx, dtx_done), (brep, br_done))):
                    P.dma("sy", lambda o, cb=cb, j=j, buf=buf: o.dma_start(
                        out=dbg_aps["zx"][cb, j], in_=buf[:]), waits=list(wt))

            if stop_after == "m2c":
                return _render(nc, P, top)
            ys_mm = {}
            for nn in range(2):
                ys_mm[nn] = mm_group(
                    PS[2 + nn][:, 0:512],
                    [(lambda: diagd[:], lambda nn=nn: xfb[:, nn * 512:nn * 512 + 512])],
                    waits=[tdg] + xf_done + tcm, start=True, stop=False)
            scan_hist, ys_hist, db_hist = {}, {}, {}
            for sl in range(16):
                st = cb * 16 + sl
                par = sl % 2
                pb0 = PS[0] if par == 0 else PS[4]
                da_toks, db_toks = [], []
                for nn in range(2):
                    psel = pb0 if nn == 0 else PS[1]
                    if nn == 0:
                        bank_free = [db_hist[(sl - 2, 0)]] if sl >= 2 else \
                            ([tg[0]] if sl == 0 else list(br_done))
                    else:
                        bank_free = [db_hist[(sl - 1, 1)]] if sl >= 1 else [tg[1]]
                    tka = mm_group(
                        psel[:, 0:512],
                        [(lambda sl=sl: sdt_s[:, sl * 128:(sl + 1) * 128],
                          lambda nn=nn: dtspb[:, nn * 512:nn * 512 + 512])],
                        waits=[tspb] + bank_free)
                    da_toks.append(P.op(
                        "se", lambda o, nn=nn, st=st, par=par, psel=psel: o.activation(
                            da_sbs[par][:, nn * 512:nn * 512 + 512], psel[:, 0:512],
                            AF.Exp, scale=ascan[:, st:st + 1]),
                        waits=[tka, scan_hist.get(sl - 2)]))
                for nn in range(2):
                    psel = pb0 if nn == 0 else PS[1]
                    tkb = mm_group(
                        psel[:, 0:512],
                        [(lambda sl=sl: sdt_s[:, sl * 128:(sl + 1) * 128],
                          lambda nn=nn: dtxb[:, nn * 512:nn * 512 + 512])],
                        waits=[tdxb, da_toks[nn]])
                    db_hist[(sl, nn)] = P.op(
                        "ve", lambda o, nn=nn, psel=psel: o.tensor_mul(
                            dbx_sb[:, nn * 512:nn * 512 + 512], psel[:, 0:512],
                            brep[:, nn * 512:nn * 512 + 512]),
                        waits=[tkb] + br_done)
                    db_toks.append(db_hist[(sl, nn)])
                if tc == 0:
                    tscan = P.op("ve", lambda o, par=par: o.tensor_tensor_scan(
                        h_sb[:], da_sbs[par][:], dbx_sb[:], 0.0,
                        ALU.mult, ALU.add),
                        waits=da_toks + db_toks)
                else:
                    tscan = P.op("ve", lambda o, st=st, par=par: o.tensor_tensor_scan(
                        h_sb[:], da_sbs[par][:], dbx_sb[:],
                        carry[:, st:st + 1], ALU.mult, ALU.add),
                        waits=da_toks + db_toks)
                scan_hist[sl] = tscan
                tcar = P.op("ve", lambda o, st=st: o.tensor_copy(
                    carry[:, st:st + 1], h_sb[:, TCH - 1:TCH]), waits=[tscan])
                tm_ = P.op("ve", lambda o: o.tensor_mul(
                    m_sb[:], h_sb[:], cm_cb[:]),
                    waits=[tscan, ys_mm[0], ys_mm[1]] + tcm)
                for nn in range(2):
                    ys_mm[nn] = mm_group(
                        PS[2 + nn][:, 0:512],
                        [(lambda sl=sl: sns_s[:, sl * 128:(sl + 1) * 128],
                          lambda nn=nn: m_sb[:, nn * 512:nn * 512 + 512])],
                        waits=[tm_, ys_mm[nn]], start=False, stop=(sl == 15))
                if d_hh is not None and tc == 0 and cb == 0 and sl == 0:
                    P.dma("sy", lambda o: o.dma_start(out=dbg_aps["hh"][0], in_=da_sbs[0][:]),
                          waits=da_toks)
                    P.dma("sy", lambda o: o.dma_start(out=dbg_aps["hh"][1], in_=dbx_sb[:]),
                          waits=db_toks)
                    P.dma("sy", lambda o: o.dma_start(out=dbg_aps["hh"][2], in_=h_sb[:]),
                          waits=[tscan])

            if stop_after == "m2d":
                return _render(nc, P, top)
            yg_toks = []
            for nn in range(2):
                yg_toks.append(P.op("ve", lambda o, nn=nn, cb=cb: o.tensor_mul(
                    yg[:, cb * TCH + nn * 512:cb * TCH + nn * 512 + 512],
                    PS[2 + nn][:, 0:512], g_cb[:, nn * 512:nn * 512 + 512]),
                    waits=[ys_mm[nn]] + tg))
            if d_y is not None:
                P.dma("gp", lambda o, cb=cb, t0=t0: o.dma_start(
                    out=dbg_aps["y"][cb, :, t0:t0 + TCH],
                    in_=yg[:, cb * TCH:(cb + 1) * TCH]), waits=yg_toks + [PAR_TOK])
            P.barrier()

        if stop_after == "m2":
            return _render(nc, P, top)
        prev_store = None
        prev_ev3 = []
        for m in range(16):
            wb = wob[m % 2]
            tw = load_kpm("gp", wb, wout[:, m * 128:(m + 1) * 128], NK, 128,
                          [wob_busy[m % 2], PAR_TOK])
            ev = []
            mmt = None
            for nn in range(2):
                mmt = mm_group(
                    PS[nn][:, 0:512],
                    [(lambda k=k, wb=wb: wb[:, k * 128:(k + 1) * 128],
                      lambda k=k, nn=nn: yg[:, k * TCH + nn * 512:k * TCH + nn * 512 + 512])
                     for k in range(NCB)],
                    waits=[tw] + prev_ev3)
                ev.append(P.op("ve", lambda o, nn=nn: o.tensor_copy(
                    stage[:, nn * 512:nn * 512 + 512], PS[nn][:, 0:512]),
                    waits=[mmt, prev_store]))
            wob_busy[m % 2] = mmt
            prev_ev3 = list(ev)

            def store_partial(o, sem, m=m, tc=tc):
                with o.If_eq(par_reg, 0):
                    o.dma_start(out=ex4[tc, m * 128:(m + 1) * 128, :],
                                in_=stage[:]).then_inc(sem, 16)
                with o.Else():
                    o.dma_start(out=ex4[2 + tc, m * 128:(m + 1) * 128, :],
                                in_=stage[:]).then_inc(sem, 16)
            prev_store = P.dma_manual("gp", store_partial, waits=ev + [PAR_TOK])
            if d_op is not None:
                P.dma("sy", lambda o, m=m, t0=t0: o.dma_start(
                    out=dbg_aps["op"][m * 128:(m + 1) * 128, t0:t0 + TCH],
                    in_=stage[:]), waits=ev)
        gp_flag(tc, tc, waits=P.now())
        P.barrier()
    esm.close()

    if stop_after == "mamba":
        return _render(nc, P, top)

    esc = ExitStack()
    hbuf = sb("hbuf", [128, TH], F32, esc)
    abuf = sb("abuf", [128, TH], F32, esc)
    bbuf = sb("bbuf", [128, TH], F32, esc)
    xcbuf = sb("xcbuf", [128, TH], F32, esc)
    sqc = sb("sqc", [128, TH], BF16, esc)
    hn = sb("hn", [128, NK * TH], BF16, esc)
    rstd2_bc = sb("rstd2_bc", [128, TH], F32, esc)

    d_h = dbgt("h", [D, TH])
    d_hn = dbgt("hn", [D, TH])

    t_poll = gp_poll(((1, 0), 0), ((0, 1), 1), waits=P.now())

    prev_mm = {0: None, 1: None}
    hw_toks = []
    pt1 = pt2 = ptw = ptq = None
    for k in range(NK):
        tx = P.dma("sy", lambda o, k=k: o.dma_start(
            out=xcbuf[:], in_=x_h[k * 128:(k + 1) * 128, :]), waits=[t_poll, pt1])

        def ld_own(o, sem, k=k):
            with o.If_eq(par_reg, 0):
                o.dma_start(out=abuf[:], in_=ex4[0, k * 128:(k + 1) * 128, :]).then_inc(sem, 16)
            with o.Else():
                o.dma_start(out=abuf[:], in_=ex4[3, k * 128:(k + 1) * 128, :]).then_inc(sem, 16)

        def ld_partner(o, sem, k=k):
            with o.If_eq(par_reg, 0):
                o.dma_start(out=bbuf[:], in_=ex4[2, k * 128:(k + 1) * 128, :]).then_inc(sem, 16)
            with o.Else():
                o.dma_start(out=bbuf[:], in_=ex4[1, k * 128:(k + 1) * 128, :]).then_inc(sem, 16)
        ta = P.dma_manual("gp", ld_own, waits=[t_poll, pt1, PAR_TOK])
        tb = P.dma_manual("gp", ld_partner, waits=[t_poll, pt2, PAR_TOK])
        pt1 = P.op("ve", lambda o: o.tensor_add(hbuf[:], xcbuf[:], abuf[:]),
                   waits=[tx, ta, ptw, ptq])
        pt2 = P.op("ve", lambda o: o.tensor_add(hbuf[:], hbuf[:], bbuf[:]),
                   waits=[pt1, tb])
        ptw = P.dma("sy", lambda o, k=k: o.dma_start(
            out=h_scr[k * 128:(k + 1) * 128, :], in_=hbuf[:]), waits=[pt2])
        if d_h is not None:
            P.dma("sy", lambda o, k=k: o.dma_start(
                out=dbg_aps["h"][k * 128:(k + 1) * 128, :], in_=hbuf[:]), waits=[pt2])
        ptq = P.op("se", lambda o: o.activation(sqc[:], hbuf[:], AF.Square), waits=[pt2])
        for nn in range(2):
            prev_mm[nn] = mm_group(
                PROW[0:1, nn * 512:nn * 512 + 512],
                [(lambda: onesc_s[:], lambda nn=nn: sqc[:, nn * 512:nn * 512 + 512])],
                waits=[ptq, prev_mm[nn]], start=(k == 0), stop=(k == NK - 1))
        hw_toks.append(ptw)
    st2 = []
    for nn in range(2):
        tr1 = P.op("se", lambda o, nn=nn: o.activation(
            rowA[0:1, nn * 512:nn * 512 + 512], PROW[0:1, nn * 512:nn * 512 + 512],
            AF.Ln, bias=eps_col[0:1, 0:1], scale=1.0 / 2048.0), waits=[prev_mm[nn]])
        tr2 = P.op("se", lambda o, nn=nn: o.activation(
            rowA[0:1, nn * 512:nn * 512 + 512], rowA[0:1, nn * 512:nn * 512 + 512],
            AF.Exp, scale=-0.5), waits=[tr1])
        trb = P.op("ve", lambda o, nn=nn: o.tensor_copy(
            rowbf[0:1, nn * 512:nn * 512 + 512], rowA[0:1, nn * 512:nn * 512 + 512]),
            waits=[tr2])
        st2.append(bcast_f32(
            lambda nn=nn: rowbf[0:1, nn * 512:nn * 512 + 512],
            lambda nn=nn: rstd2_bc[:, nn * 512:nn * 512 + 512], [trb] + st2, PS[4]))
    hn_toks = []
    prev = None
    for k in range(NK):
        tr = P.dma("sy", lambda o, k=k: o.dma_start(
            out=hbuf[:], in_=h_scr[k * 128:(k + 1) * 128, :]),
            waits=st2 + [prev, hw_toks[k], ptq])
        prev = P.op("ve", lambda o, k=k: o.scalar_tensor_tensor(
            hn[:, k * TH:(k + 1) * TH], hbuf[:], w2_s[:, k:k + 1], rstd2_bc[:],
            ALU.mult, ALU.mult), waits=[tr] + st2)
        hn_toks.append(prev)
        if d_hn is not None:
            P.dma("gp", lambda o, k=k: o.dma_start(
                out=dbg_aps["hn"][k * 128:(k + 1) * 128, :],
                in_=hn[:, k * TH:(k + 1) * TH]), waits=[prev, PAR_TOK])
    P.barrier()
    if stop_after == "combine":
        return _render(nc, P, top)

    esl = ExitStack()
    wgb = [sb(f"wgb{i}", [128, 2048], BF16, esl) for i in range(2)]
    wub = [sb(f"wub{i}", [128, 2048], BF16, esl) for i in range(2)]
    wdb = [sb(f"wdb{i}", [128, 32 * 128], BF16, esl) for i in range(2)]
    act = sb("act", [128, 32 * TH], BF16, esl)
    sgb = sb("sgb", [128, TH], F32, esl)
    mstage = sb("mstage", [128, TH], F32, esl)
    mbuf = sb("mbuf", [128, TH], F32, esl)
    hbuf2 = sb("hbuf2", [128, TH], F32, esl)
    xbuf2 = sb("xbuf2", [128, TH], F32, esl)
    s1bf = sb("s1bf", [128, TH], BF16, esl)
    sq1 = sb("sq1", [128, TH], BF16, esl)

    d_s1 = dbgt("s1", [D, TH])

    mean_mm = {0: None, 1: None}
    ssq_mm = {0: None, 1: None}
    wg_busy = [None, None]
    wd_busy = [None, None]
    for ih in range(2):
        if stop_after == "mlpd" and ih == 1:
            return _render(nc, P, top)
        act_toks = []
        for ii in range(32):
            ig = ih * 32 + ii
            bsy = wg_busy[ii % 2]
            twg = load_kpm("gp", wgb[ii % 2], wg_d[:, ig * 128:(ig + 1) * 128],
                           NK, 128, [bsy, PAR_TOK] + hn_toks[-1:])
            twu = load_kpm("gp", wub[ii % 2], wu_d[:, ig * 128:(ig + 1) * 128],
                           NK, 128, [bsy, PAR_TOK])
            for nn in range(2):
                tgm = mm_group(
                    PS[0][:, 0:512],
                    [(lambda k=k, ii=ii: wgb[ii % 2][:, k * 128:(k + 1) * 128],
                      lambda k=k, nn=nn: hn[:, k * TH + nn * 512:k * TH + nn * 512 + 512])
                     for k in range(NK)],
                    waits=[twg] + hn_toks + act_toks[-1:])
                tum = mm_group(
                    PS[1][:, 0:512],
                    [(lambda k=k, ii=ii: wub[ii % 2][:, k * 128:(k + 1) * 128],
                      lambda k=k, nn=nn: hn[:, k * TH + nn * 512:k * TH + nn * 512 + 512])
                     for k in range(NK)],
                    waits=[twu] + hn_toks + act_toks[-1:])
                tsg = P.op("se", lambda o, nn=nn: o.activation(
                    sgb[:, nn * 512:nn * 512 + 512], PS[0][:, 0:512], AF.Silu),
                    waits=[tgm])
                tac = P.op("ve", lambda o, ii=ii, nn=nn: o.tensor_mul(
                    act[:, ii * TH + nn * 512:ii * TH + nn * 512 + 512],
                    PS[1][:, 0:512], sgb[:, nn * 512:nn * 512 + 512]),
                    waits=[tum, tsg])
                act_toks.append(tac)
            wg_busy[ii % 2] = act_toks[-1]
        if stop_after == "mlpa" and ih == 0:
            return _render(nc, P, top)
        if stop_after == "mlpc" and ih == 1:
            return _render(nc, P, top)
        import os as _os
        prev_evs = []
        prev_stg = None
        for m in range(int(_os.environ.get("N_MD", "16"))):
            twdn = load_kpm("gp", wdb[m % 2],
                            wd_d[ih * 4096:(ih + 1) * 4096, m * 128:(m + 1) * 128],
                            32, 128, [wd_busy[m % 2], PAR_TOK] + act_toks[-1:])
            evs = []
            mmt = None
            for nn in range(2):
                mmt = mm_group(
                    PS[2][:, 0:512],
                    [(lambda k=k, m=m: wdb[m % 2][:, k * 128:(k + 1) * 128],
                      lambda k=k, nn=nn: act[:, k * TH + nn * 512:k * TH + nn * 512 + 512])
                     for k in range(32)],
                    waits=[twdn] + act_toks + evs + prev_evs)
                if ih == 0:
                    evs.append(P.op("ve", lambda o, nn=nn: o.tensor_copy(
                        mstage[:, nn * 512:nn * 512 + 512], PS[2][:, 0:512]),
                        waits=[mmt, prev_stg]))
                else:
                    tmb = P.dma("sy", lambda o, m=m, nn=nn: o.dma_start(
                        out=mbuf[:, nn * 512:nn * 512 + 512],
                        in_=macc_scr[m * 128:(m + 1) * 128, nn * 512:nn * 512 + 512]),
                        waits=[mmt])
                    thb = P.dma("sy", lambda o, m=m, nn=nn: o.dma_start(
                        out=hbuf2[:, nn * 512:nn * 512 + 512],
                        in_=h_scr[m * 128:(m + 1) * 128, nn * 512:nn * 512 + 512]),
                        waits=[mmt])
                    txb2 = P.dma("sy", lambda o, m=m, nn=nn: o.dma_start(
                        out=xbuf2[:, nn * 512:nn * 512 + 512],
                        in_=x_h[m * 128:(m + 1) * 128, nn * 512:nn * 512 + 512]),
                        waits=[mmt])
                    e1 = P.op("ve", lambda o, nn=nn: o.tensor_add(
                        mstage[:, nn * 512:nn * 512 + 512], PS[2][:, 0:512],
                        mbuf[:, nn * 512:nn * 512 + 512]),
                        waits=[mmt, tmb, prev_stg])
                    e2 = P.op("ve", lambda o, nn=nn: o.tensor_add(
                        mstage[:, nn * 512:nn * 512 + 512],
                        mstage[:, nn * 512:nn * 512 + 512],
                        hbuf2[:, nn * 512:nn * 512 + 512]), waits=[e1, thb])
                    e3 = P.op("ve", lambda o, nn=nn: o.tensor_add(
                        mstage[:, nn * 512:nn * 512 + 512],
                        mstage[:, nn * 512:nn * 512 + 512],
                        xbuf2[:, nn * 512:nn * 512 + 512]), waits=[e2, txb2])
                    evs.append(e3)
            prev_evs = list(evs)
            if ih == 0:
                prev_stg = P.dma("sy", lambda o, m=m: o.dma_start(
                    out=macc_scr[m * 128:(m + 1) * 128, :], in_=mstage[:]), waits=evs)
                wd_busy[m % 2] = prev_stg
            else:
                tws = P.dma("sy", lambda o, m=m: o.dma_start(
                    out=s1_scr[m * 128:(m + 1) * 128, :], in_=mstage[:]), waits=evs)
                prev_stg = tws
                if d_s1 is not None:
                    P.dma("sy", lambda o, m=m: o.dma_start(
                        out=dbg_aps["s1"][m * 128:(m + 1) * 128, :], in_=mstage[:]),
                        waits=evs)
                tb1 = P.op("ve", lambda o: o.tensor_copy(s1bf[:], mstage[:]), waits=evs)
                tq1 = P.op("se", lambda o: o.activation(sq1[:], mstage[:], AF.Square),
                           waits=evs)
                for nn in range(2):
                    psm = PS[3] if nn == 0 else PS[4]
                    mean_mm[nn] = mm_group(
                        psm[0:1, 0:512],
                        [(lambda: onesc_s[:],
                          lambda nn=nn: s1bf[:, nn * 512:nn * 512 + 512])],
                        waits=[tb1, mean_mm[nn]], start=(m == 0), stop=(m == 15))
                    ssq_mm[nn] = mm_group(
                        PROW[0:1, nn * 512:nn * 512 + 512],
                        [(lambda: onesc_s[:],
                          lambda nn=nn: sq1[:, nn * 512:nn * 512 + 512])],
                        waits=[tq1, ssq_mm[nn]], start=(m == 0), stop=(m == 15))
                wd_busy[m % 2] = tws

    if stop_after == "mlpb":
        return _render(nc, P, top)
    r_toks = []
    for nn in range(2):
        psm = PS[3] if nn == 0 else PS[4]
        t1 = P.op("se", lambda o, nn=nn, psm=psm: o.activation(
            rowA[0:1, nn * 512:nn * 512 + 512], psm[0:1, 0:512], AF.Copy,
            scale=1.0 / 2048.0), waits=[mean_mm[nn]])
        t2 = P.op("se", lambda o, nn=nn: o.activation(
            rowB[0:1, nn * 512:nn * 512 + 512], rowA[0:1, nn * 512:nn * 512 + 512],
            AF.Square), waits=[t1])
        t3 = P.op("se", lambda o, nn=nn: o.activation(
            rowC[0:1, nn * 512:nn * 512 + 512], PROW[0:1, nn * 512:nn * 512 + 512],
            AF.Copy, scale=1.0 / 2048.0), waits=[ssq_mm[nn]])
        t4 = P.op("ve", lambda o, nn=nn: o.tensor_sub(
            rowC[0:1, nn * 512:nn * 512 + 512], rowC[0:1, nn * 512:nn * 512 + 512],
            rowB[0:1, nn * 512:nn * 512 + 512]), waits=[t2, t3])
        t5 = P.op("se", lambda o, nn=nn: o.activation(
            rowB[0:1, nn * 512:nn * 512 + 512], rowC[0:1, nn * 512:nn * 512 + 512],
            AF.Ln, bias=eps_col[0:1, 0:1]), waits=[t4])
        t6 = P.op("se", lambda o, nn=nn: o.activation(
            rowD[0:1, nn * 512:nn * 512 + 512], rowB[0:1, nn * 512:nn * 512 + 512],
            AF.Exp, scale=-0.5), waits=[t5])
        t7 = P.op("se", lambda o, nn=nn: o.activation(
            rowC[0:1, nn * 512:nn * 512 + 512], rowC[0:1, nn * 512:nn * 512 + 512],
            AF.Ln, scale=2048.0), waits=[t5, t6])
        t8 = P.op("ve", lambda o, nn=nn: o.tensor_sub(
            rowB[0:1, nn * 512:nn * 512 + 512], rowB[0:1, nn * 512:nn * 512 + 512],
            rowC[0:1, nn * 512:nn * 512 + 512]), waits=[t7])
        t9 = P.op("se", lambda o, nn=nn: o.activation(
            rowC[0:1, nn * 512:nn * 512 + 512], rowB[0:1, nn * 512:nn * 512 + 512],
            AF.Exp, scale=0.5), waits=[t8])
        tmb_ = P.op("ve", lambda o, nn=nn: o.tensor_copy(
            rowbf[0:1, nn * 512:nn * 512 + 512], rowA[0:1, nn * 512:nn * 512 + 512]),
            waits=[t2, t9])
        trb_ = P.op("ve", lambda o, nn=nn: o.tensor_copy(
            rowbf2[0:1, nn * 512:nn * 512 + 512], rowD[0:1, nn * 512:nn * 512 + 512]),
            waits=[t6])
        r_toks += [tmb_, trb_, t9]
    INV_TOK = r_toks[-1]
    P.barrier()
    esl.close()
    esc.close()
    if stop_after == "mlp":
        return _render(nc, P, top)

    esf = ExitStack()
    x1bf = sb("x1bf", [128, NK * TH], BF16, esf)
    mu1_bc = sb("mu1_bc", [128, TH], F32, esf)
    rstd1_bc = sb("rstd1_bc", [128, TH], F32, esf)
    x1stage = sb("x1stage", [128, TH], F32, esf)
    s1buf = sb("s1buf", [128, TH], F32, esf)
    d_x1 = dbgt("x1", [D, TH])
    bt = []
    for nn in range(2):
        bt.append(bcast_f32(lambda nn=nn: rowbf[0:1, nn * 512:nn * 512 + 512],
                            lambda nn=nn: mu1_bc[:, nn * 512:nn * 512 + 512],
                            r_toks + bt, PS[3]))
        bt.append(bcast_f32(lambda nn=nn: rowbf2[0:1, nn * 512:nn * 512 + 512],
                            lambda nn=nn: rstd1_bc[:, nn * 512:nn * 512 + 512],
                            r_toks + bt, PS[4]))
    x1_toks = []
    prev = None
    for k in range(NK):
        trd = P.dma("sy", lambda o, k=k: o.dma_start(
            out=s1buf[:], in_=s1_scr[k * 128:(k + 1) * 128, :]),
            waits=bt + [prev])
        tsb = P.op("ve", lambda o: o.tensor_sub(x1stage[:], s1buf[:], mu1_bc[:]),
                   waits=[trd, prev] + bt)
        tmu = P.op("ve", lambda o: o.tensor_mul(x1stage[:], x1stage[:], rstd1_bc[:]),
                   waits=[tsb])
        tbf = P.op("ve", lambda o, k=k: o.tensor_copy(
            x1bf[:, k * TH:(k + 1) * TH], x1stage[:]), waits=[tmu])
        tst = P.dma("sy", lambda o, k=k: o.dma_start(
            out=x1_scr[k * 128:(k + 1) * 128, :], in_=x1stage[:]), waits=[tmu])
        if d_x1 is not None:
            P.dma("sy", lambda o, k=k: o.dma_start(
                out=dbg_aps["x1"][k * 128:(k + 1) * 128, :], in_=x1stage[:]),
                waits=[tmu])
        prev = tst
        x1_toks.append(tbf)

    if stop_after == "ffa":
        return _render(nc, P, top)
    invn_cols = sb("invn_cols", [128, 8], F32, esf)
    simb = sb("simb", [128, 16 * NE], BF16, esf)
    sqe = sb("sqe", [128, NE], BF16, esf)
    sgate_bc = sb("sgate_bc", [128, NE], F32, esf)
    cnorm_bc = sb("cnorm_bc", [128, NE], F32, esf)
    wtm = sb("wtm", [128, 8 * NE], F32, esf)
    wfm = sb("wfm", [NE, TH], F32, esf)
    wfmb = sb("wfmb", [NE, TH], BF16, esf)
    rt1 = sb("rt1", [128, NE], F32, esf)
    rt2 = sb("rt2", [128, NE], F32, esf)
    rt3 = sb("rt3", [128, NE], F32, esf)
    rt4 = sb("rt4", [128, NE], F32, esf)
    rt5 = sb("rt5", [128, NE], F32, esf)
    rcol = sb("rcol", [128, 1], F32, esf)
    rcol2 = sb("rcol2", [128, 1], F32, esf)
    rcol3 = sb("rcol3", [128, 1], F32, esf)
    negb = sb("negb", [128, 1], F32, esf)
    posb = sb("posb", [128, 1], F32, esf)
    qwb = [sb("qwb0", [128, 2048], BF16, esf)] * 2
    kwb = [sb("kwb0", [128, 2048], BF16, esf)] * 2
    vwb = [sb("vwb0", [128, 2048], BF16, esf)] * 2
    wbc = sb("wbc", [128, TH], F32, esf)
    qacc = [sb(f"qacc{i}", [128, TH], F32, esf) for i in range(2)]
    kacc = [sb(f"kacc{i}", [128, TH], F32, esf) for i in range(2)]
    vacc = [sb(f"vacc{i}", [128, TH], F32, esf) for i in range(2)]
    tmpe = sb("tmpe", [128, TH], F32, esf)
    qbf = sb("qbf", [128, TH], BF16, esf)
    kownbf = sb("kownbf", [128, TH], BF16, esf)
    vownbf = sb("vownbf", [128, TH], BF16, esf)
    vstg = sb("vstg", [128, TH], BF16, esf)
    kfm = sb("kfm", [128, T], BF16, esf)
    vtm = sb("vtm", [128, T], BF16, esf)
    pbf = sb("pbf", [128, T], BF16, esf)
    ptr_sb = sb("ptr_sb", [128, 128], BF16, esf)
    aofm = sb("aofm", [128, TH], BF16, esf)
    maskb = sb("maskb", [128, 16 * 128], BF16, esf)
    zall = sb("zall", [128, NE * TH], BF16, esf)
    owb = [sb(f"owb{i}", [128, D], BF16, esf) for i in range(2)]
    s2stage = sb("s2stage", [128, TH], F32, esf)
    s2bf = sb("s2bf", [128, TH], BF16, esf)
    sq2 = sb("sq2", [128, TH], BF16, esf)
    s2tok = sb("s2tok", [128, D], F32, esf)
    vcol = sb("vcol", [128, 8], F32, esf)
    mu2c = sb("mu2c", [128, 8], F32, esf)
    rstd2c = sb("rstd2c", [128, 8], F32, esf)

    d_w = dbgt("w", [TH, NE])
    d_qkv = dbgt("qkv_o", [3, 128, TH])
    d_ao = dbgt("ao", [128, TH])
    d_s2 = dbgt("s2", [D, TH])

    inv_toks = []
    for qc in range(8):
        tok = P.op("te", lambda o, qc=qc: o.transpose(
            PS[0][0:128, 0:1], rowC[0:1, qc * 128:(qc + 1) * 128], identf_s[0:1, 0:1]),
            waits=[INV_TOK] + inv_toks[-1:])
        inv_toks.append(P.op("se", lambda o, qc=qc: o.copy(
            invn_cols[:, qc:qc + 1], PS[0][0:128, 0:1]), waits=[tok]))

    tsim = P.dma2("gp", [
        lambda o: o.dma_start(out=simb[:, 0:8 * NE].rearrange("p (k e) -> p k e", k=8),
                              in_=sim_d[0:1024, :].rearrange("(k p) e -> p k e", p=128)),
        lambda o: o.dma_start(out=simb[:, 8 * NE:16 * NE].rearrange("p (k e) -> p k e", k=8),
                              in_=sim_d[1024:2048, :].rearrange("(k p) e -> p k e", p=128)),
    ], waits=[PAR_TOK])
    prev2 = None
    for k in range(NK):
        tq = P.op("se", lambda o, k=k: o.activation(
            sqe[:], simb[:, k * NE:(k + 1) * NE], AF.Square), waits=[tsim, prev2])
        prev2 = mm_group(PROW[0:1, 0:NE],
                         [(lambda: onesc_s[:], lambda: sqe[:])],
                         waits=[tq, prev2], start=(k == 0), stop=(k == NK - 1))
    tn1 = P.op("se", lambda o: o.activation(rowA[0:1, 0:NE], PROW[0:1, 0:NE], AF.Ln),
               waits=[prev2])
    tn2 = P.op("se", lambda o: o.activation(rowA[0:1, 0:NE], rowA[0:1, 0:NE],
                                            AF.Exp, scale=-0.5), waits=[tn1])
    tn3 = P.op("ve", lambda o: o.tensor_copy(rowbf[0:1, 0:NE], rowA[0:1, 0:NE]),
               waits=[tn2] + bt)
    tn5 = bcast_f32(lambda: rowbf[0:1, 0:NE], lambda: cnorm_bc[:], [tn3], PS[0], width=NE)
    tgd = P.dma("sy", lambda o: o.dma_start(out=rowB[0:1, 0:NE], in_=gates_d),
                waits=inv_toks)
    tg1 = P.op("se", lambda o: o.activation(rowB[0:1, 0:NE], rowB[0:1, 0:NE],
                                            AF.Sigmoid), waits=[tgd])
    tg2 = P.op("ve", lambda o: o.tensor_copy(rowbf2[0:1, 0:NE], rowB[0:1, 0:NE]),
               waits=[tg1] + bt)
    tg4 = bcast_f32(lambda: rowbf2[0:1, 0:NE], lambda: sgate_bc[:], [tg2], PS[1], width=NE)
    tnb = P.op("ve", lambda o: o.memset(negb[:], NEG))
    tpb = P.op("ve", lambda o: o.memset(posb[:], -NEG))

    w_toks = []
    for tcn in range(8):
        lmm = mm_group(
            PS[2][:, 0:NE],
            [(lambda k=k, tcn=tcn: x1bf[:, k * TH + tcn * 128:k * TH + tcn * 128 + 128],
              lambda k=k: simb[:, k * NE:(k + 1) * NE]) for k in range(NK)],
            waits=[tsim, tn5, tg4] + x1_toks + w_toks[-1:])
        tl1 = P.op("ve", lambda o, tcn=tcn: o.scalar_tensor_tensor(
            rt1[:], PS[2][:, 0:NE], invn_cols[:, tcn:tcn + 1], cnorm_bc[:],
            ALU.mult, ALU.mult), waits=[lmm, tn5] + inv_toks)
        tl2 = P.op("ve", lambda o: o.tensor_sub(rt1[:], rt1[:], sgate_bc[:]),
                   waits=[tl1, tg4])
        tmk = P.op("ve", lambda o: o.tensor_single_scalar(rt3[:], rt1[:], 0.0, ALU.is_gt),
                   waits=[tl2])
        tnp = P.op("ve", lambda o: o.reduce_sum(rcol[:], rt3[:], axis=AX.X), waits=[tmk])
        tmx = P.op("ve", lambda o: o.reduce_max(rcol2[:], rt1[:], axis=AX.X), waits=[tnp])
        teq = P.op("ve", lambda o: o.tensor_tensor(
            rt2[:], rt1[:], rcol2[:].broadcast_to([128, NE]), ALU.is_equal), waits=[tmx])
        tl2b = P.op("ve", lambda o: o.scalar_tensor_tensor(
            rt4[:], rt2[:], negb[:], rt1[:], ALU.mult, ALU.add), waits=[teq, tnb])
        tmx2 = P.op("ve", lambda o: o.reduce_max(rcol2[:], rt4[:], axis=AX.X),
                    waits=[tl2b])
        teq2 = P.op("ve", lambda o: o.tensor_tensor(
            rt5[:], rt4[:], rcol2[:].broadcast_to([128, NE]), ALU.is_equal),
            waits=[tmx2])
        tfb = P.op("ve", lambda o: o.tensor_add(rt2[:], rt2[:], rt5[:]), waits=[teq2])
        tin = P.op("ve", lambda o: o.tensor_single_scalar(rcol3[:], rcol[:], 0.5, ALU.is_lt),
                   waits=[tfb])
        tsc = P.op("ve", lambda o: o.tensor_scalar_mul(rt4[:], rt2[:], rcol3[:]),
                   waits=[tin])
        tmf = P.op("ve", lambda o: o.tensor_max(rt2[:], rt4[:], rt3[:]), waits=[tsc])
        tga = P.op("ve", lambda o: o.tensor_scalar_max(rt5[:], rt1[:], 0.0), waits=[tmf])
        tml = P.op("ve", lambda o: o.tensor_mul(rt5[:], rt5[:], rt2[:])
                   if False else o.tensor_mul(rt3[:], rt5[:], rt2[:]), waits=[tga])
        tm1 = P.op("ve", lambda o: o.tensor_single_scalar(
            rt5[:], rt2[:], 1.0, ALU.subtract), waits=[tml])
        tm2 = P.op("ve", lambda o: o.scalar_tensor_tensor(
            rt4[:], rt5[:], posb[:], rt3[:], ALU.mult, ALU.add), waits=[tm1, tpb])
        tsm = P.op("ve", lambda o: o.reduce_max(rcol2[:], rt4[:], axis=AX.X), waits=[tm2])
        tng = P.op("ve", lambda o: o.tensor_scalar_mul(rcol[:], rcol2[:], -1.0),
                   waits=[tsm])
        tex = P.op("se", lambda o: o.activation(rt5[:], rt4[:], AF.Exp, bias=rcol[:]),
                   waits=[tng])
        tss = P.op("ve", lambda o: o.reduce_sum(rcol2[:], rt5[:], axis=AX.X), waits=[tex])
        trc = P.op("ve", lambda o: o.reciprocal(rcol3[:], rcol2[:]), waits=[tss])
        tw_ = P.op("ve", lambda o, tcn=tcn: o.scalar_tensor_tensor(
            wtm[:, tcn * NE:(tcn + 1) * NE], rt5[:], rcol3[:], rt2[:],
            ALU.mult, ALU.mult), waits=[trc])
        w_toks.append(tw_)
        if d_w is not None:
            P.dma("sy", lambda o, tcn=tcn: o.dma_start(
                out=dbg_aps["w"][tcn * 128:(tcn + 1) * 128, :],
                in_=wtm[:, tcn * NE:(tcn + 1) * NE]), waits=[tw_])
    if stop_after == "ffb":
        return _render(nc, P, top)
    wfm_toks = []
    for tcn in range(8):
        tok = P.op("te", lambda o, tcn=tcn: o.transpose(
            PS[3][0:NE, 0:128], wtm[:, tcn * NE:(tcn + 1) * NE], identf_s[:]),
            waits=[w_toks[tcn]] + wfm_toks[-1:])
        wfm_toks.append(P.op("se", lambda o, tcn=tcn: o.copy(
            wfm[0:NE, tcn * 128:(tcn + 1) * 128], PS[3][0:NE, 0:128]), waits=[tok]))
    twfb = P.op("ve", lambda o: o.tensor_copy(wfmb[0:NE, :], wfm[0:NE, :]),
                waits=wfm_toks)

    if stop_after == "ffc":
        return _render(nc, P, top)
    qw_busy = [None, None]
    cur = {"q": None, "k": None, "v": None}
    prev_mm3 = {"q": None, "k": None, "v": None}
    import os as _os2
    for e in range(int(_os2.environ.get("N_E", str(NE)))):
        tqw = P.dma2("gp", [
            lambda o, e=e: o.dma_start(
                out=qwb[e % 2][:, 0:1024].rearrange("p (k h) -> p k h", k=8),
                in_=qkv_d[0, e, 0:1024, :].rearrange("(k p) h -> p k h", p=128)),
            lambda o, e=e: o.dma_start(
                out=qwb[e % 2][:, 1024:2048].rearrange("p (k h) -> p k h", k=8),
                in_=qkv_d[0, e, 1024:2048, :].rearrange("(k p) h -> p k h", p=128)),
        ], waits=[qw_busy[e % 2], PAR_TOK])
        tkw = P.dma2("gp", [
            lambda o, e=e: o.dma_start(
                out=kwb[e % 2][:, 0:1024].rearrange("p (k h) -> p k h", k=8),
                in_=qkv_d[1, e, 0:1024, :].rearrange("(k p) h -> p k h", p=128)),
            lambda o, e=e: o.dma_start(
                out=kwb[e % 2][:, 1024:2048].rearrange("p (k h) -> p k h", k=8),
                in_=qkv_d[1, e, 1024:2048, :].rearrange("(k p) h -> p k h", p=128)),
        ], waits=[qw_busy[e % 2], PAR_TOK])
        tvw = P.dma2("gp", [
            lambda o, e=e: o.dma_start(
                out=vwb[e % 2][:, 0:1024].rearrange("p (k h) -> p k h", k=8),
                in_=qkv_d[2, e, 0:1024, :].rearrange("(k p) h -> p k h", p=128)),
            lambda o, e=e: o.dma_start(
                out=vwb[e % 2][:, 1024:2048].rearrange("p (k h) -> p k h", k=8),
                in_=qkv_d[2, e, 1024:2048, :].rearrange("(k p) h -> p k h", p=128)),
        ], waits=[qw_busy[e % 2], PAR_TOK])
        tb1 = mm_group(PS[4][:, 0:512],
                       [(lambda e=e: sel16_s[:, e * 128:(e + 1) * 128],
                         lambda: wfmb[0:NE, 0:512])], waits=[twfb, cur["v"]])
        tb2 = P.op("se", lambda o: o.copy(wbc[:, 0:512], PS[4][:, 0:512]), waits=[tb1])
        tb3 = mm_group(PS[4][:, 0:512],
                       [(lambda e=e: sel16_s[:, e * 128:(e + 1) * 128],
                         lambda: wfmb[0:NE, 512:1024])], waits=[tb2])
        tb4 = P.op("se", lambda o: o.copy(wbc[:, 512:1024], PS[4][:, 0:512]), waits=[tb3])
        wbc_tok = [tb2, tb4]
        last_mm = None
        for name, wgt, tload, psix in (("q", qwb, tqw, 0), ("k", kwb, tkw, 1),
                                       ("v", vwb, tvw, 2)):
            src = {"q": qacc, "k": kacc, "v": vacc}[name]
            prev_add = None
            for nn in range(2):
                mmt = mm_group(
                    PS[psix][:, 0:512],
                    [(lambda k=k, e=e, wgt=wgt: wgt[e % 2][:, k * 128:(k + 1) * 128],
                      lambda k=k, nn=nn: x1bf[:, k * TH + nn * 512:k * TH + nn * 512 + 512])
                     for k in range(NK)],
                    waits=[tload, prev_mm3[name], twfb] + x1_toks
                          + ([prev_add] if prev_add else []))
                tmm = P.op("ve", lambda o, nn=nn, psix=psix: o.tensor_mul(
                    tmpe[:, nn * 512:nn * 512 + 512], PS[psix][:, 0:512],
                    wbc[:, nn * 512:nn * 512 + 512]), waits=[mmt] + wbc_tok)
                if e == 0:
                    tad = P.op("ve", lambda o, nn=nn, src=src: o.tensor_copy(
                        src[0][:, nn * 512:nn * 512 + 512],
                        tmpe[:, nn * 512:nn * 512 + 512]), waits=[tmm])
                else:
                    tad = P.op("ve", lambda o, nn=nn, src=src, e=e: o.tensor_add(
                        src[e % 2][:, nn * 512:nn * 512 + 512],
                        src[(e + 1) % 2][:, nn * 512:nn * 512 + 512],
                        tmpe[:, nn * 512:nn * 512 + 512]), waits=[tmm, cur[name]])
                prev_add = tad
            cur[name] = prev_add
            prev_mm3[name] = prev_add
            last_mm = prev_add
        qw_busy[0] = qw_busy[1] = last_mm
    last = (NE - 1) % 2
    tqb = P.op("ve", lambda o: o.tensor_copy(qbf[:], qacc[last][:]), waits=[cur["q"]])
    tkb = P.op("ve", lambda o: o.tensor_copy(kownbf[:], kacc[last][:]), waits=[cur["k"]])
    tvb = P.op("ve", lambda o: o.tensor_copy(vownbf[:], vacc[last][:]), waits=[cur["v"]])
    if d_qkv is not None:
        P.dma("sy", lambda o: o.dma_start(out=dbg_aps["qkv_o"][0], in_=qacc[last][:]),
              waits=[cur["q"]])
        P.dma("sy", lambda o: o.dma_start(out=dbg_aps["qkv_o"][1], in_=kacc[last][:]),
              waits=[cur["k"]])
        P.dma("sy", lambda o: o.dma_start(out=dbg_aps["qkv_o"][2], in_=vacc[last][:]),
              waits=[cur["v"]])

    if stop_after == "ffd":
        return _render(nc, P, top)
    vt_toks = []
    for tcn in range(8):
        tok = P.op("te", lambda o, tcn=tcn: o.transpose(
            PTB[:, 0:128], vownbf[:, tcn * 128:(tcn + 1) * 128], identb_s[:]),
            waits=[tvb] + vt_toks[-1:])
        vt_toks.append(P.op("ve", lambda o, tcn=tcn: o.tensor_copy(
            vstg[:, tcn * 128:(tcn + 1) * 128], PTB[:, 0:128]), waits=[tok]))

    if stop_after == "ffe":
        return _render(nc, P, top)
    def kv_store_k(o, sem):
        with o.If_eq(par_reg, 0):
            o.dma_start(out=kvx[0, 0], in_=kownbf[:]).then_inc(sem, 16)
        with o.Else():
            o.dma_start(out=kvx[1, 0], in_=kownbf[:]).then_inc(sem, 16)

    def kv_store_v(o, sem):
        with o.If_eq(par_reg, 0):
            o.dma_start(out=kvx[0, 1], in_=vstg[:]).then_inc(sem, 16)
        with o.Else():
            o.dma_start(out=kvx[1, 1], in_=vstg[:]).then_inc(sem, 16)
    tks = P.dma_manual("gp", kv_store_k, waits=[tkb, PAR_TOK])
    tvs = P.dma_manual("gp", kv_store_v, waits=vt_toks + [PAR_TOK])
    if stop_after == "kva":
        return _render(nc, P, top)
    gp_flag(2, 2, waits=[tks, tvs])
    if stop_after == "kvb":
        return _render(nc, P, top)
    t_pollkv = gp_poll(((1, 2), 2), ((0, 2), 2), waits=P.now())
    if stop_after == "kvc":
        return _render(nc, P, top)
    kv_toks = [
        P.dma("gp", lambda o: o.dma_start(out=kfm[:, 0:TH], in_=kvx[0, 0]),
              waits=[t_pollkv]),
        P.dma("gp", lambda o: o.dma_start(out=kfm[:, TH:T], in_=kvx[1, 0]),
              waits=[t_pollkv]),
        P.dma("gp", lambda o: o.dma_start(out=vtm[:, 0:TH], in_=kvx[0, 1]),
              waits=[t_pollkv]),
        P.dma("gp", lambda o: o.dma_start(out=vtm[:, TH:T], in_=kvx[1, 1]),
              waits=[t_pollkv]),
    ]

    if stop_after == "mix":
        return _render(nc, P, top)

    ps_stack.close()
    esa = ExitStack()
    PS2 = [pst(f"ps2_{i}", [128, 512], F32, esa) for i in range(3)]
    psA = pst("psA", [128, 2048], F32, esa)
    PTB2 = pst("ptb2", [128, 512], BF16, esa)

    tmask_busy = None
    ao_toks = []
    for qc in range(8):
        nkk = 9 + qc
        tml_ = P.dma2("gp", [
            lambda o, qc=qc: o.dma_start(
                out=maskb[:, 0:8 * 128].rearrange("p (c q) -> p c q", c=8),
                in_=masks_d[qc, 0:8].rearrange("c p q -> p c q")),
            lambda o, qc=qc: o.dma_start(
                out=maskb[:, 8 * 128:16 * 128].rearrange("p (c q) -> p c q", c=8),
                in_=masks_d[qc, 8:16].rearrange("c p q -> p c q")),
        ], waits=[tmask_busy, PAR_TOK])
        sc_toks = []
        for kkc in range(nkk):
            sc_toks.append(mm_group(
                psA[:, kkc * 128:(kkc + 1) * 128],
                [(lambda qc=qc: qbf[:, qc * 128:(qc + 1) * 128],
                  lambda kkc=kkc: kfm[:, kkc * 128:(kkc + 1) * 128]),
                 (lambda: identb_s[:],
                  lambda kkc=kkc: maskb[:, kkc * 128:(kkc + 1) * 128])],
                waits=[tml_, tqb] + kv_toks + ao_toks[-1:]))
        tmask_busy = sc_toks[-1]
        tmx_ = P.op("ve", lambda o, nkk=nkk: o.reduce_max(
            rcol[:], psA[:, 0:nkk * 128], axis=AX.X), waits=sc_toks)
        tng_ = P.op("ve", lambda o: o.tensor_scalar_mul(rcol[:], rcol[:], -ATT_SCALE),
                    waits=[tmx_])
        tex_ = P.op("se", lambda o, nkk=nkk: o.activation(
            pbf[:, 0:nkk * 128], psA[:, 0:nkk * 128], AF.Exp,
            bias=rcol[:], scale=ATT_SCALE), waits=[tng_])
        tsm_ = P.op("ve", lambda o, nkk=nkk: o.reduce_sum(
            rcol2[:], pbf[:, 0:nkk * 128], axis=AX.X), waits=[tex_])
        trc_ = P.op("ve", lambda o: o.reciprocal(rcol[:], rcol2[:]), waits=[tsm_])
        tp2_ = P.op("ve", lambda o, nkk=nkk: o.tensor_scalar_mul(
            pbf[:, 0:nkk * 128], pbf[:, 0:nkk * 128], rcol[:]), waits=[trc_])
        avt = None
        for kkc in range(nkk):
            ttr = P.op("te", lambda o, kkc=kkc: o.transpose(
                PTB2[:, 0:128], pbf[:, kkc * 128:(kkc + 1) * 128], identb_s[:]),
                waits=[tp2_] + ([avt] if avt else []))
            tpc = P.op("ve", lambda o: o.tensor_copy(ptr_sb[:], PTB2[:, 0:128]),
                       waits=[ttr])
            avt = mm_group(
                PS2[0][:, 0:128],
                [(lambda kkc=kkc: vtm[:, kkc * 128:(kkc + 1) * 128],
                  lambda: ptr_sb[:])],
                waits=[tpc] + kv_toks, start=(kkc == 0), stop=(kkc == nkk - 1))
        ao_toks.append(P.op("ve", lambda o, qc=qc: o.tensor_copy(
            aofm[:, qc * 128:(qc + 1) * 128], PS2[0][:, 0:128]), waits=[avt]))
    if d_ao is not None:
        P.dma("gp", lambda o: o.dma_start(out=dbg_aps["ao"], in_=aofm[:]),
              waits=ao_toks + [PAR_TOK])

    z_toks = []
    prev3 = None
    for e in range(NE):
        tb1 = mm_group(PS2[1][:, 0:512],
                       [(lambda e=e: sel16_s[:, e * 128:(e + 1) * 128],
                         lambda: wfmb[0:NE, 0:512])], waits=[twfb, prev3])
        tz1 = P.op("ve", lambda o, e=e: o.tensor_mul(
            zall[:, e * TH:e * TH + 512], aofm[:, 0:512], PS2[1][:, 0:512]),
            waits=[tb1] + ao_toks)
        tb2 = mm_group(PS2[1][:, 0:512],
                       [(lambda e=e: sel16_s[:, e * 128:(e + 1) * 128],
                         lambda: wfmb[0:NE, 512:1024])], waits=[tz1])
        tz2 = P.op("ve", lambda o, e=e: o.tensor_mul(
            zall[:, e * TH + 512:(e + 1) * TH], aofm[:, 512:1024], PS2[1][:, 0:512]),
            waits=[tb2] + ao_toks)
        prev3 = tz2
        z_toks += [tz1, tz2]

    esa.close()
    esb = ExitStack()
    PS3 = [pst(f"ps3_{i}", [128, 512], F32, esb) for i in range(5)]
    PROW2 = pst("prow2", [1, 1024], F32, esb)
    PTF = pst("ptf", [128, 512], F32, esb)

    ow_busy = [None, None]
    mean2_mm = {0: None, 1: None}
    ssq2_mm = {0: None, 1: None}
    s2_toks = []
    prev_ev = []
    for m in range(16):
        tow = P.dma2("gp", [
            lambda o, m=m: o.dma_start(
                out=owb[m % 2][:, 0:1024].rearrange("p (e c) -> p e c", e=8),
                in_=oproj_d[0:8, :, m * 128:(m + 1) * 128].rearrange("e p c -> p e c")),
            lambda o, m=m: o.dma_start(
                out=owb[m % 2][:, 1024:2048].rearrange("p (e c) -> p e c", e=8),
                in_=oproj_d[8:16, :, m * 128:(m + 1) * 128].rearrange("e p c -> p e c")),
        ], waits=[ow_busy[m % 2], PAR_TOK] + z_toks[-1:])
        evs = []
        mmt = None
        for nn in range(2):
            mmt = mm_group(
                PS3[0 + nn][:, 0:512],
                [(lambda e2=e2, m=m: owb[m % 2][:, e2 * 128:(e2 + 1) * 128],
                  lambda e2=e2, nn=nn: zall[:, e2 * TH + nn * 512:e2 * TH + nn * 512 + 512])
                 for e2 in range(NE)],
                waits=[tow] + z_toks + prev_ev)
            tx1r = P.dma("sy", lambda o, m=m, nn=nn: o.dma_start(
                out=x1stage[:, nn * 512:nn * 512 + 512],
                in_=x1_scr[m * 128:(m + 1) * 128, nn * 512:nn * 512 + 512]),
                waits=[mmt] + prev_ev)
            evs.append(P.op("ve", lambda o, nn=nn: o.tensor_add(
                s2stage[:, nn * 512:nn * 512 + 512], PS3[nn][:, 0:512],
                x1stage[:, nn * 512:nn * 512 + 512]), waits=[mmt, tx1r] + prev_ev))
        ow_busy[m % 2] = mmt
        tws2 = P.dma("sy", lambda o, m=m: o.dma_start(
            out=s2_scr[m * 128:(m + 1) * 128, :], in_=s2stage[:]), waits=evs)
        s2_toks.append(tws2)
        if d_s2 is not None:
            P.dma("sy", lambda o, m=m: o.dma_start(
                out=dbg_aps["s2"][m * 128:(m + 1) * 128, :], in_=s2stage[:]), waits=evs)
        tbf2 = P.op("ve", lambda o: o.tensor_copy(s2bf[:], s2stage[:]), waits=evs)
        tq2 = P.op("se", lambda o: o.activation(sq2[:], s2stage[:], AF.Square), waits=evs)
        for nn in range(2):
            psm = PS3[2] if nn == 0 else PS3[3]
            mean2_mm[nn] = mm_group(
                psm[0:1, 0:512],
                [(lambda: onesc_s[:], lambda nn=nn: s2bf[:, nn * 512:nn * 512 + 512])],
                waits=[tbf2, mean2_mm[nn]], start=(m == 0), stop=(m == 15))
            ssq2_mm[nn] = mm_group(
                PROW2[0:1, nn * 512:nn * 512 + 512],
                [(lambda: onesc_s[:], lambda nn=nn: sq2[:, nn * 512:nn * 512 + 512])],
                waits=[tq2, ssq2_mm[nn]], start=(m == 0), stop=(m == 15))
        prev_ev = [tws2, tbf2, tq2]

    fr = []
    for nn in range(2):
        psm = PS3[2] if nn == 0 else PS3[3]
        t1 = P.op("se", lambda o, nn=nn, psm=psm: o.activation(
            rowA[0:1, nn * 512:nn * 512 + 512], psm[0:1, 0:512], AF.Copy,
            scale=1.0 / 2048.0), waits=[mean2_mm[nn]])
        t2 = P.op("se", lambda o, nn=nn: o.activation(
            rowB[0:1, nn * 512:nn * 512 + 512], rowA[0:1, nn * 512:nn * 512 + 512],
            AF.Square), waits=[t1])
        t3 = P.op("se", lambda o, nn=nn: o.activation(
            rowC[0:1, nn * 512:nn * 512 + 512], PROW2[0:1, nn * 512:nn * 512 + 512],
            AF.Copy, scale=1.0 / 2048.0), waits=[ssq2_mm[nn]])
        t4 = P.op("ve", lambda o, nn=nn: o.tensor_sub(
            rowC[0:1, nn * 512:nn * 512 + 512], rowC[0:1, nn * 512:nn * 512 + 512],
            rowB[0:1, nn * 512:nn * 512 + 512]), waits=[t2, t3])
        fr += [t1, t4]
    col_toks = []
    for tcn in range(8):
        tok = P.op("te", lambda o, tcn=tcn: o.transpose(
            PTF[0:128, 0:1], rowA[0:1, tcn * 128:(tcn + 1) * 128], identf_s[0:1, 0:1]),
            waits=fr + col_toks[-1:])
        tmu = P.op("se", lambda o, tcn=tcn: o.copy(mu2c[:, tcn:tcn + 1],
                                                   PTF[0:128, 0:1]), waits=[tok])
        tok2 = P.op("te", lambda o, tcn=tcn: o.transpose(
            PTF[0:128, 0:1], rowC[0:1, tcn * 128:(tcn + 1) * 128], identf_s[0:1, 0:1]),
            waits=[tmu])
        tvc = P.op("se", lambda o, tcn=tcn: o.copy(vcol[:, tcn:tcn + 1],
                                                   PTF[0:128, 0:1]), waits=[tok2])
        tl_ = P.op("se", lambda o, tcn=tcn: o.activation(
            rstd2c[:, tcn:tcn + 1], vcol[:, tcn:tcn + 1], AF.Ln, bias=eps_col[:]),
            waits=[tvc])
        col_toks.append(P.op("se", lambda o, tcn=tcn: o.activation(
            rstd2c[:, tcn:tcn + 1], rstd2c[:, tcn:tcn + 1], AF.Exp, scale=-0.5),
            waits=[tl_]))

    P.barrier()
    out_toks = []
    for tcn in range(8):
        tr_toks = []
        for k in range(NK):
            trd = P.dma("sy", lambda o, k=k, tcn=tcn: o.dma_start(
                out=s2stage[:, 0:128],
                in_=s2_scr[k * 128:(k + 1) * 128, tcn * 128:(tcn + 1) * 128]),
                waits=tr_toks[-1:] + out_toks[-1:])
            ttr = P.op("te", lambda o: o.transpose(
                PTF[:, 0:128], s2stage[:, 0:128], identf_s[:]),
                waits=[trd] + tr_toks[-1:])
            tr_toks.append(P.op("ve", lambda o, k=k: o.tensor_copy(
                s2tok[:, k * 128:(k + 1) * 128], PTF[:, 0:128]), waits=[ttr]))
        tno = P.op("ve", lambda o, tcn=tcn: o.tensor_scalar(
            s2tok[:], s2tok[:], mu2c[:, tcn:tcn + 1], rstd2c[:, tcn:tcn + 1],
            ALU.subtract, ALU.mult), waits=tr_toks + col_toks)
        out_toks.append(P.dma("sy", lambda o, tcn=tcn: o.dma_start(
            out=out_d[tcn * 128:(tcn + 1) * 128, :], in_=s2tok[:]), waits=[tno]))
    esb.close()
    esf.close()

    P.barrier()
    return _render(nc, P, top)


def _render(nc, P, top):
    P.barrier()
    P.check_deadlock()
    semd = {}
    semstack = ExitStack()
    for s in ("te", "ve", "se", "gp", "d_sy", "d_gp", "gpx", "pf1", "pf2"):
        semd[s] = semstack.enter_context(nc.semaphore(s))
    with nc.Block() as block:
        @block.tensor
        def _(o):
            for f in P.fns["te"]:
                f(o, semd)

        @block.vector
        def _(o):
            for f in P.fns["ve"]:
                f(o, semd)

        @block.scalar
        def _(o):
            for f in P.fns["se"]:
                f(o, semd)

        @block.gpsimd
        def _(o):
            for f in P.fns["gp"]:
                f(o, semd)

        @block.sync
        def _(o):
            for f in P.fns["sy"]:
                f(o, semd)
    semstack.close()
    return nc


def kernel(**inputs):
    ins = host_prepare(inputs)
    nc = build_program()
    res = run_bass_kernel_spmd(nc, ins, core_ids=list(range(8)))
    out = np.empty((B, T, D), np.float32)
    for core in range(8):
        b, par = core // 2, core % 2
        out[b, par * TH:(par + 1) * TH, :] = res.results[core]["out"]
    return out


# version 03 (final)
# speedup vs baseline: 1.1745x; 1.1659x over previous
import numpy as np
import ml_dtypes
from contextlib import ExitStack

import concourse.bass as bass
import concourse.mybir as mybir
from concourse.bass_utils import run_bass_kernel_spmd

F32 = mybir.dt.float32
BF16 = mybir.dt.bfloat16
I32 = mybir.dt.int32
AF = mybir.ActivationFunctionType
ALU = mybir.AluOpType
AX = mybir.AxisListType
BF = ml_dtypes.bfloat16

B, T, D = 4, 2048, 2048
DI, DXB, DTR = 4096, 1024, 128
INTER = 8192
NE, HD = 16, 128
HALF = DI // 2
NCB, CBW = 16, 128
TCH = 1024
NK = D // 128
TH = T // 2
EPS = 1e-5
NEG = -1e9
ATT_SCALE = float(1.0 / np.sqrt(128.0))

ENGINES = ("te", "ve", "se", "gp", "sy")
PROG_SEMS = ("te", "ve", "se", "gp", "d_sy", "d_gp", "gpx")


class Prog:

    def __init__(self):
        self.fns = {e: [] for e in ENGINES}
        self.cnt = {s: 0 for s in PROG_SEMS}
        self.meta = []

    @staticmethod
    def merge(waits):
        m = {}
        for w in waits:
            if w is None:
                continue
            s, v = w
            if v > m.get(s, 0):
                m[s] = v
        return tuple(sorted(m.items()))

    def _emit(self, eng, f, waits, inc):
        waits = Prog.merge(waits)
        self.meta.append((eng, waits, inc))

        def g(ob, semd):
            for s, v in waits:
                ob.wait_ge(semd[s], v)
            inst = f(ob, semd) if getattr(f, "wants_semd", False) else f(ob)
            if inc is not None:
                s, _ = inc
                amt = 16 if s.startswith("d_") else 1
                if inst is not None:
                    inst.then_inc(semd[s], amt)
                else:
                    ob.sem_inc(semd[s], amt)
        self.fns[eng].append(g)

    def op(self, eng, f, waits=()):
        assert eng in ("te", "ve", "se", "gp")
        self.cnt[eng] += 1
        tok = (eng, self.cnt[eng])
        self._emit(eng, f, waits, tok)
        return tok

    def raw(self, eng, f, waits=()):
        self._emit(eng, f, waits, None)

    CHAIN_DMA = True

    def dma(self, eng, f, waits=()):
        sem = {"sy": "d_sy", "gp": "d_gp"}[eng]
        if Prog.CHAIN_DMA:
            waits = list(waits) + [(sem, self.cnt[sem])]
        self.cnt[sem] += 16
        tok = (sem, self.cnt[sem])
        self._emit(eng, f, waits, tok)
        return tok

    def dma_manual(self, eng, f, waits=()):
        sem = {"sy": "d_sy", "gp": "d_gp"}[eng]
        if Prog.CHAIN_DMA:
            waits = list(waits) + [(sem, self.cnt[sem])]
        self.cnt[sem] += 16
        tok = (sem, self.cnt[sem])

        @wants_semd
        def g(ob, semd):
            f(ob, semd[sem])
            return None
        self._emit(eng, g, waits, None)
        self.meta[-1] = (eng, self.meta[-1][1], tok)
        return tok

    def dma2(self, eng, fs, waits=()):
        tok = None
        for f in fs:
            tok = self.dma(eng, f, waits)
        return tok

    def mark(self, eng, f=None, waits=()):
        self.cnt["gpx"] += 1
        tok = ("gpx", self.cnt["gpx"])
        self._emit(eng, f or (lambda ob: None), waits, tok)
        return tok

    def now(self):
        return [(s, self.cnt[s]) for s in PROG_SEMS]

    def barrier(self):
        toks = self.now()
        for e in ENGINES:
            self.raw(e, lambda ob: None, waits=toks)
        return toks

    def check_deadlock(self):
        streams = {e: [] for e in ENGINES}
        for eng, waits, inc in self.meta:
            streams[eng].append((waits, inc))
        ptr = {e: 0 for e in ENGINES}
        vals = {s: 0 for s in PROG_SEMS}
        progress = True
        while progress:
            progress = False
            for e in ENGINES:
                while ptr[e] < len(streams[e]):
                    waits, inc = streams[e][ptr[e]]
                    if all(vals[s] >= v for s, v in waits):
                        if inc is not None:
                            s, n = inc
                            vals[s] = max(vals[s], n)
                        ptr[e] += 1
                        progress = True
                    else:
                        break
        stuck = {e: ptr[e] for e in ENGINES if ptr[e] < len(streams[e])}
        if stuck:
            lines = []
            for e, p in stuck.items():
                waits, _ = streams[e][p]
                unmet = [(s, v, vals[s]) for s, v in waits if vals[s] < v]
                lines.append(f"  {e}: op {p}/{len(streams[e])} unmet={unmet}")
            raise RuntimeError("schedule deadlock:\n" + "\n".join(lines))


def wants_semd(f):
    f.wants_semd = True
    return f



def host_prepare(inputs):
    x = np.asarray(inputs["L0_attn_hidden_states"], np.float32)
    w_rms1 = np.asarray(inputs["L0_attn_rms1_w"], np.float32)
    Win = np.asarray(inputs["L0_attn_in_proj_w"], np.float32)
    Wdt = np.asarray(inputs["L0_attn_dt_proj_w"], np.float32)
    dtb = np.asarray(inputs["L0_attn_dt_proj_b"], np.float32)
    Alog = np.asarray(inputs["L0_attn_A_log"], np.float32)
    Dsk = np.asarray(inputs["L0_attn_D_skip"], np.float32)
    Wout = np.asarray(inputs["L0_attn_out_proj_w"], np.float32)
    w_rms2 = np.asarray(inputs["L0_attn_rms2_w"], np.float32)
    WgT = np.ascontiguousarray(np.asarray(inputs["L0_attn_gate_w"], np.float32).T)
    WuT = np.ascontiguousarray(np.asarray(inputs["L0_attn_up_w"], np.float32).T)
    WdT = np.ascontiguousarray(np.asarray(inputs["L0_attn_down_w"], np.float32).T)
    sim = np.asarray(inputs["L0_ffn_sim_matrix"], np.float32)
    gates = np.asarray(inputs["L0_ffn_gates"], np.float32)
    Q = np.asarray(inputs["L0_ffn_q_proj"], np.float32)
    K = np.asarray(inputs["L0_ffn_k_proj"], np.float32)
    V = np.asarray(inputs["L0_ffn_v_proj"], np.float32)
    O = np.asarray(inputs["L0_ffn_o_proj"], np.float32)

    WinT = np.ascontiguousarray(Win.T)
    WdtT = np.ascontiguousarray(Wdt.T)
    WoutT = np.ascontiguousarray(Wout.T)

    per_half = {}
    for half in (0, 1):
        zoff = half * HALF
        cols = []
        for cb in range(NCB):
            base = zoff + cb * CBW
            cols.extend(range(base, base + CBW))
            cols.extend(range(6144 + base, 6144 + base + CBW))
            xvrows = sorted({((base + c) // 64) * 16 + ((base + c) % 16) for c in range(CBW)})
            cols.extend(4096 + r for r in xvrows)
            cols.extend(5120 + r for r in xvrows)
        cols.extend(range(10240, 10240 + DTR))
        win_slice = np.ascontiguousarray(WinT[:, cols])

        Aperm = np.empty((128, 256), np.float32)
        cc_i = np.arange(128) // 16
        n_i = np.arange(128) % 16
        for st in range(256):
            cb, sl = st // 16, st % 16
            Aperm[:, st] = Alog[zoff + cb * CBW + cc_i * 16 + sl, n_i]
        per_half[half] = dict(
            win=win_slice,
            wout=np.ascontiguousarray(WoutT[zoff:zoff + HALF, :]),
            wdt=np.ascontiguousarray(WdtT[:, zoff:zoff + HALF]).astype(BF),
            aperm=Aperm,
            d_arr=Dsk[zoff:zoff + HALF].reshape(NCB, CBW).T.copy(),
            dtb_arr=dtb[zoff:zoff + HALF].reshape(NCB, CBW).T.copy(),
        )

    S_dt = np.zeros((128, 16 * 128), np.float32)
    S_ns = np.zeros((128, 16 * 128), np.float32)
    for sl in range(16):
        for p in range(128):
            S_dt[(p // 16) * 16 + sl, sl * 128 + p] = 1.0
            S_ns[p, sl * 128 + (p // 16) * 16 + sl] = 1.0
    S_xf = np.zeros((64, 128), np.float32)
    S_B = np.zeros((64, 128), np.float32)
    for c in range(128):
        S_xf[(c // 64) * 16 + (c % 16), c] = 1.0
        S_B[32 + ((c // 16) // 4) * 16 + (c % 16), c] = 1.0

    tri = np.triu(np.full((128, 128), NEG, np.float32), 1)
    fullm = np.full((128, 128), NEG, np.float32)
    masks = {}
    for par in (0, 1):
        m = np.zeros((8, 16, 128, 128), np.float32)
        for qc in range(8):
            dg = par * 8 + qc
            for kkc in range(16):
                if kkc == dg:
                    m[qc, kkc] = tri
                elif kkc > dg:
                    m[qc, kkc] = fullm
        masks[par] = np.ascontiguousarray(m.astype(BF))

    nonces = np.random.default_rng().integers(1, 2**30, size=16).astype(np.int32)
    sel16 = np.zeros((16, 16 * 128), np.float32)
    for e in range(16):
        sel16[e, e * 128:(e + 1) * 128] = 1.0
    w1_arr = w_rms1.reshape(NK, 128).T.copy()
    w2_arr = w_rms2.reshape(NK, 128).T.copy()
    qkv = np.ascontiguousarray(np.stack([Q, K, V], 0))

    ins = []
    for core in range(8):
        b, par = core // 2, core % 2
        meta = np.zeros((1, 32), np.int32)
        meta[0, 0] = par
        meta[0, 1:17] = nonces
        ph = per_half[par]
        xt = np.ascontiguousarray(x[b].T)
        ins.append({
            "x_t": xt,
            "x_h": np.ascontiguousarray(xt[:, par * TH:(par + 1) * TH]),
            "win": ph["win"], "wdt": ph["wdt"], "wout": ph["wout"],
            "aperm": ph["aperm"], "d_arr": ph["d_arr"], "dtb_arr": ph["dtb_arr"],
            "w1_arr": w1_arr, "w2_arr": w2_arr,
            "wg": WgT, "wu": WuT, "wd": WdT,
            "sim": sim, "gates": np.ascontiguousarray(gates.reshape(1, NE)),
            "qkv": qkv, "oproj": O,
            "sdt": S_dt.astype(BF), "sns": S_ns.astype(BF),
            "sxf": S_xf.astype(BF), "sbsel": S_B.astype(BF),
            "identb": np.eye(128, dtype=np.float32).astype(BF),
            "sel16": sel16.astype(BF),
            "identf": np.eye(128, dtype=np.float32),
            "ones_col": np.ones((128, 1), np.float32).astype(BF),
            "ones_row": np.ones((1, 128), np.float32).astype(BF),
            "masks": masks[par],
            "meta": meta,
        })
    return ins



def build_program(dbg=(), stop_after=None, n_cb=NCB, n_tc=2):
    nc = bass.Bass("TRN2", target_bir_lowering=False)
    P = Prog()
    top = ExitStack()

    def sb(name, shape, dt, stack=None):
        return (stack or top).enter_context(nc.sbuf_tensor(name, shape, dt))

    def pst(name, shape, dt=F32, stack=None):
        return (stack or top).enter_context(nc.psum_tensor(name, shape, dt))

    def din(name, shape, dt):
        return nc.dram_tensor(name, shape, dt, kind="ExternalInput").ap()

    x_t = din("x_t", [D, T], F32)
    x_h = din("x_h", [D, TH], F32)
    win = din("win", [D, NCB * 320 + 128], F32)
    wdt = din("wdt", [128, HALF], BF16)
    wout = din("wout", [HALF, D], F32)
    aperm = din("aperm", [128, 256], F32)
    d_arr_d = din("d_arr", [128, 16], F32)
    dtb_d = din("dtb_arr", [128, 16], F32)
    w1_d = din("w1_arr", [128, 16], F32)
    w2_d = din("w2_arr", [128, 16], F32)
    wg_d = din("wg", [D, INTER], F32)
    wu_d = din("wu", [D, INTER], F32)
    wd_d = din("wd", [INTER, D], F32)
    sim_d = din("sim", [D, NE], F32)
    gates_d = din("gates", [1, NE], F32)
    qkv_d = din("qkv", [3, NE, D, HD], F32)
    oproj_d = din("oproj", [NE, HD, D], F32)
    sdt_d = din("sdt", [128, 2048], BF16)
    sns_d = din("sns", [128, 2048], BF16)
    sxf_d = din("sxf", [64, 128], BF16)
    sbsel_d = din("sbsel", [64, 128], BF16)
    identb_d = din("identb", [128, 128], BF16)
    sel16_d = din("sel16", [16, 16 * 128], BF16)
    identf_d = din("identf", [128, 128], F32)
    onesc_d = din("ones_col", [128, 1], BF16)
    onesr_d = din("ones_row", [1, 128], BF16)
    masks_d = din("masks", [8, 16, 128, 128], BF16)
    meta_d = din("meta", [1, 32], I32)

    out_d = nc.dram_tensor("out", [TH, D], F32, kind="ExternalOutput").ap()

    dbg_aps = {}
    def dbgt(name, shape):
        if name in dbg and name not in dbg_aps:
            dbg_aps[name] = nc.dram_tensor("dbg_" + name, shape, F32,
                                           kind="ExternalOutput").ap()
        return dbg_aps.get(name)

    h_scr = nc.dram_tensor("h_scr", [D, TH], F32).ap()
    s1_scr = nc.dram_tensor("s1_scr", [D, TH], F32).ap()
    x1_scr = nc.dram_tensor("x1_scr", [D, TH], F32).ap()
    s2_scr = nc.dram_tensor("s2_scr", [D, TH], F32).ap()
    macc_scr = nc.dram_tensor("macc_scr", [D, TH], F32).ap()
    ex4 = nc.dram_tensor("ex4", [4, D, TCH], F32, addr_space="Shared").ap()
    kvx = nc.dram_tensor("kvx", [2, 2, 128, TH], BF16, addr_space="Shared").ap()
    flg = nc.dram_tensor("flg", [2, 8, 16], I32, addr_space="Shared").ap()

    smeta = sb("smeta", [1, 32], I32)
    spoll = sb("spoll", [1, 16], I32)
    identb_s = sb("identb_s", [128, 128], BF16)
    sel16_s = sb("sel16_s", [16, 16 * 128], BF16)
    identf_s = sb("identf_s", [128, 128], F32)
    onesc_s = sb("onesc_s", [128, 1], BF16)
    onesr_s = sb("onesr_s", [1, 128], BF16)
    w1_s = sb("w1_s", [128, 16], F32)
    w2_s = sb("w2_s", [128, 16], F32)
    rowA = sb("rowA", [1, TH], F32)
    rowB = sb("rowB", [1, TH], F32)
    rowC = sb("rowC", [1, TH], F32)
    rowD = sb("rowD", [1, TH], F32)
    rowbf = sb("rowbf", [1, TH], BF16)
    rowbf2 = sb("rowbf2", [1, TH], BF16)
    eps_col = sb("eps_col", [128, 1], F32)
    one_col = sb("one_col", [128, 1], F32)
    esm = ExitStack()
    sdt_s = sb("sdt_s", [128, 2048], BF16, esm)
    sns_s = sb("sns_s", [128, 2048], BF16, esm)
    sxf_s = sb("sxf_s", [64, 128], BF16, esm)
    sbsel_s = sb("sbsel_s", [64, 128], BF16, esm)
    ascan = sb("ascan", [128, 256], F32, esm)
    darr_s = sb("darr_s", [128, 16], F32, esm)
    dtb_s = sb("dtb_s", [128, 16], F32, esm)
    carry = sb("carry", [128, 256], F32, esm)
    wdt_s = sb("wdt_s", [128, HALF], BF16, esm)

    t_meta = P.dma("sy", lambda o: o.dma_start(out=smeta[:], in_=meta_d))
    ct = [t_meta]
    for s, dd in ((sdt_s, sdt_d), (sns_s, sns_d), (sxf_s, sxf_d), (sbsel_s, sbsel_d),
                  (identb_s, identb_d), (sel16_s, sel16_d), (identf_s, identf_d),
                  (onesc_s, onesc_d), (onesr_s, onesr_d), (darr_s, d_arr_d),
                  (dtb_s, dtb_d), (w1_s, w1_d), (w2_s, w2_d), (wdt_s, wdt)):
        ct.append(P.dma("sy", lambda o, s=s, dd=dd: o.dma_start(out=s[:], in_=dd)))
    ct.append(P.op("ve", lambda o: o.memset(eps_col[:], EPS)))
    ct.append(P.op("ve", lambda o: o.memset(one_col[:], 1.0)))
    t_ap = P.dma("sy", lambda o: o.dma_start(out=ascan[:], in_=aperm))
    t_a1 = P.op("se", lambda o: o.activation(carry[:], ascan[:], AF.Exp), waits=[t_ap])
    ct.append(P.op("se", lambda o: o.mul(ascan[:], carry[:], -1.0), waits=[t_a1]))
    CR = list(Prog.merge(ct))

    par_reg = nc.gpsimd.alloc_register("par_reg")
    PAR_TOK = P.mark("gp", lambda o: o.reg_load(par_reg, smeta[0:1, 0:1]),
                     waits=[t_meta])

    ps_stack = ExitStack()
    PS = [pst(f"ps{i}", [128, 512], F32, ps_stack) for i in range(5)]
    PROW = pst("prow", [1, 1024], F32, ps_stack)
    PTB = pst("ptb", [128, 512], BF16, ps_stack)

    def mm_group(psum_ap, tiles, waits, start=True, stop=True):
        n = len(tiles)
        def f(o):
            inst = None
            for i, (lf, rf) in enumerate(tiles):
                inst = o.matmul(psum_ap, lf(), rf(),
                                start=(start and i == 0), stop=(stop and i == n - 1))
            return inst
        return P.op("te", f, waits=waits)

    def bcast_f32(row_bf_fn, dst_fn, waits, ps_tile, width=512):
        t1 = mm_group(ps_tile[:, 0:width], [(lambda: onesr_s[:], row_bf_fn)], waits=waits)
        return P.op("se", lambda o: o.copy(dst_fn(), ps_tile[:, 0:width]), waits=[t1])

    poll_id = [0]

    def gp_poll(spec_p0, spec_p1, waits):
        (s0, n0), (s1, n1) = spec_p0, spec_p1
        poll_id[0] += 1
        pid = poll_id[0]
        psem = f"pf{pid}"

        @wants_semd
        def f(o, semd):
            with o.register(f"rn{pid}") as rn, o.register(f"fv{pid}") as fv, \
                 o.register(f"cnt{pid}") as cnt, o.register(f"nd{pid}") as nd:
                with o.If_eq(par_reg, 0):
                    o.reg_load(rn, smeta[0:1, 1 + n0:2 + n0])
                with o.Else():
                    o.reg_load(rn, smeta[0:1, 1 + n1:2 + n1])
                o.reg_mov(nd, 1)
                o.reg_mov(cnt, 0)
                with o.While(nd):
                    with o.If_eq(par_reg, 0):
                        o.dma_start(out=spoll[0:1, 0:1],
                                    in_=flg[s0[0], s0[1], 0:1]).then_inc(semd[psem], 16)
                    with o.Else():
                        o.dma_start(out=spoll[0:1, 0:1],
                                    in_=flg[s1[0], s1[1], 0:1]).then_inc(semd[psem], 16)
                    o.reg_add(cnt, cnt, 16)
                    o.wait_ge(semd[psem], cnt)
                    o.reg_load(fv, spoll[0:1, 0:1])
                    o.reg_alu(nd, fv, rn, ALU.not_equal)
            return None
        return P.mark("gp", f, waits=list(waits) + [PAR_TOK])

    def gp_flag(slot, nidx, waits):
        def f(o, sem):
            with o.If_eq(par_reg, 0):
                o.dma_start(out=flg[0, slot, 0:1],
                            in_=smeta[0:1, 1 + nidx:2 + nidx]).then_inc(sem, 16)
            with o.Else():
                o.dma_start(out=flg[1, slot, 0:1],
                            in_=smeta[0:1, 1 + nidx:2 + nidx]).then_inc(sem, 16)
        return P.dma_manual("gp", f, waits=list(waits) + [PAR_TOK])

    xnorm = sb("xnorm", [128, NK * TCH], BF16, esm)
    yg = sb("yg", [128, NCB * TCH], BF16, esm)
    dtin_sb = sb("dtin_sb", [128, TCH], BF16, esm)
    rstd_bc = sb("rstd_bc", [128, TCH], F32, esm)
    xtl = [sb(f"xtl{i}", [128, TCH], F32, esm) for i in range(2)]
    sqb = sb("sqb", [128, 2 * TCH], BF16, esm)
    g_cb = sb("g_cb", [128, TCH], F32, esm)
    cm_cb = sb("cm_cb", [128, TCH], F32, esm)
    xb_cb = sb("xb_cb", [64, TCH], BF16, esm)
    dtsp = sb("dtsp", [128, TCH], F32, esm)
    dtspb = sb("dtspb", [128, TCH], BF16, esm)
    dtx = sb("dtx", [128, TCH], F32, esm)
    dtxb = sb("dtxb", [128, TCH], BF16, esm)
    xfb = sb("xfb", [128, TCH], BF16, esm)
    brep = sb("brep", [128, TCH], F32, esm)
    diagd = sb("diagd", [128, 128], BF16, esm)
    sp_l = sb("sp_l", [128, TCH], F32, esm)
    da_sbs = [sb(f"da_sb{i}", [128, TCH], F32, esm) for i in range(2)]
    dbx_sb = sb("dbx_sb", [128, TCH], F32, esm)
    h_sb = sb("h_sb", [128, TCH], F32, esm)
    m_sb = sb("m_sb", [128, TCH], BF16, esm)
    wcb = [sb(f"wcb{i}", [128, 16 * 320], BF16, esm) for i in range(2)]
    wob = [sb(f"wob{i}", [128, 16 * 128], BF16, esm) for i in range(2)]
    stage = sb("stage", [128, TCH], F32, esm)

    d_xn = dbgt("xn", [NK, 128, T])
    d_zx = dbgt("zx", [NCB, 5, 128, TCH])
    d_hh = dbgt("hh", [3, 128, TCH])
    d_y = dbgt("y", [NCB, 128, T])
    d_op = dbgt("op", [D, T])

    wcb_busy = [None, None]
    wob_busy = [None, None]

    def load_kpm(eng, dst, src_cols, kcnt, mw, waits):
        tok = None
        for k0 in range(0, kcnt, 8):
            kc = min(8, kcnt - k0)
            def f(o, k0=k0, kc=kc):
                return o.dma_start(
                    out=dst[:, k0 * mw:(k0 + kc) * mw].rearrange("p (k m) -> p k m", k=kc),
                    in_=src_cols[k0 * 128:(k0 + kc) * 128, :].rearrange("(k p) m -> p k m", p=128))
            tok = P.dma(eng, f, waits)
        return tok

    for tc in range(n_tc):
        t0 = tc * TCH

        prev_mm = {0: None, 1: None}
        sq_of = {}
        for k in range(NK):
            tx = P.dma("sy", lambda o, k=k, t0=t0: o.dma_start(
                out=xtl[k % 2][:], in_=x_t[k * 128:(k + 1) * 128, t0:t0 + TCH]),
                waits=CR + [sq_of.get(k - 2)])
            tq = P.op("se", lambda o, k=k: o.activation(
                sqb[:, (k % 2) * TCH:(k % 2) * TCH + TCH], xtl[k % 2][:], AF.Square),
                waits=[tx, prev_mm[1]])
            sq_of[k] = tq
            for nn in range(2):
                prev_mm[nn] = mm_group(
                    PROW[0:1, nn * 512:nn * 512 + 512],
                    [(lambda: onesc_s[:],
                      lambda k=k, nn=nn: sqb[:, (k % 2) * TCH + nn * 512:(k % 2) * TCH + nn * 512 + 512])],
                    waits=[tq, prev_mm[nn]], start=(k == 0), stop=(k == NK - 1))
        st_toks = []
        for nn in range(2):
            tr1 = P.op("se", lambda o, nn=nn: o.activation(
                rowA[0:1, nn * 512:nn * 512 + 512], PROW[0:1, nn * 512:nn * 512 + 512],
                AF.Ln, bias=eps_col[0:1, 0:1], scale=1.0 / 2048.0), waits=[prev_mm[nn]])
            tr2 = P.op("se", lambda o, nn=nn: o.activation(
                rowA[0:1, nn * 512:nn * 512 + 512], rowA[0:1, nn * 512:nn * 512 + 512],
                AF.Exp, scale=-0.5), waits=[tr1])
            trb = P.op("ve", lambda o, nn=nn: o.tensor_copy(
                rowbf[0:1, nn * 512:nn * 512 + 512], rowA[0:1, nn * 512:nn * 512 + 512]),
                waits=[tr2])
            st_toks.append(bcast_f32(
                lambda nn=nn: rowbf[0:1, nn * 512:nn * 512 + 512],
                lambda nn=nn: rstd_bc[:, nn * 512:nn * 512 + 512],
                [trb] + st_toks, PS[4]))
        xn_tok = [None] * NK
        for k in range(NK):
            tx = P.dma("sy", lambda o, k=k, t0=t0: o.dma_start(
                out=xtl[k % 2][:], in_=x_t[k * 128:(k + 1) * 128, t0:t0 + TCH]),
                waits=st_toks + [xn_tok[k - 2] if k >= 2 else None])
            xn_tok[k] = P.op("ve", lambda o, k=k: o.scalar_tensor_tensor(
                xnorm[:, k * TCH:(k + 1) * TCH], xtl[k % 2][:],
                w1_s[:, k:k + 1], rstd_bc[:], ALU.mult, ALU.mult),
                waits=[tx] + st_toks)
            if d_xn is not None:
                P.dma("gp", lambda o, k=k, t0=t0: o.dma_start(
                    out=dbg_aps["xn"][k, :, t0:t0 + TCH],
                    in_=xnorm[:, k * TCH:(k + 1) * TCH]), waits=[xn_tok[k], PAR_TOK])

        if stop_after == "m0":
            return _render(nc, P, top)
        twd = load_kpm("gp", wcb[0], win[:, NCB * 320:NCB * 320 + 128], NK, 128,
                       [wcb_busy[0], PAR_TOK])
        dtin_toks = []
        prev = None
        for nn in range(2):
            tok = mm_group(PS[0][:, 0:512],
                           [(lambda k=k: wcb[0][:, k * 128:(k + 1) * 128],
                             lambda k=k, nn=nn: xnorm[:, k * TCH + nn * 512:k * TCH + nn * 512 + 512])
                            for k in range(NK)],
                           waits=[twd, prev] + xn_tok)
            prev = P.op("ve", lambda o, nn=nn: o.tensor_copy(
                dtin_sb[:, nn * 512:nn * 512 + 512], PS[0][:, 0:512]), waits=[tok])
            dtin_toks.append(prev)
        wcb_busy[0] = dtin_toks[-1]

        if stop_after == "m1":
            return _render(nc, P, top)
        for cb in range(n_cb):
            wb = wcb[cb % 2]
            tw = load_kpm("gp", wb, win[:, cb * 320:(cb + 1) * 320], NK, 320,
                          [wcb_busy[cb % 2], PAR_TOK])

            def rhs(k, nn):
                return xnorm[:, k * TCH + nn * 512:k * TCH + nn * 512 + 512]

            z_mm, cm_mm = [], []
            for nn in range(2):
                z_mm.append(mm_group(
                    PS[0 + nn][:, 0:512],
                    [(lambda k=k, wb=wb: wb[:, k * 320:k * 320 + 128],
                      lambda k=k, nn=nn: rhs(k, nn)) for k in range(NK)],
                    waits=[tw] + xn_tok))
                cm_mm.append(mm_group(
                    PS[2 + nn][:, 0:512],
                    [(lambda k=k, wb=wb: wb[:, k * 320 + 128:k * 320 + 256],
                      lambda k=k, nn=nn: rhs(k, nn)) for k in range(NK)],
                    waits=[tw] + xn_tok))
            tg = [P.op("se", lambda o, nn=nn: o.activation(
                g_cb[:, nn * 512:nn * 512 + 512], PS[0 + nn][:, 0:512], AF.Silu),
                waits=[z_mm[nn]]) for nn in range(2)]
            tcm = [P.op("ve", lambda o, nn=nn: o.tensor_copy(
                cm_cb[:, nn * 512:nn * 512 + 512], PS[2 + nn][:, 0:512]),
                waits=[cm_mm[nn]]) for nn in range(2)]
            txb = []
            for nn in range(2):
                tok = mm_group(
                    PS[4][0:64, 0:512],
                    [(lambda k=k, wb=wb: wb[:, k * 320 + 256:k * 320 + 320],
                      lambda k=k, nn=nn: rhs(k, nn)) for k in range(NK)],
                    waits=[tw] + xn_tok + txb)
                txb.append(P.op("ve", lambda o, nn=nn: o.tensor_copy(
                    xb_cb[0:64, nn * 512:nn * 512 + 512], PS[4][0:64, 0:512]),
                    waits=[tok]))
            if stop_after == "m2a":
                return _render(nc, P, top)
            dt_done = []
            for nn in range(2):
                tok = mm_group(PS[4][:, 0:512],
                               [(lambda cb=cb: wdt_s[:, cb * 128:(cb + 1) * 128],
                                 lambda nn=nn: dtin_sb[:, nn * 512:nn * 512 + 512])],
                               waits=txb + dtin_toks + dt_done)
                trm = P.op("se", lambda o, nn=nn, cb=cb: o.activation(
                    dtsp[:, nn * 512:nn * 512 + 512], PS[4][:, 0:512], AF.Relu,
                    bias=dtb_s[:, cb:cb + 1]), waits=[tok])
                ta = P.op("se", lambda o, nn=nn, cb=cb: o.activation(
                    sp_l[:, nn * 512:nn * 512 + 512], PS[4][:, 0:512], AF.Abs,
                    bias=dtb_s[:, cb:cb + 1]), waits=[tok])
                te_ = P.op("se", lambda o, nn=nn: o.activation(
                    sp_l[:, nn * 512:nn * 512 + 512], sp_l[:, nn * 512:nn * 512 + 512],
                    AF.Exp, scale=-1.0), waits=[ta])
                tl = P.op("se", lambda o, nn=nn: o.activation(
                    sp_l[:, nn * 512:nn * 512 + 512], sp_l[:, nn * 512:nn * 512 + 512],
                    AF.Ln, bias=one_col[:]), waits=[te_])
                tsp = P.op("ve", lambda o, nn=nn: o.tensor_add(
                    dtsp[:, nn * 512:nn * 512 + 512], dtsp[:, nn * 512:nn * 512 + 512],
                    sp_l[:, nn * 512:nn * 512 + 512]), waits=[trm, tl])
                dt_done.append(tsp)
            tspb = P.op("ve", lambda o: o.tensor_copy(dtspb[:], dtsp[:]), waits=dt_done)

            if stop_after == "m2b":
                return _render(nc, P, top)
            dtx_done, xf_done = [], []
            for nn in range(2):
                tok = mm_group(PS[4][:, 0:512],
                               [(lambda: sxf_s[:],
                                 lambda nn=nn: xb_cb[0:64, nn * 512:nn * 512 + 512])],
                               waits=txb + dt_done + dtx_done + xf_done)
                td = P.op("ve", lambda o, nn=nn: o.tensor_mul(
                    dtx[:, nn * 512:nn * 512 + 512], dtsp[:, nn * 512:nn * 512 + 512],
                    PS[4][:, 0:512]), waits=[tok] + dt_done)
                tf = P.op("se", lambda o, nn=nn: o.copy(
                    xfb[:, nn * 512:nn * 512 + 512], PS[4][:, 0:512]), waits=[tok, td])
                dtx_done.append(td)
                xf_done.append(tf)
            if stop_after == "m2b1":
                return _render(nc, P, top)
            tdxb = P.op("ve", lambda o: o.tensor_copy(dtxb[:], dtx[:]), waits=dtx_done)
            br_done = []
            for nn in range(2):
                tok = mm_group(PS[4][:, 0:512],
                               [(lambda: sbsel_s[:],
                                 lambda nn=nn: xb_cb[0:64, nn * 512:nn * 512 + 512])],
                               waits=dtx_done + xf_done + br_done)
                br_done.append(P.op("ve", lambda o, nn=nn: o.tensor_copy(
                    brep[:, nn * 512:nn * 512 + 512], PS[4][:, 0:512]), waits=[tok]))
            if stop_after == "m2b2":
                return _render(nc, P, top)
            tdg = P.op("ve", lambda o, cb=cb: o.tensor_scalar_mul(
                diagd[:], identb_s[:], darr_s[:, cb:cb + 1]), waits=CR + br_done)
            if stop_after == "m2b3":
                return _render(nc, P, top)

            if d_zx is not None and tc == 0:
                for j, (buf, wt) in enumerate(((g_cb, tg), (cm_cb, tcm), (dtsp, dt_done),
                                               (dtx, dtx_done), (brep, br_done))):
                    P.dma("sy", lambda o, cb=cb, j=j, buf=buf: o.dma_start(
                        out=dbg_aps["zx"][cb, j], in_=buf[:]), waits=list(wt))

            if stop_after == "m2c":
                return _render(nc, P, top)
            ys_mm = {}
            for nn in range(2):
                ys_mm[nn] = mm_group(
                    PS[2 + nn][:, 0:512],
                    [(lambda: diagd[:], lambda nn=nn: xfb[:, nn * 512:nn * 512 + 512])],
                    waits=[tdg] + xf_done + tcm, start=True, stop=False)
            scan_hist, db_hist = {}, {}

            def emit_ys(sl, tm_tok):
                for nn in range(2):
                    ys_mm[nn] = mm_group(
                        PS[2 + nn][:, 0:512],
                        [(lambda sl=sl: sns_s[:, sl * 128:(sl + 1) * 128],
                          lambda nn=nn: m_sb[:, nn * 512:nn * 512 + 512])],
                        waits=[tm_tok, ys_mm[nn]], start=False, stop=(sl == 15))

            pend_ys = None
            for sl in range(16):
                st = cb * 16 + sl
                par = sl % 2
                pb0 = PS[0] if par == 0 else PS[4]
                da_toks, db_toks = [], []
                for nn in range(2):
                    psel = pb0 if nn == 0 else PS[1]
                    if nn == 0:
                        bank_free = [db_hist[(sl - 2, 0)]] if sl >= 2 else \
                            ([tg[0]] if sl == 0 else list(br_done))
                    else:
                        bank_free = [db_hist[(sl - 1, 1)]] if sl >= 1 else [tg[1]]
                    tka = mm_group(
                        psel[:, 0:512],
                        [(lambda sl=sl: sdt_s[:, sl * 128:(sl + 1) * 128],
                          lambda nn=nn: dtspb[:, nn * 512:nn * 512 + 512])],
                        waits=[tspb] + bank_free)
                    da_toks.append(P.op(
                        "se", lambda o, nn=nn, st=st, par=par, psel=psel: o.activation(
                            da_sbs[par][:, nn * 512:nn * 512 + 512], psel[:, 0:512],
                            AF.Exp, scale=ascan[:, st:st + 1]),
                        waits=[tka, scan_hist.get(sl - 2)]))
                for nn in range(2):
                    psel = pb0 if nn == 0 else PS[1]
                    tkb = mm_group(
                        psel[:, 0:512],
                        [(lambda sl=sl: sdt_s[:, sl * 128:(sl + 1) * 128],
                          lambda nn=nn: dtxb[:, nn * 512:nn * 512 + 512])],
                        waits=[tdxb, da_toks[nn]])
                    db_hist[(sl, nn)] = P.op(
                        "ve", lambda o, nn=nn, psel=psel: o.tensor_mul(
                            dbx_sb[:, nn * 512:nn * 512 + 512], psel[:, 0:512],
                            brep[:, nn * 512:nn * 512 + 512]),
                        waits=[tkb] + br_done)
                    db_toks.append(db_hist[(sl, nn)])
                if pend_ys is not None:
                    emit_ys(*pend_ys)
                if tc == 0:
                    tscan = P.op("ve", lambda o, par=par: o.tensor_tensor_scan(
                        h_sb[:], da_sbs[par][:], dbx_sb[:], 0.0,
                        ALU.mult, ALU.add),
                        waits=da_toks + db_toks)
                else:
                    tscan = P.op("ve", lambda o, st=st, par=par: o.tensor_tensor_scan(
                        h_sb[:], da_sbs[par][:], dbx_sb[:],
                        carry[:, st:st + 1], ALU.mult, ALU.add),
                        waits=da_toks + db_toks)
                scan_hist[sl] = tscan
                tcar = P.op("ve", lambda o, st=st: o.tensor_copy(
                    carry[:, st:st + 1], h_sb[:, TCH - 1:TCH]), waits=[tscan])
                tm_ = P.op("ve", lambda o: o.tensor_mul(
                    m_sb[:], h_sb[:], cm_cb[:]),
                    waits=[tscan, ys_mm[0], ys_mm[1]] + tcm)
                pend_ys = (sl, tm_)
                if d_hh is not None and tc == 0 and cb == 0 and sl == 0:
                    P.dma("sy", lambda o: o.dma_start(out=dbg_aps["hh"][0], in_=da_sbs[0][:]),
                          waits=da_toks)
                    P.dma("sy", lambda o: o.dma_start(out=dbg_aps["hh"][1], in_=dbx_sb[:]),
                          waits=db_toks)
                    P.dma("sy", lambda o: o.dma_start(out=dbg_aps["hh"][2], in_=h_sb[:]),
                          waits=[tscan])

            emit_ys(*pend_ys)
            if stop_after == "m2d":
                return _render(nc, P, top)
            yg_toks = []
            for nn in range(2):
                yg_toks.append(P.op("ve", lambda o, nn=nn, cb=cb: o.tensor_mul(
                    yg[:, cb * TCH + nn * 512:cb * TCH + nn * 512 + 512],
                    PS[2 + nn][:, 0:512], g_cb[:, nn * 512:nn * 512 + 512]),
                    waits=[ys_mm[nn]] + tg))
            if d_y is not None:
                P.dma("gp", lambda o, cb=cb, t0=t0: o.dma_start(
                    out=dbg_aps["y"][cb, :, t0:t0 + TCH],
                    in_=yg[:, cb * TCH:(cb + 1) * TCH]), waits=yg_toks + [PAR_TOK])
            P.barrier()

        if stop_after == "m2":
            return _render(nc, P, top)
        prev_store = None
        prev_ev3 = []
        for m in range(16):
            wb = wob[m % 2]
            tw = load_kpm("gp", wb, wout[:, m * 128:(m + 1) * 128], NK, 128,
                          [wob_busy[m % 2], PAR_TOK])
            ev = []
            mmt = None
            for nn in range(2):
                mmt = mm_group(
                    PS[nn][:, 0:512],
                    [(lambda k=k, wb=wb: wb[:, k * 128:(k + 1) * 128],
                      lambda k=k, nn=nn: yg[:, k * TCH + nn * 512:k * TCH + nn * 512 + 512])
                     for k in range(NCB)],
                    waits=[tw] + prev_ev3)
                ev.append(P.op("ve", lambda o, nn=nn: o.tensor_copy(
                    stage[:, nn * 512:nn * 512 + 512], PS[nn][:, 0:512]),
                    waits=[mmt, prev_store]))
            wob_busy[m % 2] = mmt
            prev_ev3 = list(ev)

            def store_partial(o, sem, m=m, tc=tc):
                with o.If_eq(par_reg, 0):
                    o.dma_start(out=ex4[tc, m * 128:(m + 1) * 128, :],
                                in_=stage[:]).then_inc(sem, 16)
                with o.Else():
                    o.dma_start(out=ex4[2 + tc, m * 128:(m + 1) * 128, :],
                                in_=stage[:]).then_inc(sem, 16)
            prev_store = P.dma_manual("gp", store_partial, waits=ev + [PAR_TOK])
            if d_op is not None:
                P.dma("sy", lambda o, m=m, t0=t0: o.dma_start(
                    out=dbg_aps["op"][m * 128:(m + 1) * 128, t0:t0 + TCH],
                    in_=stage[:]), waits=ev)
        gp_flag(tc, tc, waits=P.now())
        P.barrier()
    esm.close()

    if stop_after == "mamba":
        return _render(nc, P, top)

    esc = ExitStack()
    hbuf = sb("hbuf", [128, TH], F32, esc)
    abuf = sb("abuf", [128, TH], F32, esc)
    bbuf = sb("bbuf", [128, TH], F32, esc)
    xcbuf = sb("xcbuf", [128, TH], F32, esc)
    sqc = sb("sqc", [128, TH], BF16, esc)
    hn = sb("hn", [128, NK * TH], BF16, esc)
    rstd2_bc = sb("rstd2_bc", [128, TH], F32, esc)

    d_h = dbgt("h", [D, TH])
    d_hn = dbgt("hn", [D, TH])

    t_poll = gp_poll(((1, 0), 0), ((0, 1), 1), waits=P.now())

    prev_mm = {0: None, 1: None}
    hw_toks = []
    pt1 = pt2 = ptw = ptq = None
    for k in range(NK):
        tx = P.dma("sy", lambda o, k=k: o.dma_start(
            out=xcbuf[:], in_=x_h[k * 128:(k + 1) * 128, :]), waits=[t_poll, pt1])

        def ld_own(o, sem, k=k):
            with o.If_eq(par_reg, 0):
                o.dma_start(out=abuf[:], in_=ex4[0, k * 128:(k + 1) * 128, :]).then_inc(sem, 16)
            with o.Else():
                o.dma_start(out=abuf[:], in_=ex4[3, k * 128:(k + 1) * 128, :]).then_inc(sem, 16)

        def ld_partner(o, sem, k=k):
            with o.If_eq(par_reg, 0):
                o.dma_start(out=bbuf[:], in_=ex4[2, k * 128:(k + 1) * 128, :]).then_inc(sem, 16)
            with o.Else():
                o.dma_start(out=bbuf[:], in_=ex4[1, k * 128:(k + 1) * 128, :]).then_inc(sem, 16)
        ta = P.dma_manual("gp", ld_own, waits=[t_poll, pt1, PAR_TOK])
        tb = P.dma_manual("gp", ld_partner, waits=[t_poll, pt2, PAR_TOK])
        pt1 = P.op("ve", lambda o: o.tensor_add(hbuf[:], xcbuf[:], abuf[:]),
                   waits=[tx, ta, ptw, ptq])
        pt2 = P.op("ve", lambda o: o.tensor_add(hbuf[:], hbuf[:], bbuf[:]),
                   waits=[pt1, tb])
        ptw = P.dma("sy", lambda o, k=k: o.dma_start(
            out=h_scr[k * 128:(k + 1) * 128, :], in_=hbuf[:]), waits=[pt2])
        if d_h is not None:
            P.dma("sy", lambda o, k=k: o.dma_start(
                out=dbg_aps["h"][k * 128:(k + 1) * 128, :], in_=hbuf[:]), waits=[pt2])
        ptq = P.op("se", lambda o: o.activation(sqc[:], hbuf[:], AF.Square), waits=[pt2])
        for nn in range(2):
            prev_mm[nn] = mm_group(
                PROW[0:1, nn * 512:nn * 512 + 512],
                [(lambda: onesc_s[:], lambda nn=nn: sqc[:, nn * 512:nn * 512 + 512])],
                waits=[ptq, prev_mm[nn]], start=(k == 0), stop=(k == NK - 1))
        hw_toks.append(ptw)
    st2 = []
    for nn in range(2):
        tr1 = P.op("se", lambda o, nn=nn: o.activation(
            rowA[0:1, nn * 512:nn * 512 + 512], PROW[0:1, nn * 512:nn * 512 + 512],
            AF.Ln, bias=eps_col[0:1, 0:1], scale=1.0 / 2048.0), waits=[prev_mm[nn]])
        tr2 = P.op("se", lambda o, nn=nn: o.activation(
            rowA[0:1, nn * 512:nn * 512 + 512], rowA[0:1, nn * 512:nn * 512 + 512],
            AF.Exp, scale=-0.5), waits=[tr1])
        trb = P.op("ve", lambda o, nn=nn: o.tensor_copy(
            rowbf[0:1, nn * 512:nn * 512 + 512], rowA[0:1, nn * 512:nn * 512 + 512]),
            waits=[tr2])
        st2.append(bcast_f32(
            lambda nn=nn: rowbf[0:1, nn * 512:nn * 512 + 512],
            lambda nn=nn: rstd2_bc[:, nn * 512:nn * 512 + 512], [trb] + st2, PS[4]))
    hn_toks = []
    prev = None
    for k in range(NK):
        tr = P.dma("sy", lambda o, k=k: o.dma_start(
            out=hbuf[:], in_=h_scr[k * 128:(k + 1) * 128, :]),
            waits=st2 + [prev, hw_toks[k], ptq])
        prev = P.op("ve", lambda o, k=k: o.scalar_tensor_tensor(
            hn[:, k * TH:(k + 1) * TH], hbuf[:], w2_s[:, k:k + 1], rstd2_bc[:],
            ALU.mult, ALU.mult), waits=[tr] + st2)
        hn_toks.append(prev)
        if d_hn is not None:
            P.dma("gp", lambda o, k=k: o.dma_start(
                out=dbg_aps["hn"][k * 128:(k + 1) * 128, :],
                in_=hn[:, k * TH:(k + 1) * TH]), waits=[prev, PAR_TOK])
    P.barrier()
    if stop_after == "combine":
        return _render(nc, P, top)

    esl = ExitStack()
    wgb = [sb(f"wgb{i}", [128, 2048], BF16, esl) for i in range(2)]
    wub = [sb(f"wub{i}", [128, 2048], BF16, esl) for i in range(2)]
    wdb = [sb(f"wdb{i}", [128, 32 * 128], BF16, esl) for i in range(2)]
    act = sb("act", [128, 32 * TH], BF16, esl)
    sgb = sb("sgb", [128, TH], F32, esl)
    mstage = sb("mstage", [128, TH], F32, esl)
    mbuf = sb("mbuf", [128, TH], F32, esl)
    hbuf2 = sb("hbuf2", [128, TH], F32, esl)
    xbuf2 = sb("xbuf2", [128, TH], F32, esl)
    s1bf = sb("s1bf", [128, TH], BF16, esl)
    sq1 = sb("sq1", [128, TH], BF16, esl)

    d_s1 = dbgt("s1", [D, TH])

    mean_mm = {0: None, 1: None}
    ssq_mm = {0: None, 1: None}
    wg_busy = [None, None]
    wd_busy = [None, None]
    for ih in range(2):
        if stop_after == "mlpd" and ih == 1:
            return _render(nc, P, top)
        act_toks = []
        for ii in range(32):
            ig = ih * 32 + ii
            bsy = wg_busy[ii % 2]
            twg = load_kpm("gp", wgb[ii % 2], wg_d[:, ig * 128:(ig + 1) * 128],
                           NK, 128, [bsy, PAR_TOK] + hn_toks[-1:])
            twu = load_kpm("gp", wub[ii % 2], wu_d[:, ig * 128:(ig + 1) * 128],
                           NK, 128, [bsy, PAR_TOK])
            for nn in range(2):
                tgm = mm_group(
                    PS[0][:, 0:512],
                    [(lambda k=k, ii=ii: wgb[ii % 2][:, k * 128:(k + 1) * 128],
                      lambda k=k, nn=nn: hn[:, k * TH + nn * 512:k * TH + nn * 512 + 512])
                     for k in range(NK)],
                    waits=[twg] + hn_toks + act_toks[-1:])
                tum = mm_group(
                    PS[1][:, 0:512],
                    [(lambda k=k, ii=ii: wub[ii % 2][:, k * 128:(k + 1) * 128],
                      lambda k=k, nn=nn: hn[:, k * TH + nn * 512:k * TH + nn * 512 + 512])
                     for k in range(NK)],
                    waits=[twu] + hn_toks + act_toks[-1:])
                tsg = P.op("se", lambda o, nn=nn: o.activation(
                    sgb[:, nn * 512:nn * 512 + 512], PS[0][:, 0:512], AF.Silu),
                    waits=[tgm])
                tac = P.op("ve", lambda o, ii=ii, nn=nn: o.tensor_mul(
                    act[:, ii * TH + nn * 512:ii * TH + nn * 512 + 512],
                    PS[1][:, 0:512], sgb[:, nn * 512:nn * 512 + 512]),
                    waits=[tum, tsg])
                act_toks.append(tac)
            wg_busy[ii % 2] = act_toks[-1]
        if stop_after == "mlpa" and ih == 0:
            return _render(nc, P, top)
        if stop_after == "mlpc" and ih == 1:
            return _render(nc, P, top)
        import os as _os
        prev_evs = []
        prev_stg = None
        for m in range(int(_os.environ.get("N_MD", "16"))):
            twdn = load_kpm("gp", wdb[m % 2],
                            wd_d[ih * 4096:(ih + 1) * 4096, m * 128:(m + 1) * 128],
                            32, 128, [wd_busy[m % 2], PAR_TOK] + act_toks[-1:])
            evs = []
            mmt = None
            for nn in range(2):
                mmt = mm_group(
                    PS[2][:, 0:512],
                    [(lambda k=k, m=m: wdb[m % 2][:, k * 128:(k + 1) * 128],
                      lambda k=k, nn=nn: act[:, k * TH + nn * 512:k * TH + nn * 512 + 512])
                     for k in range(32)],
                    waits=[twdn] + act_toks + evs + prev_evs)
                if ih == 0:
                    evs.append(P.op("ve", lambda o, nn=nn: o.tensor_copy(
                        mstage[:, nn * 512:nn * 512 + 512], PS[2][:, 0:512]),
                        waits=[mmt, prev_stg]))
                else:
                    tmb = P.dma("sy", lambda o, m=m, nn=nn: o.dma_start(
                        out=mbuf[:, nn * 512:nn * 512 + 512],
                        in_=macc_scr[m * 128:(m + 1) * 128, nn * 512:nn * 512 + 512]),
                        waits=[mmt])
                    thb = P.dma("sy", lambda o, m=m, nn=nn: o.dma_start(
                        out=hbuf2[:, nn * 512:nn * 512 + 512],
                        in_=h_scr[m * 128:(m + 1) * 128, nn * 512:nn * 512 + 512]),
                        waits=[mmt])
                    txb2 = P.dma("sy", lambda o, m=m, nn=nn: o.dma_start(
                        out=xbuf2[:, nn * 512:nn * 512 + 512],
                        in_=x_h[m * 128:(m + 1) * 128, nn * 512:nn * 512 + 512]),
                        waits=[mmt])
                    e1 = P.op("ve", lambda o, nn=nn: o.tensor_add(
                        mstage[:, nn * 512:nn * 512 + 512], PS[2][:, 0:512],
                        mbuf[:, nn * 512:nn * 512 + 512]),
                        waits=[mmt, tmb, prev_stg])
                    e2 = P.op("ve", lambda o, nn=nn: o.tensor_add(
                        mstage[:, nn * 512:nn * 512 + 512],
                        mstage[:, nn * 512:nn * 512 + 512],
                        hbuf2[:, nn * 512:nn * 512 + 512]), waits=[e1, thb])
                    e3 = P.op("ve", lambda o, nn=nn: o.tensor_add(
                        mstage[:, nn * 512:nn * 512 + 512],
                        mstage[:, nn * 512:nn * 512 + 512],
                        xbuf2[:, nn * 512:nn * 512 + 512]), waits=[e2, txb2])
                    evs.append(e3)
            prev_evs = list(evs)
            if ih == 0:
                prev_stg = P.dma("sy", lambda o, m=m: o.dma_start(
                    out=macc_scr[m * 128:(m + 1) * 128, :], in_=mstage[:]), waits=evs)
                wd_busy[m % 2] = prev_stg
            else:
                tws = P.dma("sy", lambda o, m=m: o.dma_start(
                    out=s1_scr[m * 128:(m + 1) * 128, :], in_=mstage[:]), waits=evs)
                prev_stg = tws
                if d_s1 is not None:
                    P.dma("sy", lambda o, m=m: o.dma_start(
                        out=dbg_aps["s1"][m * 128:(m + 1) * 128, :], in_=mstage[:]),
                        waits=evs)
                tb1 = P.op("ve", lambda o: o.tensor_copy(s1bf[:], mstage[:]), waits=evs)
                tq1 = P.op("se", lambda o: o.activation(sq1[:], mstage[:], AF.Square),
                           waits=evs)
                for nn in range(2):
                    psm = PS[3] if nn == 0 else PS[4]
                    mean_mm[nn] = mm_group(
                        psm[0:1, 0:512],
                        [(lambda: onesc_s[:],
                          lambda nn=nn: s1bf[:, nn * 512:nn * 512 + 512])],
                        waits=[tb1, mean_mm[nn]], start=(m == 0), stop=(m == 15))
                    ssq_mm[nn] = mm_group(
                        PROW[0:1, nn * 512:nn * 512 + 512],
                        [(lambda: onesc_s[:],
                          lambda nn=nn: sq1[:, nn * 512:nn * 512 + 512])],
                        waits=[tq1, ssq_mm[nn]], start=(m == 0), stop=(m == 15))
                wd_busy[m % 2] = tws

    if stop_after == "mlpb":
        return _render(nc, P, top)
    r_toks = []
    for nn in range(2):
        psm = PS[3] if nn == 0 else PS[4]
        t1 = P.op("se", lambda o, nn=nn, psm=psm: o.activation(
            rowA[0:1, nn * 512:nn * 512 + 512], psm[0:1, 0:512], AF.Copy,
            scale=1.0 / 2048.0), waits=[mean_mm[nn]])
        t2 = P.op("se", lambda o, nn=nn: o.activation(
            rowB[0:1, nn * 512:nn * 512 + 512], rowA[0:1, nn * 512:nn * 512 + 512],
            AF.Square), waits=[t1])
        t3 = P.op("se", lambda o, nn=nn: o.activation(
            rowC[0:1, nn * 512:nn * 512 + 512], PROW[0:1, nn * 512:nn * 512 + 512],
            AF.Copy, scale=1.0 / 2048.0), waits=[ssq_mm[nn]])
        t4 = P.op("ve", lambda o, nn=nn: o.tensor_sub(
            rowC[0:1, nn * 512:nn * 512 + 512], rowC[0:1, nn * 512:nn * 512 + 512],
            rowB[0:1, nn * 512:nn * 512 + 512]), waits=[t2, t3])
        t5 = P.op("se", lambda o, nn=nn: o.activation(
            rowB[0:1, nn * 512:nn * 512 + 512], rowC[0:1, nn * 512:nn * 512 + 512],
            AF.Ln, bias=eps_col[0:1, 0:1]), waits=[t4])
        t6 = P.op("se", lambda o, nn=nn: o.activation(
            rowD[0:1, nn * 512:nn * 512 + 512], rowB[0:1, nn * 512:nn * 512 + 512],
            AF.Exp, scale=-0.5), waits=[t5])
        t7 = P.op("se", lambda o, nn=nn: o.activation(
            rowC[0:1, nn * 512:nn * 512 + 512], rowC[0:1, nn * 512:nn * 512 + 512],
            AF.Ln, scale=2048.0), waits=[t5, t6])
        t8 = P.op("ve", lambda o, nn=nn: o.tensor_sub(
            rowB[0:1, nn * 512:nn * 512 + 512], rowB[0:1, nn * 512:nn * 512 + 512],
            rowC[0:1, nn * 512:nn * 512 + 512]), waits=[t7])
        t9 = P.op("se", lambda o, nn=nn: o.activation(
            rowC[0:1, nn * 512:nn * 512 + 512], rowB[0:1, nn * 512:nn * 512 + 512],
            AF.Exp, scale=0.5), waits=[t8])
        tmb_ = P.op("ve", lambda o, nn=nn: o.tensor_copy(
            rowbf[0:1, nn * 512:nn * 512 + 512], rowA[0:1, nn * 512:nn * 512 + 512]),
            waits=[t2, t9])
        trb_ = P.op("ve", lambda o, nn=nn: o.tensor_copy(
            rowbf2[0:1, nn * 512:nn * 512 + 512], rowD[0:1, nn * 512:nn * 512 + 512]),
            waits=[t6])
        r_toks += [tmb_, trb_, t9]
    INV_TOK = r_toks[-1]
    P.barrier()
    esl.close()
    esc.close()
    if stop_after == "mlp":
        return _render(nc, P, top)

    esf = ExitStack()
    x1bf = sb("x1bf", [128, NK * TH], BF16, esf)
    mu1_bc = sb("mu1_bc", [128, TH], F32, esf)
    rstd1_bc = sb("rstd1_bc", [128, TH], F32, esf)
    x1stage = sb("x1stage", [128, TH], F32, esf)
    s1buf = sb("s1buf", [128, TH], F32, esf)
    d_x1 = dbgt("x1", [D, TH])
    bt = []
    for nn in range(2):
        bt.append(bcast_f32(lambda nn=nn: rowbf[0:1, nn * 512:nn * 512 + 512],
                            lambda nn=nn: mu1_bc[:, nn * 512:nn * 512 + 512],
                            r_toks + bt, PS[3]))
        bt.append(bcast_f32(lambda nn=nn: rowbf2[0:1, nn * 512:nn * 512 + 512],
                            lambda nn=nn: rstd1_bc[:, nn * 512:nn * 512 + 512],
                            r_toks + bt, PS[4]))
    x1_toks = []
    prev = None
    for k in range(NK):
        trd = P.dma("sy", lambda o, k=k: o.dma_start(
            out=s1buf[:], in_=s1_scr[k * 128:(k + 1) * 128, :]),
            waits=bt + [prev])
        tsb = P.op("ve", lambda o: o.tensor_sub(x1stage[:], s1buf[:], mu1_bc[:]),
                   waits=[trd, prev] + bt)
        tmu = P.op("ve", lambda o: o.tensor_mul(x1stage[:], x1stage[:], rstd1_bc[:]),
                   waits=[tsb])
        tbf = P.op("ve", lambda o, k=k: o.tensor_copy(
            x1bf[:, k * TH:(k + 1) * TH], x1stage[:]), waits=[tmu])
        tst = P.dma("sy", lambda o, k=k: o.dma_start(
            out=x1_scr[k * 128:(k + 1) * 128, :], in_=x1stage[:]), waits=[tmu])
        if d_x1 is not None:
            P.dma("sy", lambda o, k=k: o.dma_start(
                out=dbg_aps["x1"][k * 128:(k + 1) * 128, :], in_=x1stage[:]),
                waits=[tmu])
        prev = tst
        x1_toks.append(tbf)

    if stop_after == "ffa":
        return _render(nc, P, top)
    invn_cols = sb("invn_cols", [128, 8], F32, esf)
    simb = sb("simb", [128, 16 * NE], BF16, esf)
    sqe = sb("sqe", [128, NE], BF16, esf)
    sgate_bc = sb("sgate_bc", [128, NE], F32, esf)
    cnorm_bc = sb("cnorm_bc", [128, NE], F32, esf)
    wtm = sb("wtm", [128, 8 * NE], F32, esf)
    wfm = sb("wfm", [NE, TH], F32, esf)
    wfmb = sb("wfmb", [NE, TH], BF16, esf)
    rt1 = sb("rt1", [128, NE], F32, esf)
    rt2 = sb("rt2", [128, NE], F32, esf)
    rt3 = sb("rt3", [128, NE], F32, esf)
    rt4 = sb("rt4", [128, NE], F32, esf)
    rt5 = sb("rt5", [128, NE], F32, esf)
    rcol = sb("rcol", [128, 1], F32, esf)
    rcol2 = sb("rcol2", [128, 1], F32, esf)
    rcol3 = sb("rcol3", [128, 1], F32, esf)
    negb = sb("negb", [128, 1], F32, esf)
    posb = sb("posb", [128, 1], F32, esf)
    qwb = [sb("qwb0", [128, 2048], BF16, esf)] * 2
    kwb = [sb("kwb0", [128, 2048], BF16, esf)] * 2
    vwb = [sb("vwb0", [128, 2048], BF16, esf)] * 2
    wbc = sb("wbc", [128, TH], F32, esf)
    qacc = [sb(f"qacc{i}", [128, TH], F32, esf) for i in range(2)]
    kacc = [sb(f"kacc{i}", [128, TH], F32, esf) for i in range(2)]
    vacc = [sb(f"vacc{i}", [128, TH], F32, esf) for i in range(2)]
    tmpe = sb("tmpe", [128, TH], F32, esf)
    qbf = sb("qbf", [128, TH], BF16, esf)
    kownbf = sb("kownbf", [128, TH], BF16, esf)
    vownbf = sb("vownbf", [128, TH], BF16, esf)
    vstg = sb("vstg", [128, TH], BF16, esf)
    kfm = sb("kfm", [128, T], BF16, esf)
    vtm = sb("vtm", [128, T], BF16, esf)
    pbf = sb("pbf", [128, T], BF16, esf)
    ptr_sb = sb("ptr_sb", [128, 128], BF16, esf)
    aofm = sb("aofm", [128, TH], BF16, esf)
    maskb = sb("maskb", [128, 16 * 128], BF16, esf)
    zall = sb("zall", [128, NE * TH], BF16, esf)
    owb = [sb(f"owb{i}", [128, D], BF16, esf) for i in range(2)]
    s2stage = sb("s2stage", [128, TH], F32, esf)
    s2bf = sb("s2bf", [128, TH], BF16, esf)
    sq2 = sb("sq2", [128, TH], BF16, esf)
    s2tok = sb("s2tok", [128, D], F32, esf)
    vcol = sb("vcol", [128, 8], F32, esf)
    mu2c = sb("mu2c", [128, 8], F32, esf)
    rstd2c = sb("rstd2c", [128, 8], F32, esf)

    d_w = dbgt("w", [TH, NE])
    d_qkv = dbgt("qkv_o", [3, 128, TH])
    d_ao = dbgt("ao", [128, TH])
    d_s2 = dbgt("s2", [D, TH])

    inv_toks = []
    for qc in range(8):
        tok = P.op("te", lambda o, qc=qc: o.transpose(
            PS[0][0:128, 0:1], rowC[0:1, qc * 128:(qc + 1) * 128], identf_s[0:1, 0:1]),
            waits=[INV_TOK] + inv_toks[-1:])
        inv_toks.append(P.op("se", lambda o, qc=qc: o.copy(
            invn_cols[:, qc:qc + 1], PS[0][0:128, 0:1]), waits=[tok]))

    tsim = P.dma2("gp", [
        lambda o: o.dma_start(out=simb[:, 0:8 * NE].rearrange("p (k e) -> p k e", k=8),
                              in_=sim_d[0:1024, :].rearrange("(k p) e -> p k e", p=128)),
        lambda o: o.dma_start(out=simb[:, 8 * NE:16 * NE].rearrange("p (k e) -> p k e", k=8),
                              in_=sim_d[1024:2048, :].rearrange("(k p) e -> p k e", p=128)),
    ], waits=[PAR_TOK])
    prev2 = None
    for k in range(NK):
        tq = P.op("se", lambda o, k=k: o.activation(
            sqe[:], simb[:, k * NE:(k + 1) * NE], AF.Square), waits=[tsim, prev2])
        prev2 = mm_group(PROW[0:1, 0:NE],
                         [(lambda: onesc_s[:], lambda: sqe[:])],
                         waits=[tq, prev2], start=(k == 0), stop=(k == NK - 1))
    tn1 = P.op("se", lambda o: o.activation(rowA[0:1, 0:NE], PROW[0:1, 0:NE], AF.Ln),
               waits=[prev2])
    tn2 = P.op("se", lambda o: o.activation(rowA[0:1, 0:NE], rowA[0:1, 0:NE],
                                            AF.Exp, scale=-0.5), waits=[tn1])
    tn3 = P.op("ve", lambda o: o.tensor_copy(rowbf[0:1, 0:NE], rowA[0:1, 0:NE]),
               waits=[tn2] + bt)
    tn5 = bcast_f32(lambda: rowbf[0:1, 0:NE], lambda: cnorm_bc[:], [tn3], PS[0], width=NE)
    tgd = P.dma("sy", lambda o: o.dma_start(out=rowB[0:1, 0:NE], in_=gates_d),
                waits=inv_toks)
    tg1 = P.op("se", lambda o: o.activation(rowB[0:1, 0:NE], rowB[0:1, 0:NE],
                                            AF.Sigmoid), waits=[tgd])
    tg2 = P.op("ve", lambda o: o.tensor_copy(rowbf2[0:1, 0:NE], rowB[0:1, 0:NE]),
               waits=[tg1] + bt)
    tg4 = bcast_f32(lambda: rowbf2[0:1, 0:NE], lambda: sgate_bc[:], [tg2], PS[1], width=NE)
    tnb = P.op("ve", lambda o: o.memset(negb[:], NEG))
    tpb = P.op("ve", lambda o: o.memset(posb[:], -NEG))

    w_toks = []
    for tcn in range(8):
        lmm = mm_group(
            PS[2][:, 0:NE],
            [(lambda k=k, tcn=tcn: x1bf[:, k * TH + tcn * 128:k * TH + tcn * 128 + 128],
              lambda k=k: simb[:, k * NE:(k + 1) * NE]) for k in range(NK)],
            waits=[tsim, tn5, tg4] + x1_toks + w_toks[-1:])
        tl1 = P.op("ve", lambda o, tcn=tcn: o.scalar_tensor_tensor(
            rt1[:], PS[2][:, 0:NE], invn_cols[:, tcn:tcn + 1], cnorm_bc[:],
            ALU.mult, ALU.mult), waits=[lmm, tn5] + inv_toks)
        tl2 = P.op("ve", lambda o: o.tensor_sub(rt1[:], rt1[:], sgate_bc[:]),
                   waits=[tl1, tg4])
        tmk = P.op("ve", lambda o: o.tensor_single_scalar(rt3[:], rt1[:], 0.0, ALU.is_gt),
                   waits=[tl2])
        tnp = P.op("ve", lambda o: o.reduce_sum(rcol[:], rt3[:], axis=AX.X), waits=[tmk])
        tmx = P.op("ve", lambda o: o.reduce_max(rcol2[:], rt1[:], axis=AX.X), waits=[tnp])
        teq = P.op("ve", lambda o: o.tensor_tensor(
            rt2[:], rt1[:], rcol2[:].broadcast_to([128, NE]), ALU.is_equal), waits=[tmx])
        tl2b = P.op("ve", lambda o: o.scalar_tensor_tensor(
            rt4[:], rt2[:], negb[:], rt1[:], ALU.mult, ALU.add), waits=[teq, tnb])
        tmx2 = P.op("ve", lambda o: o.reduce_max(rcol2[:], rt4[:], axis=AX.X),
                    waits=[tl2b])
        teq2 = P.op("ve", lambda o: o.tensor_tensor(
            rt5[:], rt4[:], rcol2[:].broadcast_to([128, NE]), ALU.is_equal),
            waits=[tmx2])
        tfb = P.op("ve", lambda o: o.tensor_add(rt2[:], rt2[:], rt5[:]), waits=[teq2])
        tin = P.op("ve", lambda o: o.tensor_single_scalar(rcol3[:], rcol[:], 0.5, ALU.is_lt),
                   waits=[tfb])
        tsc = P.op("ve", lambda o: o.tensor_scalar_mul(rt4[:], rt2[:], rcol3[:]),
                   waits=[tin])
        tmf = P.op("ve", lambda o: o.tensor_max(rt2[:], rt4[:], rt3[:]), waits=[tsc])
        tga = P.op("ve", lambda o: o.tensor_scalar_max(rt5[:], rt1[:], 0.0), waits=[tmf])
        tml = P.op("ve", lambda o: o.tensor_mul(rt5[:], rt5[:], rt2[:])
                   if False else o.tensor_mul(rt3[:], rt5[:], rt2[:]), waits=[tga])
        tm1 = P.op("ve", lambda o: o.tensor_single_scalar(
            rt5[:], rt2[:], 1.0, ALU.subtract), waits=[tml])
        tm2 = P.op("ve", lambda o: o.scalar_tensor_tensor(
            rt4[:], rt5[:], posb[:], rt3[:], ALU.mult, ALU.add), waits=[tm1, tpb])
        tsm = P.op("ve", lambda o: o.reduce_max(rcol2[:], rt4[:], axis=AX.X), waits=[tm2])
        tng = P.op("ve", lambda o: o.tensor_scalar_mul(rcol[:], rcol2[:], -1.0),
                   waits=[tsm])
        tex = P.op("se", lambda o: o.activation(rt5[:], rt4[:], AF.Exp, bias=rcol[:]),
                   waits=[tng])
        tss = P.op("ve", lambda o: o.reduce_sum(rcol2[:], rt5[:], axis=AX.X), waits=[tex])
        trc = P.op("ve", lambda o: o.reciprocal(rcol3[:], rcol2[:]), waits=[tss])
        tw_ = P.op("ve", lambda o, tcn=tcn: o.scalar_tensor_tensor(
            wtm[:, tcn * NE:(tcn + 1) * NE], rt5[:], rcol3[:], rt2[:],
            ALU.mult, ALU.mult), waits=[trc])
        w_toks.append(tw_)
        if d_w is not None:
            P.dma("sy", lambda o, tcn=tcn: o.dma_start(
                out=dbg_aps["w"][tcn * 128:(tcn + 1) * 128, :],
                in_=wtm[:, tcn * NE:(tcn + 1) * NE]), waits=[tw_])
    if stop_after == "ffb":
        return _render(nc, P, top)
    wfm_toks = []
    for tcn in range(8):
        tok = P.op("te", lambda o, tcn=tcn: o.transpose(
            PS[3][0:NE, 0:128], wtm[:, tcn * NE:(tcn + 1) * NE], identf_s[:]),
            waits=[w_toks[tcn]] + wfm_toks[-1:])
        wfm_toks.append(P.op("se", lambda o, tcn=tcn: o.copy(
            wfm[0:NE, tcn * 128:(tcn + 1) * 128], PS[3][0:NE, 0:128]), waits=[tok]))
    twfb = P.op("ve", lambda o: o.tensor_copy(wfmb[0:NE, :], wfm[0:NE, :]),
                waits=wfm_toks)

    if stop_after == "ffc":
        return _render(nc, P, top)
    qw_busy = [None, None]
    cur = {"q": None, "k": None, "v": None}
    prev_mm3 = {"q": None, "k": None, "v": None}
    import os as _os2
    for e in range(int(_os2.environ.get("N_E", str(NE)))):
        tqw = P.dma2("gp", [
            lambda o, e=e: o.dma_start(
                out=qwb[e % 2][:, 0:1024].rearrange("p (k h) -> p k h", k=8),
                in_=qkv_d[0, e, 0:1024, :].rearrange("(k p) h -> p k h", p=128)),
            lambda o, e=e: o.dma_start(
                out=qwb[e % 2][:, 1024:2048].rearrange("p (k h) -> p k h", k=8),
                in_=qkv_d[0, e, 1024:2048, :].rearrange("(k p) h -> p k h", p=128)),
        ], waits=[qw_busy[e % 2], PAR_TOK])
        tkw = P.dma2("gp", [
            lambda o, e=e: o.dma_start(
                out=kwb[e % 2][:, 0:1024].rearrange("p (k h) -> p k h", k=8),
                in_=qkv_d[1, e, 0:1024, :].rearrange("(k p) h -> p k h", p=128)),
            lambda o, e=e: o.dma_start(
                out=kwb[e % 2][:, 1024:2048].rearrange("p (k h) -> p k h", k=8),
                in_=qkv_d[1, e, 1024:2048, :].rearrange("(k p) h -> p k h", p=128)),
        ], waits=[qw_busy[e % 2], PAR_TOK])
        tvw = P.dma2("gp", [
            lambda o, e=e: o.dma_start(
                out=vwb[e % 2][:, 0:1024].rearrange("p (k h) -> p k h", k=8),
                in_=qkv_d[2, e, 0:1024, :].rearrange("(k p) h -> p k h", p=128)),
            lambda o, e=e: o.dma_start(
                out=vwb[e % 2][:, 1024:2048].rearrange("p (k h) -> p k h", k=8),
                in_=qkv_d[2, e, 1024:2048, :].rearrange("(k p) h -> p k h", p=128)),
        ], waits=[qw_busy[e % 2], PAR_TOK])
        tb1 = mm_group(PS[4][:, 0:512],
                       [(lambda e=e: sel16_s[:, e * 128:(e + 1) * 128],
                         lambda: wfmb[0:NE, 0:512])], waits=[twfb, cur["v"]])
        tb2 = P.op("se", lambda o: o.copy(wbc[:, 0:512], PS[4][:, 0:512]), waits=[tb1])
        tb3 = mm_group(PS[4][:, 0:512],
                       [(lambda e=e: sel16_s[:, e * 128:(e + 1) * 128],
                         lambda: wfmb[0:NE, 512:1024])], waits=[tb2])
        tb4 = P.op("se", lambda o: o.copy(wbc[:, 512:1024], PS[4][:, 0:512]), waits=[tb3])
        wbc_tok = [tb2, tb4]
        last_mm = None
        for name, wgt, tload, psix in (("q", qwb, tqw, 0), ("k", kwb, tkw, 1),
                                       ("v", vwb, tvw, 2)):
            src = {"q": qacc, "k": kacc, "v": vacc}[name]
            prev_add = None
            for nn in range(2):
                mmt = mm_group(
                    PS[psix][:, 0:512],
                    [(lambda k=k, e=e, wgt=wgt: wgt[e % 2][:, k * 128:(k + 1) * 128],
                      lambda k=k, nn=nn: x1bf[:, k * TH + nn * 512:k * TH + nn * 512 + 512])
                     for k in range(NK)],
                    waits=[tload, prev_mm3[name], twfb] + x1_toks
                          + ([prev_add] if prev_add else []))
                tmm = P.op("ve", lambda o, nn=nn, psix=psix: o.tensor_mul(
                    tmpe[:, nn * 512:nn * 512 + 512], PS[psix][:, 0:512],
                    wbc[:, nn * 512:nn * 512 + 512]), waits=[mmt] + wbc_tok)
                if e == 0:
                    tad = P.op("ve", lambda o, nn=nn, src=src: o.tensor_copy(
                        src[0][:, nn * 512:nn * 512 + 512],
                        tmpe[:, nn * 512:nn * 512 + 512]), waits=[tmm])
                else:
                    tad = P.op("ve", lambda o, nn=nn, src=src, e=e: o.tensor_add(
                        src[e % 2][:, nn * 512:nn * 512 + 512],
                        src[(e + 1) % 2][:, nn * 512:nn * 512 + 512],
                        tmpe[:, nn * 512:nn * 512 + 512]), waits=[tmm, cur[name]])
                prev_add = tad
            cur[name] = prev_add
            prev_mm3[name] = prev_add
            last_mm = prev_add
        qw_busy[0] = qw_busy[1] = last_mm
    last = (NE - 1) % 2
    tqb = P.op("ve", lambda o: o.tensor_copy(qbf[:], qacc[last][:]), waits=[cur["q"]])
    tkb = P.op("ve", lambda o: o.tensor_copy(kownbf[:], kacc[last][:]), waits=[cur["k"]])
    tvb = P.op("ve", lambda o: o.tensor_copy(vownbf[:], vacc[last][:]), waits=[cur["v"]])
    if d_qkv is not None:
        P.dma("sy", lambda o: o.dma_start(out=dbg_aps["qkv_o"][0], in_=qacc[last][:]),
              waits=[cur["q"]])
        P.dma("sy", lambda o: o.dma_start(out=dbg_aps["qkv_o"][1], in_=kacc[last][:]),
              waits=[cur["k"]])
        P.dma("sy", lambda o: o.dma_start(out=dbg_aps["qkv_o"][2], in_=vacc[last][:]),
              waits=[cur["v"]])

    if stop_after == "ffd":
        return _render(nc, P, top)
    vt_toks = []
    for tcn in range(8):
        tok = P.op("te", lambda o, tcn=tcn: o.transpose(
            PTB[:, 0:128], vownbf[:, tcn * 128:(tcn + 1) * 128], identb_s[:]),
            waits=[tvb] + vt_toks[-1:])
        vt_toks.append(P.op("ve", lambda o, tcn=tcn: o.tensor_copy(
            vstg[:, tcn * 128:(tcn + 1) * 128], PTB[:, 0:128]), waits=[tok]))

    if stop_after == "ffe":
        return _render(nc, P, top)
    def kv_store_k(o, sem):
        with o.If_eq(par_reg, 0):
            o.dma_start(out=kvx[0, 0], in_=kownbf[:]).then_inc(sem, 16)
        with o.Else():
            o.dma_start(out=kvx[1, 0], in_=kownbf[:]).then_inc(sem, 16)

    def kv_store_v(o, sem):
        with o.If_eq(par_reg, 0):
            o.dma_start(out=kvx[0, 1], in_=vstg[:]).then_inc(sem, 16)
        with o.Else():
            o.dma_start(out=kvx[1, 1], in_=vstg[:]).then_inc(sem, 16)
    tks = P.dma_manual("gp", kv_store_k, waits=[tkb, PAR_TOK])
    tvs = P.dma_manual("gp", kv_store_v, waits=vt_toks + [PAR_TOK])
    if stop_after == "kva":
        return _render(nc, P, top)
    gp_flag(2, 2, waits=[tks, tvs])
    if stop_after == "kvb":
        return _render(nc, P, top)
    t_pollkv = gp_poll(((1, 2), 2), ((0, 2), 2), waits=P.now())
    if stop_after == "kvc":
        return _render(nc, P, top)
    kv_toks = [
        P.dma("gp", lambda o: o.dma_start(out=kfm[:, 0:TH], in_=kvx[0, 0]),
              waits=[t_pollkv]),
        P.dma("gp", lambda o: o.dma_start(out=kfm[:, TH:T], in_=kvx[1, 0]),
              waits=[t_pollkv]),
        P.dma("gp", lambda o: o.dma_start(out=vtm[:, 0:TH], in_=kvx[0, 1]),
              waits=[t_pollkv]),
        P.dma("gp", lambda o: o.dma_start(out=vtm[:, TH:T], in_=kvx[1, 1]),
              waits=[t_pollkv]),
    ]

    if stop_after == "mix":
        return _render(nc, P, top)

    ps_stack.close()
    esa = ExitStack()
    PS2 = [pst(f"ps2_{i}", [128, 512], F32, esa) for i in range(3)]
    psA = pst("psA", [128, 2048], F32, esa)
    PTB2 = pst("ptb2", [128, 512], BF16, esa)

    tmask_busy = None
    ao_toks = []
    for qc in range(8):
        nkk = 9 + qc
        tml_ = P.dma2("gp", [
            lambda o, qc=qc: o.dma_start(
                out=maskb[:, 0:8 * 128].rearrange("p (c q) -> p c q", c=8),
                in_=masks_d[qc, 0:8].rearrange("c p q -> p c q")),
            lambda o, qc=qc: o.dma_start(
                out=maskb[:, 8 * 128:16 * 128].rearrange("p (c q) -> p c q", c=8),
                in_=masks_d[qc, 8:16].rearrange("c p q -> p c q")),
        ], waits=[tmask_busy, PAR_TOK])
        sc_toks = []
        for kkc in range(nkk):
            sc_toks.append(mm_group(
                psA[:, kkc * 128:(kkc + 1) * 128],
                [(lambda qc=qc: qbf[:, qc * 128:(qc + 1) * 128],
                  lambda kkc=kkc: kfm[:, kkc * 128:(kkc + 1) * 128]),
                 (lambda: identb_s[:],
                  lambda kkc=kkc: maskb[:, kkc * 128:(kkc + 1) * 128])],
                waits=[tml_, tqb] + kv_toks + ao_toks[-1:]))
        tmask_busy = sc_toks[-1]
        tmx_ = P.op("ve", lambda o, nkk=nkk: o.reduce_max(
            rcol[:], psA[:, 0:nkk * 128], axis=AX.X), waits=sc_toks)
        tng_ = P.op("ve", lambda o: o.tensor_scalar_mul(rcol[:], rcol[:], -ATT_SCALE),
                    waits=[tmx_])
        tex_ = P.op("se", lambda o, nkk=nkk: o.activation(
            pbf[:, 0:nkk * 128], psA[:, 0:nkk * 128], AF.Exp,
            bias=rcol[:], scale=ATT_SCALE), waits=[tng_])
        tsm_ = P.op("ve", lambda o, nkk=nkk: o.reduce_sum(
            rcol2[:], pbf[:, 0:nkk * 128], axis=AX.X), waits=[tex_])
        trc_ = P.op("ve", lambda o: o.reciprocal(rcol[:], rcol2[:]), waits=[tsm_])
        tp2_ = P.op("ve", lambda o, nkk=nkk: o.tensor_scalar_mul(
            pbf[:, 0:nkk * 128], pbf[:, 0:nkk * 128], rcol[:]), waits=[trc_])
        avt = None
        for kkc in range(nkk):
            ttr = P.op("te", lambda o, kkc=kkc: o.transpose(
                PTB2[:, 0:128], pbf[:, kkc * 128:(kkc + 1) * 128], identb_s[:]),
                waits=[tp2_] + ([avt] if avt else []))
            tpc = P.op("ve", lambda o: o.tensor_copy(ptr_sb[:], PTB2[:, 0:128]),
                       waits=[ttr])
            avt = mm_group(
                PS2[0][:, 0:128],
                [(lambda kkc=kkc: vtm[:, kkc * 128:(kkc + 1) * 128],
                  lambda: ptr_sb[:])],
                waits=[tpc] + kv_toks, start=(kkc == 0), stop=(kkc == nkk - 1))
        ao_toks.append(P.op("ve", lambda o, qc=qc: o.tensor_copy(
            aofm[:, qc * 128:(qc + 1) * 128], PS2[0][:, 0:128]), waits=[avt]))
    if d_ao is not None:
        P.dma("gp", lambda o: o.dma_start(out=dbg_aps["ao"], in_=aofm[:]),
              waits=ao_toks + [PAR_TOK])

    z_toks = []
    prev3 = None
    for e in range(NE):
        tb1 = mm_group(PS2[1][:, 0:512],
                       [(lambda e=e: sel16_s[:, e * 128:(e + 1) * 128],
                         lambda: wfmb[0:NE, 0:512])], waits=[twfb, prev3])
        tz1 = P.op("ve", lambda o, e=e: o.tensor_mul(
            zall[:, e * TH:e * TH + 512], aofm[:, 0:512], PS2[1][:, 0:512]),
            waits=[tb1] + ao_toks)
        tb2 = mm_group(PS2[1][:, 0:512],
                       [(lambda e=e: sel16_s[:, e * 128:(e + 1) * 128],
                         lambda: wfmb[0:NE, 512:1024])], waits=[tz1])
        tz2 = P.op("ve", lambda o, e=e: o.tensor_mul(
            zall[:, e * TH + 512:(e + 1) * TH], aofm[:, 512:1024], PS2[1][:, 0:512]),
            waits=[tb2] + ao_toks)
        prev3 = tz2
        z_toks += [tz1, tz2]

    esa.close()
    esb = ExitStack()
    PS3 = [pst(f"ps3_{i}", [128, 512], F32, esb) for i in range(5)]
    PROW2 = pst("prow2", [1, 1024], F32, esb)
    PTF = pst("ptf", [128, 512], F32, esb)

    ow_busy = [None, None]
    mean2_mm = {0: None, 1: None}
    ssq2_mm = {0: None, 1: None}
    s2_toks = []
    prev_ev = []
    for m in range(16):
        tow = P.dma2("gp", [
            lambda o, m=m: o.dma_start(
                out=owb[m % 2][:, 0:1024].rearrange("p (e c) -> p e c", e=8),
                in_=oproj_d[0:8, :, m * 128:(m + 1) * 128].rearrange("e p c -> p e c")),
            lambda o, m=m: o.dma_start(
                out=owb[m % 2][:, 1024:2048].rearrange("p (e c) -> p e c", e=8),
                in_=oproj_d[8:16, :, m * 128:(m + 1) * 128].rearrange("e p c -> p e c")),
        ], waits=[ow_busy[m % 2], PAR_TOK] + z_toks[-1:])
        evs = []
        mmt = None
        for nn in range(2):
            mmt = mm_group(
                PS3[0 + nn][:, 0:512],
                [(lambda e2=e2, m=m: owb[m % 2][:, e2 * 128:(e2 + 1) * 128],
                  lambda e2=e2, nn=nn: zall[:, e2 * TH + nn * 512:e2 * TH + nn * 512 + 512])
                 for e2 in range(NE)],
                waits=[tow] + z_toks + prev_ev)
            tx1r = P.dma("sy", lambda o, m=m, nn=nn: o.dma_start(
                out=x1stage[:, nn * 512:nn * 512 + 512],
                in_=x1_scr[m * 128:(m + 1) * 128, nn * 512:nn * 512 + 512]),
                waits=[mmt] + prev_ev)
            evs.append(P.op("ve", lambda o, nn=nn: o.tensor_add(
                s2stage[:, nn * 512:nn * 512 + 512], PS3[nn][:, 0:512],
                x1stage[:, nn * 512:nn * 512 + 512]), waits=[mmt, tx1r] + prev_ev))
        ow_busy[m % 2] = mmt
        tws2 = P.dma("sy", lambda o, m=m: o.dma_start(
            out=s2_scr[m * 128:(m + 1) * 128, :], in_=s2stage[:]), waits=evs)
        s2_toks.append(tws2)
        if d_s2 is not None:
            P.dma("sy", lambda o, m=m: o.dma_start(
                out=dbg_aps["s2"][m * 128:(m + 1) * 128, :], in_=s2stage[:]), waits=evs)
        tbf2 = P.op("ve", lambda o: o.tensor_copy(s2bf[:], s2stage[:]), waits=evs)
        tq2 = P.op("se", lambda o: o.activation(sq2[:], s2stage[:], AF.Square), waits=evs)
        for nn in range(2):
            psm = PS3[2] if nn == 0 else PS3[3]
            mean2_mm[nn] = mm_group(
                psm[0:1, 0:512],
                [(lambda: onesc_s[:], lambda nn=nn: s2bf[:, nn * 512:nn * 512 + 512])],
                waits=[tbf2, mean2_mm[nn]], start=(m == 0), stop=(m == 15))
            ssq2_mm[nn] = mm_group(
                PROW2[0:1, nn * 512:nn * 512 + 512],
                [(lambda: onesc_s[:], lambda nn=nn: sq2[:, nn * 512:nn * 512 + 512])],
                waits=[tq2, ssq2_mm[nn]], start=(m == 0), stop=(m == 15))
        prev_ev = [tws2, tbf2, tq2]

    fr = []
    for nn in range(2):
        psm = PS3[2] if nn == 0 else PS3[3]
        t1 = P.op("se", lambda o, nn=nn, psm=psm: o.activation(
            rowA[0:1, nn * 512:nn * 512 + 512], psm[0:1, 0:512], AF.Copy,
            scale=1.0 / 2048.0), waits=[mean2_mm[nn]])
        t2 = P.op("se", lambda o, nn=nn: o.activation(
            rowB[0:1, nn * 512:nn * 512 + 512], rowA[0:1, nn * 512:nn * 512 + 512],
            AF.Square), waits=[t1])
        t3 = P.op("se", lambda o, nn=nn: o.activation(
            rowC[0:1, nn * 512:nn * 512 + 512], PROW2[0:1, nn * 512:nn * 512 + 512],
            AF.Copy, scale=1.0 / 2048.0), waits=[ssq2_mm[nn]])
        t4 = P.op("ve", lambda o, nn=nn: o.tensor_sub(
            rowC[0:1, nn * 512:nn * 512 + 512], rowC[0:1, nn * 512:nn * 512 + 512],
            rowB[0:1, nn * 512:nn * 512 + 512]), waits=[t2, t3])
        fr += [t1, t4]
    col_toks = []
    for tcn in range(8):
        tok = P.op("te", lambda o, tcn=tcn: o.transpose(
            PTF[0:128, 0:1], rowA[0:1, tcn * 128:(tcn + 1) * 128], identf_s[0:1, 0:1]),
            waits=fr + col_toks[-1:])
        tmu = P.op("se", lambda o, tcn=tcn: o.copy(mu2c[:, tcn:tcn + 1],
                                                   PTF[0:128, 0:1]), waits=[tok])
        tok2 = P.op("te", lambda o, tcn=tcn: o.transpose(
            PTF[0:128, 0:1], rowC[0:1, tcn * 128:(tcn + 1) * 128], identf_s[0:1, 0:1]),
            waits=[tmu])
        tvc = P.op("se", lambda o, tcn=tcn: o.copy(vcol[:, tcn:tcn + 1],
                                                   PTF[0:128, 0:1]), waits=[tok2])
        tl_ = P.op("se", lambda o, tcn=tcn: o.activation(
            rstd2c[:, tcn:tcn + 1], vcol[:, tcn:tcn + 1], AF.Ln, bias=eps_col[:]),
            waits=[tvc])
        col_toks.append(P.op("se", lambda o, tcn=tcn: o.activation(
            rstd2c[:, tcn:tcn + 1], rstd2c[:, tcn:tcn + 1], AF.Exp, scale=-0.5),
            waits=[tl_]))

    P.barrier()
    out_toks = []
    for tcn in range(8):
        tr_toks = []
        for k in range(NK):
            trd = P.dma("sy", lambda o, k=k, tcn=tcn: o.dma_start(
                out=s2stage[:, 0:128],
                in_=s2_scr[k * 128:(k + 1) * 128, tcn * 128:(tcn + 1) * 128]),
                waits=tr_toks[-1:] + out_toks[-1:])
            ttr = P.op("te", lambda o: o.transpose(
                PTF[:, 0:128], s2stage[:, 0:128], identf_s[:]),
                waits=[trd] + tr_toks[-1:])
            tr_toks.append(P.op("ve", lambda o, k=k: o.tensor_copy(
                s2tok[:, k * 128:(k + 1) * 128], PTF[:, 0:128]), waits=[ttr]))
        tno = P.op("ve", lambda o, tcn=tcn: o.tensor_scalar(
            s2tok[:], s2tok[:], mu2c[:, tcn:tcn + 1], rstd2c[:, tcn:tcn + 1],
            ALU.subtract, ALU.mult), waits=tr_toks + col_toks)
        out_toks.append(P.dma("sy", lambda o, tcn=tcn: o.dma_start(
            out=out_d[tcn * 128:(tcn + 1) * 128, :], in_=s2tok[:]), waits=[tno]))
    esb.close()
    esf.close()

    P.barrier()
    return _render(nc, P, top)


def _render(nc, P, top):
    P.barrier()
    P.check_deadlock()
    semd = {}
    semstack = ExitStack()
    for s in ("te", "ve", "se", "gp", "d_sy", "d_gp", "gpx", "pf1", "pf2"):
        semd[s] = semstack.enter_context(nc.semaphore(s))
    with nc.Block() as block:
        @block.tensor
        def _(o):
            for f in P.fns["te"]:
                f(o, semd)

        @block.vector
        def _(o):
            for f in P.fns["ve"]:
                f(o, semd)

        @block.scalar
        def _(o):
            for f in P.fns["se"]:
                f(o, semd)

        @block.gpsimd
        def _(o):
            for f in P.fns["gp"]:
                f(o, semd)

        @block.sync
        def _(o):
            for f in P.fns["sy"]:
                f(o, semd)
    semstack.close()
    return nc


def kernel(**inputs):
    ins = host_prepare(inputs)
    nc = build_program()
    res = run_bass_kernel_spmd(nc, ins, core_ids=list(range(8)))
    out = np.empty((B, T, D), np.float32)
    for core in range(8):
        b, par = core // 2, core % 2
        out[b, par * TH:(par + 1) * TH, :] = res.results[core]["out"]
    return out
